# version 15
# baseline (speedup 1.0000x reference)
import math
import numpy as np
import ml_dtypes
from contextlib import ExitStack

import concourse.bass as bass
import concourse.mybir as mybir
from concourse.bass_utils import run_bass_kernel_spmd

F32 = mybir.dt.float32
BF16 = mybir.dt.bfloat16
I32 = mybir.dt.int32
AF = mybir.ActivationFunctionType
ALU = mybir.AluOpType
AX = mybir.AxisListType.X
BFD = ml_dtypes.bfloat16

T = 2048
C = 1024
NCH = 8
EPS = 1e-5
RND = 12582912.0
N_FFT = 8192
K1H = 65


def _fft_consts():
    s1 = np.arange(64); k1 = np.arange(K1H); s2 = np.arange(64)
    k2 = np.arange(64); t1 = np.arange(64)
    base = np.exp(-2j * np.pi * np.outer(s1, k1) / 128)
    tw = np.exp(-2j * np.pi * np.outer(s2, k1) / N_FFT)
    Cc = base[None, :, :] * tw[:, None, :]
    E = np.exp(-2j * np.pi * np.outer(s2, k2) / 64)
    Finv = np.exp(2j * np.pi * np.outer(np.arange(64), k2) / 64)
    w_k1 = np.where((k1 == 0) | (k1 == 64), 1.0, 2.0) / N_FFT
    Mt2 = np.empty((64, 64, K1H), np.complex128)
    for tt in range(64):
        Mt2[tt] = np.exp(2j * np.pi * (64 * t1[:, None] + tt) * k1[None, :] / N_FFT) * w_k1[None, :]
    c_re = np.ascontiguousarray(Cc.real.transpose(1, 0, 2).reshape(64, 64 * K1H)).astype(BFD)
    c_im = np.ascontiguousarray(Cc.imag.transpose(1, 0, 2).reshape(64, 64 * K1H)).astype(BFD)
    LE1 = np.concatenate([E.real, E.imag], 1).astype(BFD)
    LE2 = np.concatenate([-E.imag, E.real], 1).astype(BFD)
    Fr = Finv.real.T; Fi = Finv.imag.T
    La_rS = np.concatenate([Fr, -Fr], 0).astype(BFD)
    La_rS2 = np.concatenate([-Fi, -Fi], 0).astype(BFD)
    La_iS = np.concatenate([Fi, -Fi], 0).astype(BFD)
    La_iS2 = np.concatenate([Fr, Fr], 0).astype(BFD)
    Mb_r = np.ascontiguousarray(Mt2.real.transpose(2, 0, 1).reshape(K1H, 64 * 64)).astype(BFD)
    Mb_mi = np.ascontiguousarray((-Mt2.imag).transpose(2, 0, 1).reshape(K1H, 64 * 64)).astype(BFD)
    return dict(c_re=c_re, c_im=c_im, LE1=LE1, LE2=LE2,
                La_rS=La_rS, La_rS2=La_rS2, La_iS=La_iS, La_iS2=La_iS2,
                Mb_r=Mb_r, Mb_mi=Mb_mi)


def _pos_emb_T():
    L = 4096
    positions = np.arange(L, dtype=np.float64)[:, None]
    div = np.exp(np.arange(0, 64, 2, dtype=np.float64) * -(np.log(10000.0) / 64))
    pe = np.zeros((L, 64), np.float64)
    pe[:, 0::2] = np.sin(positions * div)
    pe[:, 1::2] = np.cos(positions * div)
    return np.ascontiguousarray(pe.T).astype(BFD)


def _decay_cols():
    d = np.exp(-0.01 * np.arange(4096, dtype=np.float64)).astype(np.float32)
    return np.ascontiguousarray(d.reshape(32, 128).T)


def chunked_lhsT(wT, out_cols=128):
    d_in, d_out = wT.shape
    nk, nj = d_in // 128, d_out // out_cols
    a = wT.reshape(nk, 128, nj, out_cols).transpose(2, 1, 0, 3)
    return np.ascontiguousarray(a.reshape(nj, 128, nk * out_cols))


def fm_layout(x2d):
    R, Co = x2d.shape
    nk = R // 128
    return np.ascontiguousarray(x2d.reshape(nk, 128, Co).transpose(1, 0, 2).reshape(128, nk * Co))


def un_fm(x, nk, Co):
    return np.ascontiguousarray(np.asarray(x).reshape(128, nk, Co).transpose(1, 0, 2).reshape(nk * 128, Co))



ENGS = ("t", "v", "s", "g", "y")


class SQ:

    def __init__(self, nc, es, n_dma_sems=24):
        self.nc = nc
        self.obj = {"t": nc.tensor, "v": nc.vector, "s": nc.scalar,
                    "g": nc.gpsimd, "y": nc.sync}
        self.csem = {e: es.enter_context(nc.semaphore(f"c_{e}")) for e in ENGS}
        self.cval = {e: 0 for e in ENGS}
        self.dsem = {e: [es.enter_context(nc.semaphore(f"d_{e}{i}")) for i in range(n_dma_sems)]
                     for e in ("g", "y")}
        self.dval = {e: [0] * n_dma_sems for e in ("g", "y")}
        self.dcur = {e: 0 for e in ("g", "y")}
        self.dopen = {e: False for e in ("g", "y")}
        self.seen = {c: {} for c in ENGS}
        self.writer = {}
        self.readers = {}
        self._last_mm = None
        self._mm_reads = set()
        self._mm_writes = set()

    def _sem_of(self, key):
        if key[0] == "c":
            return self.csem[key[1]]
        return self.dsem[key[1]][key[2]]

    def _wait(self, cons, key, val):
        if key == ("c", cons):
            return
        if key[0] == "d":
            val = self.dval[key[1]][key[2]]
            if self.dcur[key[1]] == key[2] and self.dopen[key[1]]:
                self.dopen[key[1]] = False
                self.dcur[key[1]] = (key[2] + 1) % len(self.dsem[key[1]])
        if self.seen[cons].get(key, 0) < val:
            self.obj[cons].wait_ge(self._sem_of(key), val)
            self.seen[cons][key] = val

    def _self_key(self, cons):
        return ("c", cons)

    def _wait_same_engine_raw(self, cons, key, val):
        if key[0] == "d":
            val = self.dval[key[1]][key[2]]
            if self.dcur[key[1]] == key[2] and self.dopen[key[1]]:
                self.dopen[key[1]] = False
                self.dcur[key[1]] = (key[2] + 1) % len(self.dsem[key[1]])
        if self.seen[cons].get(key, 0) < val:
            self.obj[cons].wait_ge(self._sem_of(key), val)
            self.seen[cons][key] = val

    def _touch_read(self, cons, names):
        for nm in names:
            w = self.writer.get(nm)
            if w is not None:
                key, val = w
                if key == ("c", cons):
                    self._wait_same_engine_raw(cons, key, val)
                else:
                    self._wait(cons, key, val)

    def _touch_write(self, cons, names):
        for nm in names:
            w = self.writer.get(nm)
            if w is not None:
                key, val = w
                if key == ("c", cons):
                    self._wait_same_engine_raw(cons, key, val)
                else:
                    self._wait(cons, key, val)
            for key, val in self.readers.get(nm, {}).items():
                if key == ("c", cons):
                    self._wait_same_engine_raw(cons, key, val)
                else:
                    self._wait(cons, key, val)

    def _record(self, cons, clock, reads, writes):
        for nm in reads:
            self.readers.setdefault(nm, {})[clock[0]] = clock[1]
        for nm in writes:
            self.writer[nm] = clock
            self.readers[nm] = {}

    @staticmethod
    def _nm(ap):
        t = getattr(ap, "tensor", None)
        return t.name if t is not None else None

    def _split(self, a, kw):
        aps = [x for x in a if hasattr(x, "tensor")]
        out = aps[0]
        reads = {self._nm(x) for x in aps[1:]}
        wr = {self._nm(out)}
        for k, v in kw.items():
            if hasattr(v, "tensor"):
                if k == "accum_out":
                    wr.add(self._nm(v))
                else:
                    reads.add(self._nm(v))
        return reads, wr

    def _flush_if_conflict(self, names):
        if self._last_mm is not None and (names & (self._mm_reads | self._mm_writes)):
            self.mm_fence()

    def op(self, cons, name, *a, **kw):
        reads, writes = self._split(a, kw)
        self._flush_if_conflict(reads | writes)
        if cons == "t" and self._last_mm is not None:
            self.mm_fence()
        self._touch_read(cons, reads)
        self._touch_write(cons, writes)
        ins = getattr(self.obj[cons], name)(*a, **kw)
        self.cval[cons] += 1
        ins.then_inc(self.csem[cons], 1)
        clock = (("c", cons), self.cval[cons])
        self._record(cons, clock, reads, writes)
        return ins

    def mm(self, out, lhsT, rhs, start=True, stop=True):
        reads = {self._nm(lhsT), self._nm(rhs)}
        writes = {self._nm(out)}
        self._touch_read("t", reads)
        self._touch_write("t", writes)
        ins = self.obj["t"].matmul(out, lhsT, rhs, start=start, stop=stop)
        self._last_mm = ins
        self._mm_reads |= reads
        self._mm_writes |= writes
        return ins

    def mm_fence(self):
        if self._last_mm is None:
            return
        self.cval["t"] += 1
        self._last_mm.then_inc(self.csem["t"], 1)
        clock = (("c", "t"), self.cval["t"])
        self._record("t", clock, self._mm_reads, self._mm_writes)
        self._last_mm = None
        self._mm_reads = set()
        self._mm_writes = set()

    def dma(self, issuer, out, in_, **kw):
        reads = {self._nm(in_)}
        writes = {self._nm(out)}
        self._flush_if_conflict(reads | writes)
        self._touch_read(issuer, reads)
        self._touch_write(issuer, writes)
        ins = self.obj[issuer].dma_start(out, in_, **kw)
        i = self.dcur[issuer]
        self.dval[issuer][i] += 16
        ins.then_inc(self.dsem[issuer][i], 16)
        self.dopen[issuer] = True
        clock = (("d", issuer, i), self.dval[issuer][i])
        self._record(issuer, clock, reads, writes)
        return ins

    def cc(self, kind, op, groups, ins_, outs_):
        reads = {self._nm(x) for x in ins_}
        writes = {self._nm(x) for x in outs_}
        self._flush_if_conflict(reads | writes)
        self._touch_read("g", reads)
        self._touch_write("g", writes)
        ins = self.obj["g"].collective_compute(kind, op, replica_groups=groups,
                                               ins=ins_, outs=outs_)
        self.cval["g"] += 1
        ins.then_inc(self.csem["g"], 1)
        clock = (("c", "g"), self.cval["g"])
        self._record("g", clock, reads, writes)
        return ins

    def barrier(self):
        self.mm_fence()
        for e in ("g", "y"):
            self.dopen[e] = False
            self.dcur[e] = (self.dcur[e] + 1) % len(self.dsem[e])
        keys = [(("c", e2), self.cval[e2]) for e2 in ENGS if self.cval[e2]]
        for e in ("g", "y"):
            for i, v in enumerate(self.dval[e]):
                if v:
                    keys.append((("d", e, i), v))
        for cons in ENGS:
            for key, val in keys:
                if key == ("c", cons):
                    continue
                if self.seen[cons].get(key, 0) < val:
                    self.obj[cons].wait_ge(self._sem_of(key), val)
                    self.seen[cons][key] = val

    def finish(self):
        self.mm_fence()
        eng = self.obj["y"]
        for e in ENGS:
            if e != "y" and self.cval[e]:
                if self.seen["y"].get(("c", e), 0) < self.cval[e]:
                    eng.wait_ge(self.csem[e], self.cval[e])
                    self.seen["y"][("c", e)] = self.cval[e]
        for e in ("g", "y"):
            for i, v in enumerate(self.dval[e]):
                if v and self.seen["y"].get(("d", e, i), 0) < v:
                    eng.wait_ge(self.dsem[e][i], v)
                    self.seen["y"][("d", e, i)] = v


class SQ_SERIAL:
    def __init__(self, nc, es, n_dma_sems=24):
        self.nc = nc
        self.obj = {"t": nc.tensor, "v": nc.vector, "s": nc.scalar,
                    "g": nc.gpsimd, "y": nc.sync}
        self.csem = {e: es.enter_context(nc.semaphore(f"c_{e}")) for e in ENGS}
        self.cval = {e: 0 for e in ENGS}
        self.dsem = {e: [es.enter_context(nc.semaphore(f"d_{e}{i}")) for i in range(n_dma_sems)]
                     for e in ("g", "y")}
        self.dval = {e: [0] * n_dma_sems for e in ("g", "y")}
        self.dcur = {e: 0 for e in ("g", "y")}
        self.dopen = {e: False for e in ("g", "y")}
        self.seen = {c: {} for c in ENGS}
        self._last_mm = None

    def _targets(self):
        t = {}
        for e in ENGS:
            t[("c", e)] = self.cval[e]
        for e in ("g", "y"):
            for i, v in enumerate(self.dval[e]):
                if v:
                    t[("d", e, i)] = v
        return t

    def _wait_all(self, cons):
        eng = self.obj[cons]
        for key, tgt in self._targets().items():
            if key == ("c", cons):
                continue
            if self.seen[cons].get(key, 0) < tgt:
                sem = self.csem[key[1]] if key[0] == "c" else self.dsem[key[1]][key[2]]
                eng.wait_ge(sem, tgt)
                self.seen[cons][key] = tgt
        for e in ("g", "y"):
            if self.dopen[e]:
                self.dopen[e] = False
                self.dcur[e] = (self.dcur[e] + 1) % len(self.dsem[e])

    def _self_wait(self, cons):
        if cons == "t":
            return
        tgt = self.cval[cons]
        if tgt and self.seen[cons].get(("self",), 0) < tgt:
            self.obj[cons].wait_ge(self.csem[cons], tgt)
            self.seen[cons][("self",)] = tgt

    def op(self, cons, name, *a, **kw):
        self._wait_all(cons)
        self._self_wait(cons)
        ins = getattr(self.obj[cons], name)(*a, **kw)
        self.cval[cons] += 1
        ins.then_inc(self.csem[cons], 1)
        return ins

    def mm(self, out, lhsT, rhs, start=True, stop=True):
        self._wait_all("t")
        ins = self.obj["t"].matmul(out, lhsT, rhs, start=start, stop=stop)
        self._last_mm = ins
        return ins

    def mm_fence(self):
        assert self._last_mm is not None
        self.cval["t"] += 1
        self._last_mm.then_inc(self.csem["t"], 1)
        self._last_mm = None

    def dma(self, issuer, out, in_, **kw):
        self._wait_all(issuer)
        ins = self.obj[issuer].dma_start(out, in_, **kw)
        i = self.dcur[issuer]
        self.dval[issuer][i] += 16
        ins.then_inc(self.dsem[issuer][i], 16)
        self.dopen[issuer] = True
        return ins

    def cc(self, kind, op, groups, ins_, outs_):
        self._wait_all("g")
        ins = self.obj["g"].collective_compute(kind, op, replica_groups=groups,
                                               ins=ins_, outs=outs_)
        self.cval["g"] += 1
        ins.then_inc(self.csem["g"], 1)
        return ins

    def finish(self):
        self._wait_all("y")



class Scratch:

    def __init__(self, nc, es):
        sb = lambda n, sh, dt: es.enter_context(nc.sbuf_tensor(n, sh, dt))
        psa = lambda n, sh, dt: es.enter_context(nc.psum_tensor(n, sh, dt))
        self.ones_f = sb("ones_f", [128, 1], F32)
        self.ones_fr = sb("ones_fr", [1, 128], F32)
        self.ones_bc = sb("ones_bc", [128, 1], BF16)
        self.ones_b = sb("ones_b", [1, 128], BF16)
        self.x2 = sb("x2", [128, 512], F32)
        self.t1 = sb("t1", [128, 512], F32)
        self.rrow = sb("rrow", [1, T], F32)
        self.mb = sb("mb", [1, T], BF16)
        self.rb = sb("rb", [1, T], BF16)
        self.srows = sb("srows", [1, 2 * T], F32)
        self.bc_ps = psa("bc_ps", [128, 1024], F32)
        self.stat_ps = self.bc_ps

    def init(self, q):
        q.op("g", "memset", self.ones_f[:, :], 1.0)
        q.op("g", "memset", self.ones_fr[:, :], 1.0)
        q.op("g", "memset", self.ones_bc[:, :], 1.0)
        q.op("g", "memset", self.ones_b[:, :], 1.0)


def emit_ln_stats(q, S, X):
    ones = S.ones_f if X.dtype == F32 else S.ones_bc
    for sl in range(4):
        for ch in range(NCH):
            o = ch * T + sl * 512
            q.op("s", "activation", S.x2[:, 0:512], X[:, o:o + 512], AF.Square)
            q.mm(S.stat_ps[0:1, 0:512], ones[:, 0:1], X[:, o:o + 512],
                 start=(ch == 0), stop=(ch == NCH - 1))
            q.mm(S.stat_ps[0:1, 512:1024], S.ones_f[:, 0:1], S.x2[:, 0:512],
                 start=(ch == 0), stop=(ch == NCH - 1))
            q.mm_fence()
        q.op("v", "tensor_copy", S.srows[0:1, sl * 512:(sl + 1) * 512], S.stat_ps[0:1, 0:512])
        q.op("v", "tensor_copy", S.srows[0:1, T + sl * 512:T + (sl + 1) * 512], S.stat_ps[0:1, 512:1024])
    mrow = S.srows[0:1, 0:T]
    vrow = S.srows[0:1, T:2 * T]
    q.op("v", "tensor_scalar", mrow, mrow, 1.0 / C, None, ALU.mult)
    q.op("v", "tensor_tensor", S.rrow[0:1, :], mrow, mrow, ALU.mult)
    q.op("v", "scalar_tensor_tensor", vrow, vrow, 1.0 / C, S.rrow[0:1, :],
         ALU.mult, ALU.subtract)
    q.op("v", "tensor_scalar", vrow, vrow, float(EPS), None, ALU.add)
    q.op("s", "activation", S.rrow[0:1, :], vrow, AF.Ln)
    q.op("s", "activation", S.rrow[0:1, :], S.rrow[0:1, :], AF.Exp, scale=-0.5)
    q.op("s", "copy", S.mb[0:1, :], mrow)
    q.op("s", "copy", S.rb[0:1, :], S.rrow[0:1, :])


def emit_ln_apply(q, S, X, xn):
    for sl in range(4):
        q.mm(S.bc_ps[:, 0:512], S.ones_b[0:1, :], S.mb[0:1, sl * 512:(sl + 1) * 512])
        q.mm(S.bc_ps[:, 512:1024], S.ones_b[0:1, :], S.rb[0:1, sl * 512:(sl + 1) * 512])
        q.mm_fence()
        for ch in range(NCH):
            o = ch * T + sl * 512
            q.op("v", "tensor_tensor", S.t1[:, 0:512], X[:, o:o + 512],
                 S.bc_ps[:, 0:512], ALU.subtract)
            q.op("v", "tensor_tensor", xn[:, o:o + 512], S.t1[:, 0:512],
                 S.bc_ps[:, 512:1024], ALU.mult)


def emit_postln_residual(q, S, Y, X):
    emit_ln_stats(q, S, Y)
    for sl in range(4):
        q.mm(S.bc_ps[:, 0:512], S.ones_b[0:1, :], S.mb[0:1, sl * 512:(sl + 1) * 512])
        q.mm(S.bc_ps[:, 512:1024], S.ones_b[0:1, :], S.rb[0:1, sl * 512:(sl + 1) * 512])
        q.mm_fence()
        for ch in range(NCH):
            o = ch * T + sl * 512
            q.op("v", "tensor_tensor", S.t1[:, 0:512], Y[:, o:o + 512],
                 S.bc_ps[:, 0:512], ALU.subtract)
            q.op("v", "tensor_tensor", S.t1[:, 0:512], S.t1[:, 0:512],
                 S.bc_ps[:, 512:1024], ALU.mult)
            q.op("v", "tensor_tensor", X[:, o:o + 512], X[:, o:o + 512],
                 S.t1[:, 0:512], ALU.add)


def emit_fm_matmul(q, xn, wq_dram, nj, nk, psum, out_cb, wbuf, kind="copy",
                   t_total=T, rhs_off=0):
    nts = t_total // 512
    psl = psum if isinstance(psum, (list, tuple)) else [psum]
    wbl = wbuf if isinstance(wbuf, (list, tuple)) else [wbuf]
    for j in range(nj):
        wb = wbl[j % len(wbl)]
        q.dma("g", wb[:, 0:nk * 128], wq_dram[j])
        for sl in range(nts):
            ps = psl[(j * nts + sl) % len(psl)]
            for k in range(nk):
                q.mm(ps[:, 0:512], wb[:, k * 128:(k + 1) * 128],
                     xn[:, k * t_total + rhs_off + sl * 512: k * t_total + rhs_off + sl * 512 + 512],
                     start=(k == 0), stop=(k == nk - 1))
            q.mm_fence()
            out_cb(j, sl, ps)



_L1_CACHE = {}


def build_l1():
    nc = bass.Bass("TRN2", target_bir_lowering=False)
    Xd = nc.dram_tensor("X", [128, NCH * T], F32, kind="ExternalInput").ap()
    Wd = nc.dram_tensor("W", [24, 128, NCH * 128], F32, kind="ExternalInput").ap()
    wc1d = nc.dram_tensor("wc1", [128, NCH], F32, kind="ExternalInput").ap()
    kout = nc.dram_tensor("kbf", [128, NCH * T], BF16, kind="ExternalOutput").ap()
    vout = nc.dram_tensor("vbf", [128, NCH * T], BF16, kind="ExternalOutput").ap()
    aout = nc.dram_tensor("agent", [1, T], F32, kind="ExternalOutput").ap()
    zout = nc.dram_tensor("zloc", [128, NCH], F32, kind="ExternalOutput").ap()

    with ExitStack() as es:
        sb = lambda n, sh, dt: es.enter_context(nc.sbuf_tensor(n, sh, dt))
        psa = lambda n, sh, dt: es.enter_context(nc.psum_tensor(n, sh, dt))
        S = Scratch(nc, es)
        q = SQ(nc, es)
        X = sb("Xs", [128, NCH * T], F32)
        xn = sb("xn", [128, NCH * T], BF16)
        kb = sb("kb", [128, T], BF16)
        vb = sb("vb", [128, T], BF16)
        qb = sb("qb", [128, T], BF16)
        wbuf = sb("wbuf", [128, NCH * 128], BF16)
        wc1b = sb("wc1b", [128, NCH], BF16)
        arow = sb("arow", [1, T], F32)
        abrow = sb("abrow", [1, T], BF16)
        abc = sb("abc", [128, T], BF16)
        prod = qb
        zcol = sb("zcol", [128, NCH], F32)
        mm_ps = psa("mm_ps", [128, 512], F32)
        ag_ps = psa("ag_ps", [1, 512], F32)
        ab_ps = psa("ab_ps", [128, 512], F32)

        q.dma("y", X[:, :], Xd[:, :])
        q.dma("g", wc1b[:, :], wc1d[:, :])
        S.init(q)
        q.op("g", "memset", arow[0:1, :], 0.0)
        emit_ln_stats(q, S, X)
        emit_ln_apply(q, S, X, xn)

        state = {"abc_done": False}

        def make_abc():
            q.op("s", "activation", abrow[0:1, :], arow[0:1, :], AF.Copy, scale=float(C ** -0.5))
            for sl in range(4):
                q.mm(ab_ps[:, 0:512], S.ones_b[0:1, :], abrow[0:1, sl * 512:(sl + 1) * 512])
                q.mm_fence()
                q.op("s", "copy", abc[:, sl * 512:(sl + 1) * 512], ab_ps[:, 0:512])

        def out_cb(j, sl, ps):
            if j < 8:
                q.op("s", "copy", qb[:, sl * 512:(sl + 1) * 512], ps[:, 0:512])
                q.mm(ag_ps[0:1, 0:512], wc1b[:, j:j + 1],
                     qb[:, sl * 512:(sl + 1) * 512], start=True, stop=True)
                q.mm_fence()
                q.op("v", "tensor_tensor", arow[0:1, sl * 512:(sl + 1) * 512],
                     arow[0:1, sl * 512:(sl + 1) * 512], ag_ps[0:1, 0:512], ALU.add)
            elif j < 16:
                if not state["abc_done"]:
                    state["abc_done"] = True
                    make_abc()
                ch = j - 8
                q.op("s", "copy", kb[:, sl * 512:(sl + 1) * 512], ps[:, 0:512])
                if sl == 3:
                    q.op("v", "tensor_tensor", prod[:, :], kb[:, :], abc[:, :], ALU.mult)
                    q.op("v", "reduce_sum", zcol[:, ch:ch + 1], prod[:, :], AX)
                    q.dma("y", kout[:, ch * T:(ch + 1) * T], kb[:, :])
            else:
                ch = j - 16
                q.op("s", "copy", vb[:, sl * 512:(sl + 1) * 512], ps[:, 0:512])
                if sl == 3:
                    q.dma("y", vout[:, ch * T:(ch + 1) * T], vb[:, :])

        emit_fm_matmul(q, xn, Wd, 24, NCH, mm_ps, out_cb, wbuf)
        q.dma("y", aout[0:1, :], arow[0:1, :])
        q.dma("y", zout[:, :], zcol[:, :])
        q.finish()
    return nc


def build_l2():
    nc = bass.Bass("TRN2", target_bir_lowering=False)
    zb = nc.dram_tensor("z_both", [2, 128, NCH], F32, kind="ExternalInput").ap()
    vd = nc.dram_tensor("vbf", [128, NCH * T], BF16, kind="ExternalInput").ap()
    mzout = nc.dram_tensor("mz", [1, 2], F32, kind="ExternalOutput").ap()
    wvout = nc.dram_tensor("wv", [128, NCH], F32, kind="ExternalOutput").ap()

    with ExitStack() as es:
        sb = lambda n, sh, dt: es.enter_context(nc.sbuf_tensor(n, sh, dt))
        psa = lambda n, sh, dt: es.enter_context(nc.psum_tensor(n, sh, dt))
        S = Scratch(nc, es)
        q = SQ(nc, es)
        vb = sb("vb", [128, NCH * T], BF16)
        z2 = sb("z2", [2, 128 * NCH], F32)
        zc = sb("zc", [128, NCH], F32)
        zrow = sb("zrow", [1, C], F32)
        acr = sb("acr", [1, C], F32)
        acb = sb("acb", [128, NCH], BF16)
        srow = sb("srow", [1, T], F32)
        erow = sb("erow", [1, T], F32)
        ebrow = sb("ebrow", [1, T], BF16)
        ebc = sb("ebc", [128, T], BF16)
        prod = sb("prod", [128, T], BF16)
        wvc = sb("wvc", [128, NCH], F32)
        red = sb("red", [1, 4], F32)
        mzrow = sb("mz2", [1, 2], F32)
        s_ps = psa("s_ps", [1, 512], F32)
        b_ps = psa("b_ps", [128, 512], F32)
        zdram = nc.dram_tensor("zrow_d", [128, NCH], F32)
        acd = nc.dram_tensor("acd", [128, NCH], F32)

        q.dma("y", vb[:, :], vd[:, :])
        q.dma("y", zc[:, :], zb[0])
        z1 = sb("z1", [128, NCH], F32)
        q.dma("y", z1[:, :], zb[1])
        S.init(q)
        q.op("v", "tensor_tensor", zc[:, :], zc[:, :], z1[:, :], ALU.add)
        q.dma("y", zdram[:, :], zc[:, :])
        q.dma("y", zrow[0:1, :], zdram.rearrange("p k -> () (p k)"))
        q.op("v", "reduce_max", red[0:1, 0:1], zrow[0:1, :], AX)
        q.op("v", "tensor_scalar", acr[0:1, :], zrow[0:1, :], red[0:1, 0:1], None, ALU.subtract)
        q.op("s", "activation", acr[0:1, :], acr[0:1, :], AF.Exp)
        q.op("v", "reduce_sum", red[0:1, 1:2], acr[0:1, :], AX)
        q.op("v", "reciprocal", red[0:1, 2:3], red[0:1, 1:2])
        q.op("v", "tensor_scalar", acr[0:1, :], acr[0:1, :], red[0:1, 2:3], None, ALU.mult)
        q.dma("y", acd.rearrange("p k -> () (p k)"), acr[0:1, :])
        q.dma("g", acb[:, :], acd[:, :])
        for sl in range(4):
            for ch in range(NCH):
                q.mm(s_ps[0:1, 0:512], acb[:, ch:ch + 1],
                     vb[:, ch * T + sl * 512:ch * T + sl * 512 + 512],
                     start=(ch == 0), stop=(ch == NCH - 1))
            q.mm_fence()
            q.op("s", "copy", srow[0:1, sl * 512:(sl + 1) * 512], s_ps[0:1, 0:512])
        q.op("v", "reduce_max", red[0:1, 0:1], srow[0:1, :], AX)
        q.op("v", "tensor_scalar", erow[0:1, :], srow[0:1, :], red[0:1, 0:1], None, ALU.subtract)
        q.op("s", "activation", erow[0:1, :], erow[0:1, :], AF.Exp)
        q.op("v", "reduce_sum", red[0:1, 1:2], erow[0:1, :], AX)
        q.op("v", "tensor_copy", mzrow[0:1, 0:1], red[0:1, 0:1])
        q.op("v", "tensor_copy", mzrow[0:1, 1:2], red[0:1, 1:2])
        q.op("s", "copy", ebrow[0:1, :], erow[0:1, :])
        for sl in range(4):
            q.mm(b_ps[:, 0:512], S.ones_b[0:1, :], ebrow[0:1, sl * 512:(sl + 1) * 512])
            q.mm_fence()
            q.op("s", "copy", ebc[:, sl * 512:(sl + 1) * 512], b_ps[:, 0:512])
        for ch in range(NCH):
            q.op("v", "tensor_tensor", prod[:, :], vb[:, ch * T:(ch + 1) * T], ebc[:, :], ALU.mult)
            q.op("v", "reduce_sum", wvc[:, ch:ch + 1], prod[:, :], AX)
        q.dma("y", mzout[0:1, :], mzrow[0:1, :])
        q.dma("y", wvout[:, :], wvc[:, :])
        q.finish()
    return nc






def emit_colbcast(q, S, ps, row_bf, ch, width=512):
    q.mm(ps[:, 0:width], row_bf[0:1, ch * 128:(ch + 1) * 128], S.ones_brow[0:1, 0:width])
    q.mm_fence()


def build_l3():
    nc = bass.Bass("TRN2", target_bir_lowering=False)
    Xd = nc.dram_tensor("X", [128, NCH * T], F32, kind="ExternalInput").ap()
    std = nc.dram_tensor("stats", [1, 4], F32, kind="ExternalInput").ap()
    wvd = nc.dram_tensor("wv2", [2, 1, C], F32, kind="ExternalInput").ap()
    wgd = nc.dram_tensor("wg", [24, 128, NCH * 128], I32, kind="ExternalInput").ap()
    wud = nc.dram_tensor("wu", [24, 128, NCH * 128], I32, kind="ExternalInput").ap()
    wdd = nc.dram_tensor("wd", [24, 128, NCH * 128], I32, kind="ExternalInput").ap()
    Wud = nc.dram_tensor("Wu", [NCH, 128, C], F32, kind="ExternalInput").ap()
    Wvd = nc.dram_tensor("Wv", [NCH, 128, C], F32, kind="ExternalInput").ap()
    bud = nc.dram_tensor("bu", [1, C], F32, kind="ExternalInput").ap()
    bvd = nc.dram_tensor("bv", [1, C], F32, kind="ExternalInput").ap()
    X2out = nc.dram_tensor("X2", [128, NCH * T], F32, kind="ExternalOutput").ap()
    uout = nc.dram_tensor("u_tm", [T, C], BF16, kind="ExternalOutput").ap()
    vout = nc.dram_tensor("v_tm", [T, C], BF16, kind="ExternalOutput").ap()

    with ExitStack() as es:
        sb = lambda n, sh, dt: es.enter_context(nc.sbuf_tensor(n, sh, dt))
        psa = lambda n, sh, dt: es.enter_context(nc.psum_tensor(n, sh, dt))
        S = Scratch(nc, es)
        q = SQ(nc, es)
        S.ones_brow = sb("ones_brow", [1, 512], BF16)
        X = sb("Xs", [128, NCH * T], F32)
        xn = sb("xn", [128, NCH * T], BF16)
        attn_c = ExitStack()
        sb2 = lambda n, sh, dt: attn_c.enter_context(nc.sbuf_tensor(n, sh, dt))
        rows = sb2("rows", [1, 3 * C], F32)
        st4 = sb2("st4", [1, 16], F32)
        lnvb = sb2("lnvb", [1, C], BF16)
        bc_ps2 = psa("bc_ps2", [128, 512], F32)

        q.dma("y", X[:, :], Xd[:, :])
        q.dma("y", rows[0:1, 0:C], wvd[0])
        q.dma("y", rows[0:1, C:2 * C], wvd[1])
        q.dma("y", st4[0:1, 0:4], std[0:1, :])
        S.init(q)
        q.op("g", "memset", S.ones_brow[:, :], 1.0)

        q.op("v", "tensor_tensor", st4[0:1, 4:5], st4[0:1, 0:1], st4[0:1, 2:3], ALU.max)
        q.op("v", "tensor_tensor", st4[0:1, 5:6], st4[0:1, 0:1], st4[0:1, 4:5], ALU.subtract)
        q.op("v", "tensor_tensor", st4[0:1, 6:7], st4[0:1, 2:3], st4[0:1, 4:5], ALU.subtract)
        q.op("s", "activation", st4[0:1, 5:7], st4[0:1, 5:7], AF.Exp)
        q.op("v", "tensor_tensor", st4[0:1, 7:8], st4[0:1, 5:6], st4[0:1, 1:2], ALU.mult)
        q.op("v", "scalar_tensor_tensor", st4[0:1, 7:8], st4[0:1, 3:4], st4[0:1, 6:7],
             st4[0:1, 7:8], ALU.mult, ALU.add)
        q.op("v", "reciprocal", st4[0:1, 8:9], st4[0:1, 7:8])
        q.op("v", "tensor_tensor", st4[0:1, 9:10], st4[0:1, 5:6], st4[0:1, 8:9], ALU.mult)
        q.op("v", "tensor_tensor", st4[0:1, 10:11], st4[0:1, 6:7], st4[0:1, 8:9], ALU.mult)
        q.op("v", "tensor_scalar", rows[0:1, 2 * C:3 * C], rows[0:1, 0:C], st4[0:1, 9:10],
             None, ALU.mult)
        q.op("v", "scalar_tensor_tensor", rows[0:1, 2 * C:3 * C], rows[0:1, C:2 * C],
             st4[0:1, 10:11], rows[0:1, 2 * C:3 * C], ALU.mult, ALU.add)
        av = rows[0:1, 2 * C:3 * C]
        q.op("v", "reduce_sum", st4[0:1, 11:12], av, AX)
        q.op("v", "tensor_scalar", st4[0:1, 11:12], st4[0:1, 11:12], 1.0 / C, None, ALU.mult)
        q.op("v", "tensor_scalar", av, av, st4[0:1, 11:12], None, ALU.subtract)
        q.op("s", "activation", rows[0:1, 0:C], av, AF.Square)
        q.op("v", "reduce_sum", st4[0:1, 12:13], rows[0:1, 0:C], AX)
        q.op("v", "tensor_scalar", st4[0:1, 12:13], st4[0:1, 12:13], 1.0 / C, None, ALU.mult)
        q.op("v", "tensor_scalar", st4[0:1, 12:13], st4[0:1, 12:13], float(EPS), None, ALU.add)
        q.op("s", "activation", st4[0:1, 12:13], st4[0:1, 12:13], AF.Ln)
        q.op("s", "activation", st4[0:1, 12:13], st4[0:1, 12:13], AF.Exp, scale=-0.5)
        q.op("v", "tensor_scalar", av, av, st4[0:1, 12:13], None, ALU.mult)
        q.op("s", "copy", lnvb[0:1, :], av)
        for ch in range(NCH):
            emit_colbcast(q, S, bc_ps2, lnvb, ch)
            for sl in range(4):
                o = ch * T + sl * 512
                q.op("v", "tensor_tensor", X[:, o:o + 512], X[:, o:o + 512],
                     bc_ps2[:, 0:512], ALU.add)

        attn_c.close()
        emit_ln_stats(q, S, X)
        emit_ln_apply(q, S, X, xn)
        with nc.sbuf_tensor("yacc", [128, NCH * T], F32) as yacc, \
             nc.sbuf_tensor("gch", [128, T], BF16) as gch, \
             nc.sbuf_tensor("wbuf3", [128, NCH * 128], BF16) as wbuf3, \
             nc.psum_tensor("ff_ps", [128, 512], F32) as ff_ps, \
             nc.psum_tensor("dn_ps", [128, 512], F32) as dn_ps:
            hj = gch
            q.op("g", "memset", yacc[:, :], 0.0)
            for j in range(24):
                q.dma("g", wbuf3[:, 0:NCH * 128], wgd[j])
                for sl in range(4):
                    for k in range(NCH):
                        q.mm(ff_ps[:, 0:512], wbuf3[:, k * 128:(k + 1) * 128],
                             xn[:, k * T + sl * 512:k * T + sl * 512 + 512],
                             start=(k == 0), stop=(k == NCH - 1))
                    q.mm_fence()
                    q.op("s", "activation", gch[:, sl * 512:(sl + 1) * 512], ff_ps[:, 0:512],
                         AF.Silu, scale=1.0 / 127.0)
                q.dma("g", wbuf3[:, 0:NCH * 128], wud[j])
                for sl in range(4):
                    for k in range(NCH):
                        q.mm(ff_ps[:, 0:512], wbuf3[:, k * 128:(k + 1) * 128],
                             xn[:, k * T + sl * 512:k * T + sl * 512 + 512],
                             start=(k == 0), stop=(k == NCH - 1))
                    q.mm_fence()
                    q.op("v", "tensor_tensor", hj[:, sl * 512:(sl + 1) * 512],
                         gch[:, sl * 512:(sl + 1) * 512], ff_ps[:, 0:512], ALU.mult)
                q.dma("g", wbuf3[:, 0:NCH * 128], wdd[j])
                for co in range(NCH):
                    for sl in range(4):
                        q.mm(dn_ps[:, 0:512], wbuf3[:, co * 128:(co + 1) * 128],
                             hj[:, sl * 512:(sl + 1) * 512], start=True, stop=True)
                        q.mm_fence()
                        o = co * T + sl * 512
                        q.op("v", "tensor_tensor", yacc[:, o:o + 512], yacc[:, o:o + 512],
                             dn_ps[:, 0:512], ALU.add)
            for ch in range(NCH):
                q.op("s", "activation", yacc[:, ch * T:(ch + 1) * T], yacc[:, ch * T:(ch + 1) * T],
                     AF.Copy, scale=1.0 / (127.0 * 127.0))
            emit_ln_stats(q, S, yacc)
            for sl in range(4):
                q.mm(S.bc_ps[:, 0:512], S.ones_b[0:1, :], S.mb[0:1, sl * 512:(sl + 1) * 512])
                q.mm(S.bc_ps[:, 512:1024], S.ones_b[0:1, :], S.rb[0:1, sl * 512:(sl + 1) * 512])
                q.mm_fence()
                for ch in range(NCH):
                    o = ch * T + sl * 512
                    q.op("v", "tensor_tensor", S.t1[:, 0:512], yacc[:, o:o + 512],
                         S.bc_ps[:, 0:512], ALU.subtract)
                    q.op("v", "tensor_tensor", S.t1[:, 0:512], S.t1[:, 0:512],
                         S.bc_ps[:, 512:1024], ALU.mult)
                    q.op("v", "tensor_tensor", X[:, o:o + 512], X[:, o:o + 512],
                         S.t1[:, 0:512], ALU.add)
        q.dma("y", X2out[:, :], X[:, :])

        emit_ln_stats(q, S, X)
        emit_ln_apply(q, S, X, xn)
        with nc.sbuf_tensor("wup", [128, NCH * C], BF16) as wup, \
             nc.sbuf_tensor("wvp", [128, NCH * C], BF16) as wvp, \
             nc.sbuf_tensor("brow", [1, 2 * C], BF16) as brow, \
             nc.sbuf_tensor("uvt", [128, C], BF16) as uvt, \
             nc.psum_tensor("uv_ps", [128, 512], F32) as uv_ps:
            for k in range(NCH):
                q.dma("g", wup[:, k * C:(k + 1) * C], Wud[k])
                q.dma("g", wvp[:, k * C:(k + 1) * C], Wvd[k])
            q.dma("g", brow[0:1, 0:C], bud[0:1, :])
            q.dma("g", brow[0:1, C:2 * C], bvd[0:1, :])
            for which, wpl, bofs, outd in ((0, wup, 0, uout), (1, wvp, C, vout)):
                for tc in range(16):
                    for half in range(2):
                        for k in range(NCH):
                            q.mm(uv_ps[:, 0:512],
                                 xn[:, k * T + tc * 128:k * T + (tc + 1) * 128],
                                 wpl[:, k * C + half * 512:k * C + half * 512 + 512],
                                 start=(k == 0), stop=False)
                        q.mm(uv_ps[:, 0:512], S.ones_b[0:1, 0:128],
                             brow[0:1, bofs + half * 512:bofs + half * 512 + 512],
                             start=False, stop=True)
                        q.mm_fence()
                        q.op("s", "copy", uvt[:, half * 512:half * 512 + 512], uv_ps[:, 0:512])
                    q.dma("y", outd[tc * 128:(tc + 1) * 128, :], uvt[:, :])
        q.finish()
    return nc


def build_l4():
    fc = _fft_consts()
    nc = bass.Bass("TRN2", target_bir_lowering=False)
    vsl = nc.dram_tensor("vsl", [4096, 512], BF16, kind="ExternalInput").ap()
    W1d = nc.dram_tensor("W1", [64, 64], F32, kind="ExternalInput").ap()
    b1d = nc.dram_tensor("b1c", [64, 1], F32, kind="ExternalInput").ap()
    W2d = nc.dram_tensor("W2", [64, 64], F32, kind="ExternalInput").ap()
    b2d = nc.dram_tensor("b2c", [64, 1], F32, kind="ExternalInput").ap()
    W3d = nc.dram_tensor("W3o", [64, 512], F32, kind="ExternalInput").ap()
    b3d = nc.dram_tensor("b3r", [1, 512], F32, kind="ExternalInput").ap()
    peTd = nc.dram_tensor("peT", [64, 4096], BF16, kind="ExternalInput").ap()
    dkd = nc.dram_tensor("dkc", [128, 32], F32, kind="ExternalInput").ap()
    cre_d = nc.dram_tensor("c_re", [64, 64 * K1H], BF16, kind="ExternalInput").ap()
    cim_d = nc.dram_tensor("c_im", [64, 64 * K1H], BF16, kind="ExternalInput").ap()
    le1_d = nc.dram_tensor("LE1", [64, 128], BF16, kind="ExternalInput").ap()
    le2_d = nc.dram_tensor("LE2", [64, 128], BF16, kind="ExternalInput").ap()
    lars_d = nc.dram_tensor("La_rS", [128, 64], BF16, kind="ExternalInput").ap()
    lars2_d = nc.dram_tensor("La_rS2", [128, 64], BF16, kind="ExternalInput").ap()
    lais_d = nc.dram_tensor("La_iS", [128, 64], BF16, kind="ExternalInput").ap()
    lais2_d = nc.dram_tensor("La_iS2", [128, 64], BF16, kind="ExternalInput").ap()
    mbr_d = nc.dram_tensor("Mb_r", [K1H, 64 * 64], BF16, kind="ExternalInput").ap()
    mbmi_d = nc.dram_tensor("Mb_mi", [K1H, 64 * 64], BF16, kind="ExternalInput").ap()
    yout = nc.dram_tensor("y_conv", [4096, 512], BF16, kind="ExternalOutput").ap()

    h_d = nc.dram_tensor("h_d", [4096, 512], BF16)
    Av_d = nc.dram_tensor("Av_d", [2, 64, K1H, 512], BF16)
    Ah_d = nc.dram_tensor("Ah_d", [2, 64, K1H, 512], BF16)
    Hp_d = nc.dram_tensor("Hp_d", [K1H, 128, 512], BF16)
    Hs_d = nc.dram_tensor("Hs_d", [K1H, 128, 512], BF16)
    B_d = nc.dram_tensor("B_d", [2, K1H, 64, 512], BF16)

    with ExitStack() as es:
        sb = lambda n, sh, dt: es.enter_context(nc.sbuf_tensor(n, sh, dt))
        psa = lambda n, sh, dt: es.enter_context(nc.psum_tensor(n, sh, dt))
        q = SQ(nc, es)
        ones_b = sb("ones_b", [1, 512], BF16)
        q.op("g", "memset", ones_b[:, :], 1.0)

        with nc.sbuf_tensor("pe", [64, 4096], BF16) as pe, \
             nc.sbuf_tensor("w12", [64, 128], BF16) as w12, \
             nc.sbuf_tensor("b12", [64, 2], F32) as b12, \
             nc.sbuf_tensor("w3", [64, 512], BF16) as w3, \
             nc.sbuf_tensor("b3rs", [1, 512], BF16) as b3r, \
             nc.sbuf_tensor("cur1", [64, 4096], BF16) as cur1, \
             nc.sbuf_tensor("cur2", [64, 4096], BF16) as cur2, \
             nc.sbuf_tensor("dk", [128, 32], F32) as dk, \
             nc.sbuf_tensor("htile", [128, 512], BF16) as htile, \
             nc.psum_tensor("f_ps", [64, 512], F32) as f_ps, \
             nc.psum_tensor("f2_ps", [128, 512], F32) as f2_ps:
            q.dma("y", pe[:, :], peTd[:, :])
            q.dma("g", w12[:, 0:64], W1d[:, :])
            q.dma("g", w12[:, 64:128], W2d[:, :])
            q.dma("y", b12[:, 0:1], b1d[:, :])
            q.dma("y", b12[:, 1:2], b2d[:, :])
            q.dma("g", w3[:, :], W3d[:, :])
            q.dma("g", b3r[0:1, :], b3d[0:1, :])
            q.dma("y", dk[:, :], dkd[:, :])
            for stage, (src, dst, bcol) in enumerate(((pe, cur1, 0), (cur1, cur2, 1))):
                for sl in range(8):
                    q.mm(f_ps[:, 0:512], w12[:, stage * 64:stage * 64 + 64],
                         src[:, sl * 512:(sl + 1) * 512], start=True, stop=True)
                    q.mm_fence()
                    q.op("s", "activation", dst[:, sl * 512:(sl + 1) * 512], f_ps[:, 0:512],
                         AF.Gelu_apprx_tanh, bias=b12[:, bcol:bcol + 1])
            for g in range(32):
                q.mm(f2_ps[:, 0:512], cur2[:, g * 128:(g + 1) * 128], w3[:, :],
                     start=True, stop=False)
                q.mm(f2_ps[:, 0:512], ones_b[0:1, 0:128], b3r[0:1, :],
                     start=False, stop=True)
                q.mm_fence()
                q.op("s", "activation", htile[:, :], f2_ps[:, 0:512], AF.Copy,
                     scale=dk[:, g:g + 1])
                q.dma("y", h_d[g * 128:(g + 1) * 128, :], htile[:, :])

        with nc.sbuf_tensor("grid", [64, 64 * 512], BF16) as grid, \
             nc.sbuf_tensor("gtmp", [64, 8192], BF16) as gtmp, \
             nc.sbuf_tensor("selsb", [128, 2], F32) as selc, \
             nc.sbuf_tensor("cre", [64, 64 * K1H], BF16) as cre, \
             nc.sbuf_tensor("cim", [64, 64 * K1H], BF16) as cim, \
             nc.sbuf_tensor("abuf", [K1H, 8 * 2 * 512], BF16) as abuf, \
             nc.psum_tensor("s1_ps", [K1H, 1024], F32) as s1_ps:
            q.dma("y", selc[:, :], seld[:, :])
            q.dma("y", cre[:, :], cre_d[:, :])
            q.dma("y", cim[:, :], cim_d[:, :])
            for src, dstA in ((vex_out, Av_d), (h_d, Ah_d)):
                if src is vex_out:
                    for s2b8 in range(8):
                        rows8 = src.rearrange("(s1 s2) c -> s1 s2 c", s2=64)[:, s2b8 * 8:(s2b8 + 1) * 8, :]
                        q.dma("y", gtmp[:, 0:4096].rearrange("p (s c) -> p s c", s=8),
                              rows8[:, :, 0:512])
                        q.dma("y", gtmp[:, 4096:8192].rearrange("p (s c) -> p s c", s=8),
                              rows8[:, :, 512:1024])
                        q.op("v", "tensor_scalar", gtmp[:, 0:4096], gtmp[:, 0:4096],
                             selc[0:64, 0:1], None, ALU.mult)
                        q.op("v", "scalar_tensor_tensor", grid[:, s2b8 * 4096:(s2b8 + 1) * 4096],
                             gtmp[:, 4096:8192], selc[0:64, 1:2], gtmp[:, 0:4096],
                             ALU.mult, ALU.add)
                else:
                    q.dma("y", grid[:, :].rearrange("p (s c) -> p s c", s=64),
                          src.rearrange("(s1 s2) c -> s1 s2 c", s2=64))
                for s2b in range(8):
                    for i in range(8):
                        s2 = s2b * 8 + i
                        q.mm(s1_ps[:, 0:512], cre[:, s2 * K1H:(s2 + 1) * K1H],
                             grid[:, s2 * 512:(s2 + 1) * 512], start=True, stop=True)
                        q.mm_fence()
                        q.op("s", "copy", abuf[:, (i * 2) * 512:(i * 2) * 512 + 512],
                             s1_ps[:, 0:512])
                        q.mm(s1_ps[:, 512:1024], cim[:, s2 * K1H:(s2 + 1) * K1H],
                             grid[:, s2 * 512:(s2 + 1) * 512], start=True, stop=True)
                        q.mm_fence()
                        q.op("s", "copy", abuf[:, (i * 2 + 1) * 512:(i * 2 + 1) * 512 + 512],
                             s1_ps[:, 512:1024])
                    av = abuf[:, :].rearrange("k (s comp c) -> k s comp c", s=8, comp=2)
                    for comp in range(2):
                        q.dma("y", dstA[comp, s2b * 8:(s2b + 1) * 8, :, :].rearrange("s k c -> k s c"),
                              av[:, :, comp, :])

        with nc.sbuf_tensor("ar", [64, 512], BF16) as ar, \
             nc.sbuf_tensor("ai", [64, 512], BF16) as ai, \
             nc.sbuf_tensor("le", [64, 256], BF16) as le, \
             nc.sbuf_tensor("hp", [128, 512], BF16) as hp, \
             nc.psum_tensor("x_ps", [128, 512], F32) as x_ps:
            q.dma("y", le[:, 0:128], le1_d[:, :])
            q.dma("y", le[:, 128:256], le2_d[:, :])
            for k1 in range(K1H):
                q.dma("y", ar[:, :], Ah_d[0, :, k1, :])
                q.dma("y", ai[:, :], Ah_d[1, :, k1, :])
                q.mm(x_ps[:, 0:512], le[:, 0:128], ar[:, :], start=True, stop=False)
                q.mm(x_ps[:, 0:512], le[:, 128:256], ai[:, :], start=False, stop=True)
                q.mm_fence()
                q.op("s", "copy", hp[:, :], x_ps[:, 0:512])
                q.dma("y", Hp_d[k1, :, :], hp[:, :])
                q.dma("y", Hs_d[k1, 64:128, :], hp[0:64, :])
                q.dma("y", Hs_d[k1, 0:64, :], hp[64:128, :])

        with nc.sbuf_tensor("ar2", [64, 512], BF16) as ar, \
             nc.sbuf_tensor("ai2", [64, 512], BF16) as ai, \
             nc.sbuf_tensor("le2", [64, 256], BF16) as le, \
             nc.sbuf_tensor("la", [128, 256], BF16) as la, \
             nc.sbuf_tensor("hps", [128, 1024], BF16) as hps, \
             nc.sbuf_tensor("ss", [128, 1024], BF16) as ss, \
             nc.sbuf_tensor("bb", [128, 512], BF16) as bb, \
             nc.psum_tensor("x2_ps", [128, 512], F32) as x_ps, \
             nc.psum_tensor("b_ps", [128, 512], F32) as b_ps:
            q.dma("y", le[:, 0:128], le1_d[:, :])
            q.dma("y", le[:, 128:256], le2_d[:, :])
            q.dma("y", la[:, 0:64], lars_d[:, :])
            q.dma("y", la[:, 64:128], lars2_d[:, :])
            q.dma("y", la[:, 128:192], lais_d[:, :])
            q.dma("y", la[:, 192:256], lais2_d[:, :])
            for k1 in range(K1H):
                q.dma("y", ar[:, :], Av_d[0, :, k1, :])
                q.dma("y", ai[:, :], Av_d[1, :, k1, :])
                q.dma("y", hps[:, 0:512], Hp_d[k1, :, :])
                q.dma("y", hps[:, 512:1024], Hs_d[k1, :, :])
                q.mm(x_ps[:, 0:512], le[:, 0:128], ar[:, :], start=True, stop=False)
                q.mm(x_ps[:, 0:512], le[:, 128:256], ai[:, :], start=False, stop=True)
                q.mm_fence()
                q.op("v", "tensor_tensor", ss[:, 0:512], x_ps[:, 0:512], hps[:, 0:512], ALU.mult)
                q.op("v", "tensor_tensor", ss[:, 512:1024], x_ps[:, 0:512], hps[:, 512:1024], ALU.mult)
                q.mm(b_ps[0:64, 0:512], la[:, 0:64], ss[:, 0:512], start=True, stop=False)
                q.mm(b_ps[0:64, 0:512], la[:, 64:128], ss[:, 512:1024], start=False, stop=True)
                q.mm(b_ps[64:128, 0:512], la[:, 128:192], ss[:, 0:512], start=True, stop=False)
                q.mm(b_ps[64:128, 0:512], la[:, 192:256], ss[:, 512:1024], start=False, stop=True)
                q.mm_fence()
                q.op("s", "copy", bb[:, :], b_ps[:, 0:512])
                q.dma("y", B_d[0, k1, :, :], bb[0:64, :])
                q.dma("y", B_d[1, k1, :, :], bb[64:128, :])

        with nc.sbuf_tensor("mbr", [K1H, 64 * 64], BF16) as mbr, \
             nc.sbuf_tensor("mbmi", [K1H, 64 * 64], BF16) as mbmi, \
             nc.sbuf_tensor("bgr", [K1H, 1024], BF16) as bgr, \
             nc.sbuf_tensor("yt", [64, 512], BF16) as yt, \
             nc.psum_tensor("y_ps", [64, 512], F32) as y_ps:
            q.dma("y", mbr[:, :], mbr_d[:, :])
            q.dma("y", mbmi[:, :], mbmi_d[:, :])
            for t2 in range(64):
                q.dma("y", bgr[:, 0:512], B_d[0, :, t2, :])
                q.dma("y", bgr[:, 512:1024], B_d[1, :, t2, :])
                q.mm(y_ps[:, 0:512], mbr[:, t2 * 64:(t2 + 1) * 64], bgr[:, 0:512],
                     start=True, stop=False)
                q.mm(y_ps[:, 0:512], mbmi[:, t2 * 64:(t2 + 1) * 64], bgr[:, 512:1024],
                     start=False, stop=True)
                q.mm_fence()
                q.op("s", "copy", yt[:, :], y_ps[:, 0:512])
                q.dma("y", yout.rearrange("(t1 t2) c -> t2 t1 c", t2=64)[t2], yt[:, :])
        q.finish()
    return nc


def build_l5():
    nc = bass.Bass("TRN2", target_bir_lowering=False)
    Xd = nc.dram_tensor("X2", [128, NCH * T], F32, kind="ExternalInput").ap()
    ud = nc.dram_tensor("u_tm", [T, C], BF16, kind="ExternalInput").ap()
    ycd = nc.dram_tensor("yc_tm", [T, C], BF16, kind="ExternalInput").ap()
    wod = nc.dram_tensor("Wo_l", [NCH, 128, NCH * 128], F32, kind="ExternalInput").ap()
    bod = nc.dram_tensor("bo_c", [128, NCH], F32, kind="ExternalInput").ap()
    identd = nc.dram_tensor("identin", [128, 128], BF16, kind="ExternalInput").ap()
    seld = nc.dram_tensor("selc", [128, 2], F32, kind="ExternalInput").ap()
    l1gd = nc.dram_tensor("ln1g_c", [128, NCH], F32, kind="ExternalInput").ap()
    l1bd = nc.dram_tensor("ln1b_c", [128, NCH], F32, kind="ExternalInput").ap()
    w1d = nc.dram_tensor("w1T", [NCH, 128, 4096], F32, kind="ExternalInput").ap()
    b1rd = nc.dram_tensor("b1r", [1, 4096], F32, kind="ExternalInput").ap()
    l2gd = nc.dram_tensor("ln2g_r", [1, 4096], F32, kind="ExternalInput").ap()
    l2bd = nc.dram_tensor("ln2b_r", [1, 4096], F32, kind="ExternalInput").ap()
    w2d = nc.dram_tensor("w2T", [32, 128, C], F32, kind="ExternalInput").ap()
    b2d = nc.dram_tensor("b2_c", [128, NCH], F32, kind="ExternalInput").ap()
    X4out = nc.dram_tensor("X4", [128, NCH * T], F32, kind="ExternalOutput").ap()

    Xsp_d = nc.dram_tensor("Xsp_d", [128, NCH * T], F32)
    w1q_d = nc.dram_tensor("w1q_d", [NCH, 128, 4096], BF16)
    w2q_d = nc.dram_tensor("w2q_d", [32, 128, C], BF16)
    sxm_d = nc.dram_tensor("sxm_d", [16, 128], BF16)

    H = 4096
    with ExitStack() as es:
        sb = lambda n, sh, dt: es.enter_context(nc.sbuf_tensor(n, sh, dt))
        psa = lambda n, sh, dt: es.enter_context(nc.psum_tensor(n, sh, dt))
        S = Scratch(nc, es)
        q = SQ(nc, es)
        Yb = sb("Yb", [128, NCH * T], BF16)
        ident = sb("ident", [128, 128], BF16)
        wsc = sb("wsc", [1, 8], F32)
        wcol = sb("wcol", [128, 4], F32)
        q1a = sb("q1a", [128, NCH * T], BF16)
        rowx = sb("rowx", [1, 2 * T], BF16)
        x_scope = ExitStack()
        X = x_scope.enter_context(nc.sbuf_tensor("Xs", [128, NCH * T], F32))
        q.dma("y", X[:, :], Xd[:, :])
        q.dma("y", ident[:, :], identd[:, :])
        S.init(q)

        yvfm = q1a
        with nc.sbuf_tensor("ut", [128, C], BF16) as ut, \
             nc.sbuf_tensor("yct", [128, C], BF16) as yct, \
             nc.sbuf_tensor("wbuf5", [128, NCH * 128], BF16) as wbuf5, \
             nc.sbuf_tensor("boc", [128, NCH], F32) as boc, \
             nc.psum_tensor("tp_ps", [128, 128], BF16) as tp_ps, \
             nc.psum_tensor("wo_ps", [128, 512], F32) as wo_ps:
            q.dma("y", boc[:, :], bod[:, :])
            for tc in range(16):
                q.dma("y", ut[:, :], ud[tc * 128:(tc + 1) * 128, :])
                q.dma("y", yct[:, :], ycd[tc * 128:(tc + 1) * 128, :])
                q.op("v", "tensor_tensor", ut[:, :], ut[:, :], yct[:, :], ALU.mult)
                for ch in range(NCH):
                    q.op("t", "transpose", tp_ps[:, :], ut[:, ch * 128:(ch + 1) * 128],
                         ident[:, :])
                    q.op("s", "copy", yvfm[:, ch * T + tc * 128:ch * T + (tc + 1) * 128],
                         tp_ps[:, :])

            def wo_cb(j, sl, ps):
                q.op("v", "tensor_scalar", Yb[:, j * T + sl * 512:j * T + sl * 512 + 512],
                     ps[:, 0:512], boc[:, j:j + 1], None, ALU.add)

            emit_fm_matmul(q, yvfm, wod, NCH, NCH, wo_ps, wo_cb, wbuf5)
        emit_postln_residual(q, S, Yb, X)

        with nc.sbuf_tensor("wpl", [128, H], F32) as wpl, \
             nc.sbuf_tensor("acc", [128, 40], F32) as acc, \
             nc.sbuf_tensor("wqb", [128, H], BF16) as wqb, \
             nc.psum_tensor("wq_ps", [1, 512], F32) as wq_ps, \
             nc.psum_tensor("wc_ps", [128, 64], F32) as wc_ps:
            for wi, (src, dst, npl, wid) in enumerate(((w1d, w1q_d, NCH, H), (w2d, w2q_d, 32, C))):
                for p in range(npl):
                    q.dma("y", wpl[:, 0:wid], src[p])
                    q.op("s", "activation", wqb[:, 0:wid], wpl[:, 0:wid], AF.Abs,
                         accum_out=acc[:, p:p + 1])
                q.mm(wq_ps[0:1, 0:npl], S.ones_f[:, 0:1], acc[:, 0:npl], start=True, stop=True)
                q.mm_fence()
                q.op("v", "reduce_sum", wsc[0:1, wi * 4:wi * 4 + 1], wq_ps[0:1, 0:npl], AX)
                q.op("v", "tensor_scalar", wsc[0:1, wi * 4 + 1:wi * 4 + 2],
                     wsc[0:1, wi * 4:wi * 4 + 1], 1.0 / (npl * 128 * wid), None, ALU.mult)
                q.op("v", "tensor_scalar", wsc[0:1, wi * 4 + 1:wi * 4 + 2],
                     wsc[0:1, wi * 4 + 1:wi * 4 + 2], 1e-5, None, ALU.max)
                q.op("v", "reciprocal", wsc[0:1, wi * 4 + 2:wi * 4 + 3],
                     wsc[0:1, wi * 4 + 1:wi * 4 + 2])
                q.mm(wc_ps[:, 0:1], S.ones_fr[0:1, :], wsc[0:1, wi * 4 + 2:wi * 4 + 3],
                     start=True, stop=True)
                q.mm(wc_ps[:, 1:2], S.ones_fr[0:1, :], wsc[0:1, wi * 4 + 1:wi * 4 + 2],
                     start=True, stop=True)
                q.mm_fence()
                q.op("v", "tensor_copy", wcol[:, 2 * wi:2 * wi + 1], wc_ps[:, 0:1])
                q.op("v", "tensor_copy", wcol[:, 2 * wi + 1:2 * wi + 2], wc_ps[:, 1:2])
                for p in range(npl):
                    q.dma("y", wpl[:, 0:wid], src[p])
                    q.op("v", "tensor_scalar", wpl[:, 0:wid], wpl[:, 0:wid],
                         wcol[:, 2 * wi:2 * wi + 1], RND, ALU.mult, ALU.add)
                    q.op("v", "tensor_scalar", wpl[:, 0:wid], wpl[:, 0:wid],
                         -RND, -1.0, ALU.add, ALU.max)
                    q.op("v", "tensor_scalar", wqb[:, 0:wid], wpl[:, 0:wid],
                         1.0, None, ALU.min)
                    q.dma("y", dst[p], wqb[:, 0:wid])

        with nc.sbuf_tensor("ln1b", [128, NCH * T], BF16) as ln1b, \
             nc.sbuf_tensor("l1g", [128, 2 * NCH], F32) as l1g, \
             nc.sbuf_tensor("mx", [128, 16], BF16) as mx, \
             nc.sbuf_tensor("mrow16", [16, 128], BF16) as mrow16, \
             nc.psum_tensor("trb_ps", [128, 128], BF16) as trb_ps, \
             nc.psum_tensor("bcx_ps", [128, 1024], F32) as bcx_ps:
            q.dma("y", l1g[:, 0:NCH], l1gd[:, :])
            q.dma("y", l1g[:, NCH:2 * NCH], l1bd[:, :])
            mrowf = S.srows[0:1, 0:T]
            emit_ln_stats(q, S, X)
            for sl in range(4):
                q.mm(S.bc_ps[:, 0:512], S.ones_b[0:1, :], S.mb[0:1, sl * 512:(sl + 1) * 512])
                q.mm(S.bc_ps[:, 512:1024], S.ones_b[0:1, :], S.rb[0:1, sl * 512:(sl + 1) * 512])
                q.mm_fence()
                for ch in range(NCH):
                    o = ch * T + sl * 512
                    q.op("v", "tensor_tensor", S.t1[:, 0:512], X[:, o:o + 512],
                         S.bc_ps[:, 0:512], ALU.subtract)
                    q.op("v", "tensor_tensor", S.t1[:, 0:512], S.t1[:, 0:512],
                         S.bc_ps[:, 512:1024], ALU.mult)
                    q.op("v", "tensor_scalar", ln1b[:, o:o + 512], S.t1[:, 0:512],
                         l1g[:, ch:ch + 1], l1g[:, NCH + ch:NCH + ch + 1], ALU.mult, ALU.add)
            q.dma("y", Xsp_d[:, :], X[:, :])
            for ch in range(NCH):
                q.op("s", "activation", X[:, ch * T:(ch + 1) * T], ln1b[:, ch * T:(ch + 1) * T],
                     AF.Abs)
            q.op("v", "tensor_tensor", X[:, 0:T], X[:, 0:T], X[:, T:2 * T], ALU.max)
            q.op("v", "tensor_tensor", X[:, 2 * T:3 * T], X[:, 2 * T:3 * T], X[:, 3 * T:4 * T], ALU.max)
            q.op("v", "tensor_tensor", X[:, 4 * T:5 * T], X[:, 4 * T:5 * T], X[:, 5 * T:6 * T], ALU.max)
            q.op("v", "tensor_tensor", X[:, 6 * T:7 * T], X[:, 6 * T:7 * T], X[:, 7 * T:8 * T], ALU.max)
            q.op("v", "tensor_tensor", X[:, 0:T], X[:, 0:T], X[:, 2 * T:3 * T], ALU.max)
            q.op("v", "tensor_tensor", X[:, 4 * T:5 * T], X[:, 4 * T:5 * T], X[:, 6 * T:7 * T], ALU.max)
            q.op("v", "tensor_tensor", X[:, 0:T], X[:, 0:T], X[:, 4 * T:5 * T], ALU.max)
            q.op("s", "copy", Yb[:, 0:T], X[:, 0:T])
            for tc in range(16):
                q.op("t", "transpose", trb_ps[:, :], Yb[:, tc * 128:(tc + 1) * 128], ident[:, :])
                q.op("v", "reduce_max", mx[:, tc:tc + 1], trb_ps[:, :], AX)
            q.op("t", "transpose", trb_ps[0:16, :], mx[:, 0:16], ident[:, :])
            q.op("s", "copy", mrow16[:, :], trb_ps[0:16, 0:128])
            q.dma("y", sxm_d[:, :], mrow16[:, :])
            q.dma("g", mrowf[0:1, :], sxm_d.rearrange("a b -> () (a b)"))
            q.op("v", "tensor_scalar", mrowf[0:1, :], mrowf[0:1, :], 1e-5, None, ALU.max)
            q.op("v", "tensor_scalar", rowx[0:1, T:2 * T], mrowf[0:1, :],
                 wsc[0:1, 1:2], 1.0 / 127.0, ALU.mult, ALU.mult)
            q.op("s", "activation", mrowf[0:1, :], mrowf[0:1, :], AF.Ln)
            q.op("s", "activation", mrowf[0:1, :], mrowf[0:1, :], AF.Exp, scale=-1.0)
            q.op("v", "tensor_scalar", rowx[0:1, 0:T], mrowf[0:1, :], 127.0, None, ALU.mult)
            for sl in range(4):
                q.mm(bcx_ps[:, 0:512], S.ones_b[0:1, :], rowx[0:1, sl * 512:(sl + 1) * 512])
                q.mm(bcx_ps[:, 512:1024], S.ones_b[0:1, :], rowx[0:1, T + sl * 512:T + (sl + 1) * 512])
                q.mm_fence()
                for ch in range(NCH):
                    o = ch * T + sl * 512
                    q.op("v", "tensor_tensor", S.t1[:, 0:512], ln1b[:, o:o + 512],
                         bcx_ps[:, 0:512], ALU.mult)
                    q.op("v", "tensor_scalar", S.t1[:, 0:512], S.t1[:, 0:512],
                         RND, -RND, ALU.add, ALU.add)
                    q.op("v", "tensor_tensor", q1a[:, o:o + 512], S.t1[:, 0:512],
                         bcx_ps[:, 512:1024], ALU.mult)
        x_scope.close()

        with nc.sbuf_tensor("wbig", [128, 8 * 2048], BF16) as wbig, \
             nc.sbuf_tensor("gbf", [128, H], BF16) as gbf, \
             nc.sbuf_tensor("gsl", [128, 512], F32) as gsl, \
             nc.sbuf_tensor("zt", [128, H], BF16) as zt, \
             nc.sbuf_tensor("q2fm", [128, 32 * 128], BF16) as q2fm, \
             nc.sbuf_tensor("cols", [128, 16], F32) as cols, \
             nc.sbuf_tensor("rows2", [1, 3 * H], BF16) as rows2, \
             nc.psum_tensor("b1_ps", [128, 512], F32) as b1_ps, \
             nc.psum_tensor("tq_ps", [128, 128], BF16) as tq_ps, \
             nc.sbuf_tensor("b2c", [128, NCH], F32) as b2c, \
             nc.psum_tensor("bl2_ps", [128, 1024], F32) as bl2_ps:
            g2_ps = S.bc_ps
            q.dma("g", rows2[0:1, 0:H], b1rd[0:1, :])
            q.dma("g", rows2[0:1, H:2 * H], l2gd[0:1, :])
            q.dma("g", rows2[0:1, 2 * H:3 * H], l2bd[0:1, :])
            q.dma("y", b2c[:, :], b2d[:, :])
            for tc in range(16):
                q.op("g", "memset", cols[:, 0:2], 0.0)
                for half in range(2):
                    q.dma("y", wbig[:, :].rearrange("p (k c) -> p k c", k=NCH),
                          w1q_d[:, :, half * 2048:(half + 1) * 2048].rearrange("k p c -> p k c"))
                    for hs in range(4):
                        for k in range(NCH):
                            q.mm(b1_ps[:, 0:512],
                                 q1a[:, k * T + tc * 128:k * T + (tc + 1) * 128],
                                 wbig[:, k * 2048 + hs * 512:k * 2048 + hs * 512 + 512],
                                 start=(k == 0), stop=False)
                        ho = half * 2048 + hs * 512
                        q.mm(b1_ps[:, 0:512], S.ones_b[0:1, 0:128],
                             rows2[0:1, ho:ho + 512], start=False, stop=True)
                        q.mm_fence()
                        q.op("s", "activation", gsl[:, 0:512], b1_ps[:, 0:512], AF.Gelu,
                             accum_out=cols[:, 2:3])
                        q.op("v", "tensor_tensor", cols[:, 0:1], cols[:, 0:1],
                             cols[:, 2:3], ALU.add)
                        q.op("s", "activation", S.t1[:, 0:512], gsl[:, 0:512], AF.Square,
                             accum_out=cols[:, 3:4])
                        q.op("v", "tensor_tensor", cols[:, 1:2], cols[:, 1:2],
                             cols[:, 3:4], ALU.add)
                        q.op("s", "copy", gbf[:, ho:ho + 512], gsl[:, 0:512])
                q.op("v", "tensor_scalar", cols[:, 4:5], cols[:, 0:1], 1.0 / H, None, ALU.mult)
                q.op("v", "tensor_tensor", cols[:, 5:6], cols[:, 4:5], cols[:, 4:5], ALU.mult)
                q.op("v", "scalar_tensor_tensor", cols[:, 5:6], cols[:, 1:2], 1.0 / H,
                     cols[:, 5:6], ALU.mult, ALU.subtract)
                q.op("v", "tensor_scalar", cols[:, 5:6], cols[:, 5:6], float(EPS), None, ALU.add)
                q.op("s", "activation", cols[:, 6:7], cols[:, 5:6], AF.Ln)
                q.op("s", "activation", cols[:, 6:7], cols[:, 6:7], AF.Exp, scale=-0.5)
                q.op("v", "tensor_tensor", cols[:, 7:8], cols[:, 4:5], cols[:, 6:7], ALU.mult)
                q.op("v", "tensor_scalar", cols[:, 7:8], cols[:, 7:8], -1.0, None, ALU.mult)
                for hs in range(8):
                    ho = hs * 512
                    q.mm(g2_ps[:, 0:512], S.ones_b[0:1, :], rows2[0:1, H + ho:H + ho + 512])
                    q.mm(g2_ps[:, 512:1024], S.ones_b[0:1, :], rows2[0:1, 2 * H + ho:2 * H + ho + 512])
                    q.mm_fence()
                    q.op("v", "tensor_scalar", zt[:, ho:ho + 512], gbf[:, ho:ho + 512],
                         cols[:, 6:7], cols[:, 7:8], ALU.mult, ALU.add)
                    q.op("v", "tensor_tensor", zt[:, ho:ho + 512], zt[:, ho:ho + 512],
                         g2_ps[:, 0:512], ALU.mult)
                    q.op("v", "tensor_tensor", zt[:, ho:ho + 512], zt[:, ho:ho + 512],
                         g2_ps[:, 512:1024], ALU.add)
                q.op("v", "tensor_reduce", cols[:, 8:9], zt[:, :], AX, ALU.max,
                     apply_absolute_value=True)
                q.op("v", "tensor_scalar", cols[:, 8:9], cols[:, 8:9], 1e-5, None, ALU.max)
                q.op("v", "tensor_scalar", cols[:, 9:10], cols[:, 8:9],
                     wcol[:, 3:4], 1.0 / 127.0, ALU.mult, ALU.mult)
                q.op("s", "activation", cols[:, 10:11], cols[:, 8:9], AF.Ln)
                q.op("s", "activation", cols[:, 10:11], cols[:, 10:11], AF.Exp, scale=-1.0)
                q.op("v", "tensor_scalar", cols[:, 11:12], cols[:, 10:11], 127.0, None, ALU.mult)
                for hs in range(8):
                    ho = hs * 512
                    q.op("v", "tensor_scalar", S.t1[:, 0:512], zt[:, ho:ho + 512],
                         cols[:, 11:12], RND, ALU.mult, ALU.add)
                    q.op("v", "tensor_scalar", S.t1[:, 0:512], S.t1[:, 0:512],
                         -RND, None, ALU.add)
                    q.op("v", "tensor_scalar", zt[:, ho:ho + 512], S.t1[:, 0:512],
                         cols[:, 9:10], None, ALU.mult)
                for j2 in range(32):
                    q.op("t", "transpose", tq_ps[:, :], zt[:, j2 * 128:(j2 + 1) * 128], ident[:, :])
                    q.op("s", "copy", q2fm[:, j2 * 128:(j2 + 1) * 128], tq_ps[:, :])
                for j2 in range(32):
                    q.dma("y", wbig[:, 0:C], w2q_d[j2])
                    for half in range(2):
                        q.mm(bl2_ps[:, half * 512:(half + 1) * 512],
                             q2fm[:, j2 * 128:(j2 + 1) * 128],
                             wbig[:, half * 512:(half + 1) * 512],
                             start=(j2 == 0), stop=(j2 == 31))
                    q.mm_fence()
                q.op("s", "copy", zt[:, 0:C], bl2_ps[:, 0:1024])
                for co in range(NCH):
                    q.op("t", "transpose", tq_ps[:, :], zt[:, co * 128:(co + 1) * 128],
                         ident[:, :])
                    q.op("v", "tensor_scalar", Yb[:, co * T + tc * 128:co * T + (tc + 1) * 128],
                         tq_ps[:, :], b2c[:, co:co + 1], None, ALU.add)

        with nc.sbuf_tensor("Xs2", [128, NCH * T], F32) as X2s:
            q.dma("y", X2s[:, :], Xsp_d[:, :])
            emit_postln_residual(q, S, Yb, X2s)
            q.dma("y", X4out[:, :], X2s[:, :])
            q.finish()
    return nc


def _cols(vec):
    return np.ascontiguousarray(np.asarray(vec, np.float32).reshape(NCH, 128).T)


def kernel_multilaunch(**inputs):
    f32 = np.float32
    x = np.asarray(inputs["L0_attn_x"], f32)
    w_qkv = np.asarray(inputs["L0_attn_w_qkv"], f32)
    wc1 = np.asarray(inputs["L0_attn_w_conv1"], f32)
    wg = np.asarray(inputs["L0_ffn_w_gate"], np.int32)
    wu = np.asarray(inputs["L0_ffn_w_up"], np.int32)
    wd = np.asarray(inputs["L0_ffn_w_down"], np.int32)
    P1 = {k[8:]: np.asarray(v, f32) for k, v in inputs.items() if k.startswith("L1_attn_")}
    P2 = {k[7:]: np.asarray(v, f32) for k, v in inputs.items() if k.startswith("L1_ffn_")}

    cores = list(range(8))
    Xs = []
    for core in cores:
        b, h = core // 2, core % 2
        Xs.append(fm_layout(np.ascontiguousarray(x[b, 2048 * h:2048 * (h + 1), :].T)))

    nc1 = build_l1()
    wqkv_l = chunked_lhsT(np.ascontiguousarray(w_qkv.T))
    wc1c = _cols(wc1)
    r1 = run_bass_kernel_spmd(nc1, [{"X": Xs[c], "W": wqkv_l, "wc1": wc1c} for c in cores],
                              core_ids=cores).results

    nc2 = build_l2()
    maps2 = []
    for core in cores:
        pair = core ^ 1
        maps2.append({"z_both": np.stack([r1[core]["zloc"], r1[pair]["zloc"]]).astype(f32),
                      "vbf": r1[core]["vbf"]})
    r2 = run_bass_kernel_spmd(nc2, maps2, core_ids=cores).results

    nc3 = build_l3()
    wg_l = chunked_lhsT(np.ascontiguousarray(wg.T)).astype(np.int32)
    wu_l = chunked_lhsT(np.ascontiguousarray(wu.T)).astype(np.int32)
    wd_l = np.ascontiguousarray(wd.T.reshape(24, 128, 1024)).astype(np.int32)
    Wup = np.ascontiguousarray(P1["Wu"].reshape(8, 128, 1024))
    Wvp = np.ascontiguousarray(P1["Wv"].reshape(8, 128, 1024))
    maps3 = []
    for core in cores:
        a, bb_ = (core & ~1), (core | 1)
        stats = np.array([[r2[a]["mz"][0, 0], r2[a]["mz"][0, 1],
                           r2[bb_]["mz"][0, 0], r2[bb_]["mz"][0, 1]]], f32)
        wv2 = np.stack([un_fm(r2[a]["wv"], NCH, 1)[:, 0][None, :],
                        un_fm(r2[bb_]["wv"], NCH, 1)[:, 0][None, :]]).astype(f32)
        maps3.append({"X": Xs[core], "stats": stats, "wv2": wv2,
                      "wg": wg_l, "wu": wu_l, "wd": wd_l,
                      "Wu": Wup, "Wv": Wvp,
                      "bu": P1["bu"][None, :], "bv": P1["bv"][None, :]})
    r3 = run_bass_kernel_spmd(nc3, maps3, core_ids=cores).results

    nc4 = build_l4()
    fc = _fft_consts()
    peT = _pos_emb_T()
    dkc = _decay_cols()
    maps4 = []
    for core in cores:
        b, h = core // 2, core % 2
        vpair = np.concatenate([np.asarray(r3[2 * b]["v_tm"]),
                                np.asarray(r3[2 * b + 1]["v_tm"])], axis=0)
        csl = slice(512 * h, 512 * h + 512)
        maps4.append({"vsl": np.ascontiguousarray(vpair[:, csl]),
                      "W1": P1["W1"], "b1c": P1["b1"][:, None],
                      "W2": P1["W2"], "b2c": P1["b2"][:, None],
                      "W3o": np.ascontiguousarray(P1["W3"][:, csl]),
                      "b3r": np.ascontiguousarray(P1["b3"][None, csl]),
                      "peT": peT, "dkc": dkc, **fc})
    r4 = run_bass_kernel_spmd(nc4, maps4, core_ids=cores).results

    nc5 = build_l5()
    Wo_l = chunked_lhsT(P1["Wo"])
    maps5 = []
    for core in cores:
        b, h = core // 2, core % 2
        rows = slice(2048 * h, 2048 * h + 2048)
        yc = np.concatenate([np.asarray(r4[2 * b]["y_conv"])[rows, :],
                             np.asarray(r4[2 * b + 1]["y_conv"])[rows, :]], axis=1)
        maps5.append({"X2": np.asarray(r3[core]["X2"]),
                      "u_tm": np.asarray(r3[core]["u_tm"]),
                      "yc_tm": np.ascontiguousarray(yc),
                      "Wo_l": Wo_l, "bo_c": _cols(P1["bo"]),
                      "identin": np.eye(128, dtype=BFD),
                      "ln1g_c": _cols(P2["ln1_g"]), "ln1b_c": _cols(P2["ln1_b"]),
                      "w1T": np.ascontiguousarray(P2["w1"].T.reshape(8, 128, 4096)),
                      "b1r": P2["b1"][None, :],
                      "ln2g_r": P2["ln2_g"][None, :], "ln2b_r": P2["ln2_b"][None, :],
                      "w2T": np.ascontiguousarray(P2["w2"].T.reshape(32, 128, 1024)),
                      "b2_c": _cols(P2["b2"])})
    r5 = run_bass_kernel_spmd(nc5, maps5, core_ids=cores).results

    out = np.empty((4, 4096, 1024), f32)
    for core in cores:
        b, h = core // 2, core % 2
        out[b, 2048 * h:2048 * h + 2048, :] = un_fm(np.asarray(r5[core]["X4"]), NCH, T).T
    return out


def build_fused():
    fc = _fft_consts()
    nc = bass.Bass("TRN2", target_bir_lowering=False)
    AG = lambda name, shape, dt: nc.dram_tensor(name, shape, dt)
    Xd = nc.dram_tensor("X", [128, NCH * T], F32, kind="ExternalInput").ap()
    Wqkv = nc.dram_tensor("Wqkv", [24, 128, NCH * 128], F32, kind="ExternalInput").ap()
    wc1d = nc.dram_tensor("wc1", [128, NCH], F32, kind="ExternalInput").ap()
    wgd = nc.dram_tensor("wg", [24, 128, NCH * 128], I32, kind="ExternalInput").ap()
    wud = nc.dram_tensor("wu", [24, 128, NCH * 128], I32, kind="ExternalInput").ap()
    wdd = nc.dram_tensor("wd", [24, 128, NCH * 128], I32, kind="ExternalInput").ap()
    Wud = nc.dram_tensor("Wu", [NCH, 128, C], F32, kind="ExternalInput").ap()
    Wvd = nc.dram_tensor("Wv", [NCH, 128, C], F32, kind="ExternalInput").ap()
    bud = nc.dram_tensor("bu", [1, C], F32, kind="ExternalInput").ap()
    bvd = nc.dram_tensor("bv", [1, C], F32, kind="ExternalInput").ap()
    W1d = nc.dram_tensor("W1", [64, 64], F32, kind="ExternalInput").ap()
    b1d = nc.dram_tensor("b1c", [64, 1], F32, kind="ExternalInput").ap()
    W2d = nc.dram_tensor("W2", [64, 64], F32, kind="ExternalInput").ap()
    b2d = nc.dram_tensor("b2c", [64, 1], F32, kind="ExternalInput").ap()
    W3d = nc.dram_tensor("W3o", [64, 512], F32, kind="ExternalInput").ap()
    b3d = nc.dram_tensor("b3r", [1, 512], F32, kind="ExternalInput").ap()
    peTd = nc.dram_tensor("peT", [64, 4096], BF16, kind="ExternalInput").ap()
    dkd = nc.dram_tensor("dkc", [128, 32], F32, kind="ExternalInput").ap()
    cre_d = nc.dram_tensor("c_re", [64, 64 * K1H], BF16, kind="ExternalInput").ap()
    cim_d = nc.dram_tensor("c_im", [64, 64 * K1H], BF16, kind="ExternalInput").ap()
    le1_d = nc.dram_tensor("LE1", [64, 128], BF16, kind="ExternalInput").ap()
    le2_d = nc.dram_tensor("LE2", [64, 128], BF16, kind="ExternalInput").ap()
    lars_d = nc.dram_tensor("La_rS", [128, 64], BF16, kind="ExternalInput").ap()
    lars2_d = nc.dram_tensor("La_rS2", [128, 64], BF16, kind="ExternalInput").ap()
    lais_d = nc.dram_tensor("La_iS", [128, 64], BF16, kind="ExternalInput").ap()
    lais2_d = nc.dram_tensor("La_iS2", [128, 64], BF16, kind="ExternalInput").ap()
    mbr_d = nc.dram_tensor("Mb_r", [K1H, 64 * 64], BF16, kind="ExternalInput").ap()
    mbmi_d = nc.dram_tensor("Mb_mi", [K1H, 64 * 64], BF16, kind="ExternalInput").ap()
    wod = nc.dram_tensor("Wo_l", [NCH, 128, NCH * 128], F32, kind="ExternalInput").ap()
    bod = nc.dram_tensor("bo_c", [128, NCH], F32, kind="ExternalInput").ap()
    identd = nc.dram_tensor("identin", [128, 128], BF16, kind="ExternalInput").ap()
    seld = nc.dram_tensor("selc", [128, 2], F32, kind="ExternalInput").ap()
    l1gd = nc.dram_tensor("ln1g_c", [128, NCH], F32, kind="ExternalInput").ap()
    l1bd = nc.dram_tensor("ln1b_c", [128, NCH], F32, kind="ExternalInput").ap()
    w1d = nc.dram_tensor("w1T", [NCH, 128, 4096], F32, kind="ExternalInput").ap()
    b1rd = nc.dram_tensor("b1r", [1, 4096], F32, kind="ExternalInput").ap()
    l2gd = nc.dram_tensor("ln2g_r", [1, 4096], F32, kind="ExternalInput").ap()
    l2bd = nc.dram_tensor("ln2b_r", [1, 4096], F32, kind="ExternalInput").ap()
    w2d = nc.dram_tensor("w2T", [32, 128, C], F32, kind="ExternalInput").ap()
    b2cd = nc.dram_tensor("b2_c", [128, NCH], F32, kind="ExternalInput").ap()
    X4out = nc.dram_tensor("X4", [128, NCH * T], F32, kind="ExternalOutput").ap()
    zx_in = AG("zx_in", [128, NCH], F32)
    zx_out = AG("zx_out", [256, NCH], F32)
    mz_in = AG("mz_in", [1, 1026], F32)
    mz_out = AG("mz_out", [2, 1026], F32)
    vex_in = AG("vex_in", [T, C], BF16)
    vex_out = AG("vex_out", [2 * T, C], BF16)
    yx_in = AG("yx_in", [4096, 512], BF16)
    yx_out = AG("yx_out", [8192, 512], BF16)
    acd = AG("acd2", [128, NCH], F32)
    h_d = AG("h_d", [4096, 512], BF16)
    Av_d = AG("Av_d", [2, 64, K1H, 512], BF16)
    Ah_d = AG("Ah_d", [2, 64, K1H, 512], BF16)
    Hp_d = AG("Hp_d", [K1H, 128, 512], BF16)
    Hs_d = AG("Hs_d", [K1H, 128, 512], BF16)
    B_d = AG("B_d", [2, K1H, 64, 512], BF16)
    Xsp_d = AG("Xsp_d", [128, NCH * T], F32)
    u_d = AG("u_d", [T, C], BF16)
    hffn_d = AG("hffn_d", [24, 128, T], BF16)
    w1q_d = AG("w1q_d", [NCH, 128, 4096], BF16)
    w2q_d = AG("w2q_d", [32, 128, C], BF16)
    sxm_d = AG("sxm_d", [16, 128], BF16)
    PAIRS = [[0, 1], [2, 3], [4, 5], [6, 7]]

    H = 4096
    with ExitStack() as es:
        sb = lambda n, sh, dt: es.enter_context(nc.sbuf_tensor(n, sh, dt))
        psa = lambda n, sh, dt: es.enter_context(nc.psum_tensor(n, sh, dt))
        S = Scratch(nc, es)
        q = SQ(nc, es, n_dma_sems=24)
        S.ones_brow = sb("ones_brow", [1, 512], BF16)
        ident = sb("ident", [128, 128], BF16)
        wsc = sb("wsc", [1, 8], F32)
        wcol = sb("wcol", [128, 4], F32)
        q.dma("y", ident[:, :], identd[:, :])
        S.init(q)
        q.op("g", "memset", S.ones_brow[:, :], 1.0)

        q.barrier()
        x_scope = ExitStack()
        X = x_scope.enter_context(nc.sbuf_tensor("Xs", [128, NCH * T], F32))
        xn = x_scope.enter_context(nc.sbuf_tensor("xn", [128, NCH * T], BF16))
        q.dma("y", X[:, :], Xd[:, :])

        q.barrier()
        with nc.sbuf_tensor("vres", [128, NCH * T], BF16) as vres, \
             nc.sbuf_tensor("qb", [128, T], BF16) as qb, \
             nc.sbuf_tensor("wbuf", [128, NCH * 128], BF16) as wbuf, \
             nc.sbuf_tensor("wbufB", [128, NCH * 128], BF16) as wbufB, \
             nc.sbuf_tensor("wc1b", [128, NCH], BF16) as wc1b, \
             nc.sbuf_tensor("arow", [1, T], F32) as arow, \
             nc.sbuf_tensor("abrow", [1, T], BF16) as abrow, \
             nc.sbuf_tensor("abc", [128, T], BF16) as abc, \
             nc.sbuf_tensor("zcol", [128, NCH], F32) as zcol, \
             nc.sbuf_tensor("rows0", [1, 3 * C + 64], F32) as rows0, \
             nc.sbuf_tensor("acb", [128, NCH], BF16) as acb, \
             nc.psum_tensor("mm_ps", [128, 512], F32) as mm_ps, \
             nc.psum_tensor("mm_psB", [128, 512], F32) as mm_psB, \
             nc.psum_tensor("ag_ps", [1, 512], F32) as ag_ps, \
             nc.psum_tensor("ab_ps", [128, 512], F32) as ab_ps:
            lnvb = abrow
            kb = qb
            q.dma("g", wc1b[:, :], wc1d[:, :])
            q.op("g", "memset", arow[0:1, :], 0.0)
            emit_ln_stats(q, S, X)
            emit_ln_apply(q, S, X, xn)
            prod = qb
            state = {"abc_done": False}

            def make_abc():
                q.op("s", "activation", abrow[0:1, :], arow[0:1, :], AF.Copy,
                     scale=float(C ** -0.5))
                for sl in range(4):
                    q.mm(ab_ps[:, 0:512], S.ones_b[0:1, :], abrow[0:1, sl * 512:(sl + 1) * 512])
                    q.mm_fence()
                    q.op("s", "copy", abc[:, sl * 512:(sl + 1) * 512], ab_ps[:, 0:512])

            def out_cb(j, sl, ps):
                if j < 8:
                    q.op("s", "copy", qb[:, sl * 512:(sl + 1) * 512], ps[:, 0:512])
                    q.mm(ag_ps[0:1, 0:512], wc1b[:, j:j + 1],
                         qb[:, sl * 512:(sl + 1) * 512], start=True, stop=True)
                    q.mm_fence()
                    q.op("v", "tensor_tensor", arow[0:1, sl * 512:(sl + 1) * 512],
                         arow[0:1, sl * 512:(sl + 1) * 512], ag_ps[0:1, 0:512], ALU.add)
                elif j < 16:
                    if not state["abc_done"]:
                        state["abc_done"] = True
                        make_abc()
                    ch = j - 8
                    q.op("s", "copy", kb[:, sl * 512:(sl + 1) * 512], ps[:, 0:512])
                    if sl == 3:
                        q.op("v", "tensor_tensor", prod[:, :], kb[:, :], abc[:, :], ALU.mult)
                        q.op("v", "reduce_sum", zcol[:, ch:ch + 1], prod[:, :], AX)
                else:
                    ch = j - 16
                    q.op("s", "copy", vres[:, ch * T + sl * 512:ch * T + sl * 512 + 512],
                         ps[:, 0:512])

            emit_fm_matmul(q, xn, Wqkv, 24, NCH, [mm_ps, mm_psB], out_cb, [wbuf, wbufB])
            q.dma("y", zx_in[:, :], zcol[:, :])
            q.cc("AllGather", ALU.bypass, PAIRS, [zx_in.ap().opt()], [zx_out.ap().opt()])
            q.dma("y", zcol[:, :], zx_out[0:128, :])
            q.dma("y", S.t1[:, 0:NCH], zx_out[128:256, :])
            q.op("v", "tensor_tensor", zcol[:, :], zcol[:, :], S.t1[:, 0:NCH], ALU.add)
            q.dma("y", zx_in[:, :], zcol[:, :])
            zrow = rows0[0:1, 0:C]
            acr = rows0[0:1, C:2 * C]
            red = rows0[0:1, 3 * C:3 * C + 16]
            q.dma("y", zrow, zx_in.rearrange("p k -> () (p k)"))
            q.op("v", "reduce_max", red[0:1, 0:1], zrow, AX)
            q.op("v", "tensor_scalar", acr, zrow, red[0:1, 0:1], None, ALU.subtract)
            q.op("s", "activation", acr, acr, AF.Exp)
            q.op("v", "reduce_sum", red[0:1, 1:2], acr, AX)
            q.op("v", "reciprocal", red[0:1, 2:3], red[0:1, 1:2])
            q.op("v", "tensor_scalar", acr, acr, red[0:1, 2:3], None, ALU.mult)
            q.dma("y", acd.rearrange("p k -> () (p k)"), acr)
            q.dma("g", acb[:, :], acd[:, :])
            srow = rows0[0:1, 2 * C:3 * C + 0]
            srow = arow[0:1, :]
            for sl in range(4):
                for ch in range(NCH):
                    q.mm(ag_ps[0:1, 0:512], acb[:, ch:ch + 1],
                         vres[:, ch * T + sl * 512:ch * T + sl * 512 + 512],
                         start=(ch == 0), stop=(ch == NCH - 1))
                q.mm_fence()
                q.op("s", "copy", srow[0:1, sl * 512:(sl + 1) * 512], ag_ps[0:1, 0:512])
            erow = abrow
            q.op("v", "reduce_max", red[0:1, 3:4], srow, AX)
            q.op("v", "tensor_scalar", srow, srow, red[0:1, 3:4], None, ALU.subtract)
            q.op("s", "activation", srow, srow, AF.Exp)
            q.op("v", "reduce_sum", red[0:1, 4:5], srow, AX)
            q.op("s", "copy", erow[0:1, :], srow)
            for sl in range(4):
                q.mm(ab_ps[:, 0:512], S.ones_b[0:1, :], erow[0:1, sl * 512:(sl + 1) * 512])
                q.mm_fence()
                q.op("s", "copy", abc[:, sl * 512:(sl + 1) * 512], ab_ps[:, 0:512])
            for ch in range(NCH):
                q.op("v", "tensor_tensor", prod[:, :], vres[:, ch * T:(ch + 1) * T],
                     abc[:, :], ALU.mult)
                q.op("v", "reduce_sum", zcol[:, ch:ch + 1], prod[:, :], AX)
            q.dma("y", zx_in[:, :], zcol[:, :])
            q.dma("y", mz_in[0:1, 0:C].rearrange("o c -> o c"), zx_in.rearrange("p k -> () (p k)"))
            q.dma("y", mz_in[0:1, 1024:1026], red[0:1, 3:5])
            q.cc("AllGather", ALU.bypass, PAIRS, [mz_in.ap().opt()], [mz_out.ap().opt()])
            wv0 = rows0[0:1, 0:C]
            wv1 = rows0[0:1, C:2 * C]
            st4 = rows0[0:1, 3 * C:3 * C + 32]
            q.dma("y", wv0, mz_out[0:1, 0:C])
            q.dma("y", wv1, mz_out[1:2, 0:C])
            q.dma("y", st4[0:1, 0:2], mz_out[0:1, 1024:1026])
            q.dma("y", st4[0:1, 2:4], mz_out[1:2, 1024:1026])
            q.op("v", "tensor_tensor", st4[0:1, 4:5], st4[0:1, 0:1], st4[0:1, 2:3], ALU.max)
            q.op("v", "tensor_tensor", st4[0:1, 5:6], st4[0:1, 0:1], st4[0:1, 4:5], ALU.subtract)
            q.op("v", "tensor_tensor", st4[0:1, 6:7], st4[0:1, 2:3], st4[0:1, 4:5], ALU.subtract)
            q.op("s", "activation", st4[0:1, 5:7], st4[0:1, 5:7], AF.Exp)
            q.op("v", "tensor_tensor", st4[0:1, 7:8], st4[0:1, 5:6], st4[0:1, 1:2], ALU.mult)
            q.op("v", "scalar_tensor_tensor", st4[0:1, 7:8], st4[0:1, 3:4], st4[0:1, 6:7],
                 st4[0:1, 7:8], ALU.mult, ALU.add)
            q.op("v", "reciprocal", st4[0:1, 8:9], st4[0:1, 7:8])
            q.op("v", "tensor_tensor", st4[0:1, 9:10], st4[0:1, 5:6], st4[0:1, 8:9], ALU.mult)
            q.op("v", "tensor_tensor", st4[0:1, 10:11], st4[0:1, 6:7], st4[0:1, 8:9], ALU.mult)
            av = rows0[0:1, 2 * C:3 * C]
            q.op("v", "tensor_scalar", av, wv0, st4[0:1, 9:10], None, ALU.mult)
            q.op("v", "scalar_tensor_tensor", av, wv1, st4[0:1, 10:11], av, ALU.mult, ALU.add)
            q.op("v", "reduce_sum", st4[0:1, 11:12], av, AX)
            q.op("v", "tensor_scalar", st4[0:1, 11:12], st4[0:1, 11:12], 1.0 / C, None, ALU.mult)
            q.op("v", "tensor_scalar", av, av, st4[0:1, 11:12], None, ALU.subtract)
            q.op("s", "activation", wv0, av, AF.Square)
            q.op("v", "reduce_sum", st4[0:1, 12:13], wv0, AX)
            q.op("v", "tensor_scalar", st4[0:1, 12:13], st4[0:1, 12:13], 1.0 / C, None, ALU.mult)
            q.op("v", "tensor_scalar", st4[0:1, 12:13], st4[0:1, 12:13], float(EPS), None, ALU.add)
            q.op("s", "activation", st4[0:1, 12:13], st4[0:1, 12:13], AF.Ln)
            q.op("s", "activation", st4[0:1, 12:13], st4[0:1, 12:13], AF.Exp, scale=-0.5)
            q.op("v", "tensor_scalar", av, av, st4[0:1, 12:13], None, ALU.mult)
            q.op("s", "copy", lnvb[0:1, 0:C], av)
            for ch in range(NCH):
                q.mm(ab_ps[:, 0:512], lnvb[0:1, ch:C:NCH], S.ones_brow[0:1, 0:512])
                q.mm_fence()
                for sl in range(4):
                    o = ch * T + sl * 512
                    q.op("v", "tensor_tensor", X[:, o:o + 512], X[:, o:o + 512],
                         ab_ps[:, 0:512], ALU.add)

        emit_ln_stats(q, S, X)
        emit_ln_apply(q, S, X, xn)
        with nc.sbuf_tensor("Ybf0", [128, NCH * T], BF16) as Ybf, \
             nc.psum_tensor("ff_ps", [128, 512], F32) as ff_psa, \
             nc.psum_tensor("ff_psB", [128, 512], F32) as ff_psb, \
             nc.psum_tensor("dn_ps", [128, 512], F32) as dn_psa, \
             nc.psum_tensor("dn_psB", [128, 512], F32) as dn_psb:
          with nc.sbuf_tensor("gch", [128, T], BF16) as gch, \
               nc.sbuf_tensor("wbuf3", [128, NCH * 128], BF16) as wbuf3a, \
               nc.sbuf_tensor("wbuf3B", [128, NCH * 128], BF16) as wbuf3b:
            hj = gch
            for j in range(24):
                wA = wbuf3a if j % 2 == 0 else wbuf3b
                q.dma("g", wA[:, 0:NCH * 128], wgd[j])
                for sl in range(4):
                    fp = ff_psa if sl % 2 == 0 else ff_psb
                    for k in range(NCH):
                        q.mm(fp[:, 0:512], wA[:, k * 128:(k + 1) * 128],
                             xn[:, k * T + sl * 512:k * T + sl * 512 + 512],
                             start=(k == 0), stop=(k == NCH - 1))
                    q.mm_fence()
                    q.op("s", "activation", gch[:, sl * 512:(sl + 1) * 512], fp[:, 0:512],
                         AF.Silu, scale=1.0 / 127.0)
                wB = wbuf3b if j % 2 == 0 else wbuf3a
                q.dma("g", wB[:, 0:NCH * 128], wud[j])
                for sl in range(4):
                    fp = ff_psa if sl % 2 == 0 else ff_psb
                    for k in range(NCH):
                        q.mm(fp[:, 0:512], wB[:, k * 128:(k + 1) * 128],
                             xn[:, k * T + sl * 512:k * T + sl * 512 + 512],
                             start=(k == 0), stop=(k == NCH - 1))
                    q.mm_fence()
                    q.op("v", "tensor_tensor", hj[:, sl * 512:(sl + 1) * 512],
                         gch[:, sl * 512:(sl + 1) * 512], fp[:, 0:512], ALU.mult)
                q.dma("y", hffn_d[j], hj[:, :])
          q.barrier()
          with nc.sbuf_tensor("hslab", [128, 24 * 512], BF16) as hslab, \
               nc.sbuf_tensor("wdc", [128, 24 * 128], BF16) as wdca, \
               nc.sbuf_tensor("wdcB", [128, 24 * 128], BF16) as wdcb:
            for sl in range(4):
                q.dma("y", hslab[:, :].rearrange("p (j c) -> p j c", j=24),
                      hffn_d[:, :, sl * 512:(sl + 1) * 512].rearrange("j p c -> p j c"))
                for co in range(NCH):
                    wd_b = wdca if co % 2 == 0 else wdcb
                    if sl == 0 or True:
                        q.dma("g", wd_b[:, :].rearrange("p (j c) -> p j c", j=24),
                              wdd[:, :, co * 128:(co + 1) * 128].rearrange("j p c -> p j c"))
                    dp = dn_psa if co % 2 == 0 else dn_psb
                    for j in range(24):
                        q.mm(dp[:, 0:512], wd_b[:, j * 128:(j + 1) * 128],
                             hslab[:, j * 512:(j + 1) * 512],
                             start=(j == 0), stop=(j == 23))
                    q.mm_fence()
                    q.op("s", "activation", Ybf[:, co * T + sl * 512:co * T + sl * 512 + 512],
                         dp[:, 0:512], AF.Copy, scale=1.0 / (127.0 * 127.0))
          q.barrier()
        emit_postln_residual(q, S, Ybf, X)

        emit_ln_stats(q, S, X)
        emit_ln_apply(q, S, X, xn)
        q.barrier()
        with nc.sbuf_tensor("wup", [128, NCH * C], BF16) as wup, \
             nc.sbuf_tensor("wvp", [128, NCH * C], BF16) as wvp, \
             nc.sbuf_tensor("brow", [1, 2 * C], BF16) as brow, \
             nc.sbuf_tensor("uvt", [128, C], BF16) as uvta, \
             nc.sbuf_tensor("uvtB", [128, C], BF16) as uvtb, \
             nc.psum_tensor("uv_ps", [128, 512], F32) as uv_psa, \
             nc.psum_tensor("uv_psB", [128, 512], F32) as uv_psb:
            for k in range(NCH):
                q.dma("g", wup[:, k * C:(k + 1) * C], Wud[k])
                q.dma("g", wvp[:, k * C:(k + 1) * C], Wvd[k])
            q.dma("g", brow[0:1, 0:C], bud[0:1, :])
            q.dma("g", brow[0:1, C:2 * C], bvd[0:1, :])
            for which, wpl, bofs in ((0, wup, 0), (1, wvp, C)):
                for tc in range(16):
                    uvt = uvta if tc % 2 == 0 else uvtb
                    for half in range(2):
                        up = uv_psa if half == 0 else uv_psb
                        for k in range(NCH):
                            q.mm(up[:, 0:512],
                                 xn[:, k * T + tc * 128:k * T + (tc + 1) * 128],
                                 wpl[:, k * C + half * 512:k * C + half * 512 + 512],
                                 start=(k == 0), stop=False)
                        q.mm(up[:, 0:512], S.ones_b[0:1, 0:128],
                             brow[0:1, bofs + half * 512:bofs + half * 512 + 512],
                             start=False, stop=True)
                        q.mm_fence()
                        q.op("s", "copy", uvt[:, half * 512:half * 512 + 512], up[:, 0:512])
                    if which == 0:
                        q.dma("y", u_d[tc * 128:(tc + 1) * 128, :], uvt[:, :])
                    else:
                        q.dma("y", vex_in[tc * 128:(tc + 1) * 128, :], uvt[:, :])
        q.cc("AllGather", ALU.bypass, PAIRS, [vex_in.ap().opt()], [vex_out.ap().opt()])
        q.dma("y", Xsp_d[:, :], X[:, :])
        x_scope.close()

        q.barrier()
        cs1 = ExitStack()
        pe = cs1.enter_context(nc.sbuf_tensor("pe", [64, 4096], BF16))
        w12 = cs1.enter_context(nc.sbuf_tensor("w12", [64, 128], BF16))
        b12 = cs1.enter_context(nc.sbuf_tensor("b12", [64, 2], F32))
        w3 = cs1.enter_context(nc.sbuf_tensor("w3", [64, 512], BF16))
        b3r = cs1.enter_context(nc.sbuf_tensor("b3rs", [1, 512], BF16))
        cur1 = cs1.enter_context(nc.sbuf_tensor("cur1", [64, 4096], BF16))
        cur2 = cs1.enter_context(nc.sbuf_tensor("cur2", [64, 4096], BF16))
        dk = cs1.enter_context(nc.sbuf_tensor("dk", [128, 32], F32))
        htile = cs1.enter_context(nc.sbuf_tensor("htile", [128, 512], BF16))
        f_ps = cs1.enter_context(nc.psum_tensor("f_ps", [64, 512], F32))
        f2_ps = cs1.enter_context(nc.psum_tensor("f2_ps", [128, 512], F32))
        q.dma("y", pe[:, :], peTd[:, :])
        q.dma("g", w12[:, 0:64], W1d[:, :])
        q.dma("g", w12[:, 64:128], W2d[:, :])
        q.dma("y", b12[:, 0:1], b1d[:, :])
        q.dma("y", b12[:, 1:2], b2d[:, :])
        q.dma("g", w3[:, :], W3d[:, :])
        q.dma("g", b3r[0:1, :], b3d[0:1, :])
        q.dma("y", dk[:, :], dkd[:, :])
        for stage, (src, dst, bcol) in enumerate(((pe, cur1, 0), (cur1, cur2, 1))):
            for sl in range(8):
                q.mm(f_ps[:, 0:512], w12[:, stage * 64:stage * 64 + 64],
                     src[:, sl * 512:(sl + 1) * 512], start=True, stop=True)
                q.mm_fence()
                q.op("s", "activation", dst[:, sl * 512:(sl + 1) * 512], f_ps[:, 0:512],
                     AF.Gelu_apprx_tanh, bias=b12[:, bcol:bcol + 1])
        for g in range(32):
            q.mm(f2_ps[:, 0:512], cur2[:, g * 128:(g + 1) * 128], w3[:, :],
                 start=True, stop=False)
            q.mm(f2_ps[:, 0:512], S.ones_b[0:1, 0:128], b3r[0:1, :], start=False, stop=True)
            q.mm_fence()
            q.op("s", "activation", htile[:, :], f2_ps[:, 0:512], AF.Copy,
                 scale=dk[:, g:g + 1])
            q.dma("y", h_d[g * 128:(g + 1) * 128, :], htile[:, :])
        grid = cs1.enter_context(nc.sbuf_tensor("grid", [64, 64 * 512], BF16))
        gtmp = cs1.enter_context(nc.sbuf_tensor("gtmp", [64, 8192], BF16))
        selc = cs1.enter_context(nc.sbuf_tensor("selsb", [128, 2], F32))
        cre = cs1.enter_context(nc.sbuf_tensor("cre", [64, 64 * K1H], BF16))
        cim = cs1.enter_context(nc.sbuf_tensor("cim", [64, 64 * K1H], BF16))
        abuf = cs1.enter_context(nc.sbuf_tensor("abuf", [K1H, 8 * 2 * 512], BF16))
        s1_ps = cs1.enter_context(nc.psum_tensor("s1_ps", [K1H, 1024], F32))
        q.dma("y", selc[:, :], seld[:, :])
        q.dma("y", cre[:, :], cre_d[:, :])
        q.dma("y", cim[:, :], cim_d[:, :])
        for src, dstA in ((vex_out, Av_d), (h_d, Ah_d)):
            if src is vex_out:
                for s2b8 in range(8):
                    rows8 = src.rearrange("(s1 s2) c -> s1 s2 c", s2=64)[:, s2b8 * 8:(s2b8 + 1) * 8, :]
                    q.dma("y", gtmp[:, 0:4096].rearrange("p (s c) -> p s c", s=8),
                          rows8[:, :, 0:512])
                    q.dma("y", gtmp[:, 4096:8192].rearrange("p (s c) -> p s c", s=8),
                          rows8[:, :, 512:1024])
                    q.op("v", "tensor_scalar", gtmp[:, 0:4096], gtmp[:, 0:4096],
                         selc[0:64, 0:1], None, ALU.mult)
                    q.op("v", "scalar_tensor_tensor", grid[:, s2b8 * 4096:(s2b8 + 1) * 4096],
                         gtmp[:, 4096:8192], selc[0:64, 1:2], gtmp[:, 0:4096],
                         ALU.mult, ALU.add)
            else:
                q.dma("y", grid[:, :].rearrange("p (s c) -> p s c", s=64),
                      src.rearrange("(s1 s2) c -> s1 s2 c", s2=64))
            for s2b in range(8):
                for i in range(8):
                    s2 = s2b * 8 + i
                    q.mm(s1_ps[:, 0:512], cre[:, s2 * K1H:(s2 + 1) * K1H],
                         grid[:, s2 * 512:(s2 + 1) * 512], start=True, stop=True)
                    q.mm_fence()
                    q.op("s", "copy", abuf[:, (i * 2) * 512:(i * 2) * 512 + 512],
                         s1_ps[:, 0:512])
                    q.mm(s1_ps[:, 512:1024], cim[:, s2 * K1H:(s2 + 1) * K1H],
                         grid[:, s2 * 512:(s2 + 1) * 512], start=True, stop=True)
                    q.mm_fence()
                    q.op("s", "copy", abuf[:, (i * 2 + 1) * 512:(i * 2 + 1) * 512 + 512],
                         s1_ps[:, 512:1024])
                av2 = abuf[:, :].rearrange("k (s comp c) -> k s comp c", s=8, comp=2)
                for comp in range(2):
                    q.dma("y", dstA[comp, s2b * 8:(s2b + 1) * 8, :, :].rearrange("s k c -> k s c"),
                          av2[:, :, comp, :])
        q.barrier()
        cs1.close()
        cs2 = ExitStack()
        arb = cs2.enter_context(nc.sbuf_tensor("arb", [64, 2 * 4 * 512], BF16))
        aib = cs2.enter_context(nc.sbuf_tensor("aib", [64, 2 * 4 * 512], BF16))
        le = cs2.enter_context(nc.sbuf_tensor("le", [64, 256], BF16))
        hpb = cs2.enter_context(nc.sbuf_tensor("hpb", [128, 2 * 4 * 512], BF16))
        x_psa = cs2.enter_context(nc.psum_tensor("x_ps", [128, 512], F32))
        x_psb = cs2.enter_context(nc.psum_tensor("x_psB", [128, 512], F32))
        q.dma("y", le[:, 0:128], le1_d[:, :])
        q.dma("y", le[:, 128:256], le2_d[:, :])
        for kb in range(17):
            n = min(4, K1H - kb * 4)
            par = kb % 2
            q.dma("y", arb[:, par * 2048:par * 2048 + n * 512].rearrange("p (k c) -> p k c", k=n),
                  Ah_d[0, :, kb * 4:kb * 4 + n, :].rearrange("s k c -> p_is_s k c".replace("p_is_s", "s")))
            q.dma("y", aib[:, par * 2048:par * 2048 + n * 512].rearrange("p (k c) -> p k c", k=n),
                  Ah_d[1, :, kb * 4:kb * 4 + n, :])
            for kk in range(n):
                k1 = kb * 4 + kk
                xp = x_psa if k1 % 2 == 0 else x_psb
                q.mm(xp[:, 0:512], le[:, 0:128],
                     arb[:, par * 2048 + kk * 512:par * 2048 + (kk + 1) * 512],
                     start=True, stop=False)
                q.mm(xp[:, 0:512], le[:, 128:256],
                     aib[:, par * 2048 + kk * 512:par * 2048 + (kk + 1) * 512],
                     start=False, stop=True)
                q.mm_fence()
                q.op("s", "copy", hpb[:, par * 2048 + kk * 512:par * 2048 + (kk + 1) * 512],
                     xp[:, 0:512])
            hps_sl = hpb[:, par * 2048:par * 2048 + n * 512]
            q.dma("y", Hp_d[kb * 4:kb * 4 + n, :, :].rearrange("k p c -> p k c"),
                  hps_sl.rearrange("p (k c) -> p k c", k=n))
            q.dma("y", Hs_d[kb * 4:kb * 4 + n, 64:128, :].rearrange("k p c -> p k c"),
                  hpb[0:64, par * 2048:par * 2048 + n * 512].rearrange("p (k c) -> p k c", k=n))
            q.dma("y", Hs_d[kb * 4:kb * 4 + n, 0:64, :].rearrange("k p c -> p k c"),
                  hpb[64:128, par * 2048:par * 2048 + n * 512].rearrange("p (k c) -> p k c", k=n))
        arb2 = cs2.enter_context(nc.sbuf_tensor("ar2", [64, 2 * 4 * 512], BF16))
        aib2 = cs2.enter_context(nc.sbuf_tensor("ai2", [64, 2 * 4 * 512], BF16))
        la = cs2.enter_context(nc.sbuf_tensor("la", [128, 256], BF16))
        hpsb = cs2.enter_context(nc.sbuf_tensor("hps", [128, 2 * 4 * 1024], BF16))
        ssb = cs2.enter_context(nc.sbuf_tensor("ss", [128, 2 * 1024], BF16))
        bbb = cs2.enter_context(nc.sbuf_tensor("bb", [128, 2 * 4 * 512], BF16))
        b_psa = cs2.enter_context(nc.psum_tensor("b_ps", [128, 512], F32))
        b_psb = cs2.enter_context(nc.psum_tensor("b_psB", [128, 512], F32))
        q.dma("y", la[:, 0:64], lars_d[:, :])
        q.dma("y", la[:, 64:128], lars2_d[:, :])
        q.dma("y", la[:, 128:192], lais_d[:, :])
        q.dma("y", la[:, 192:256], lais2_d[:, :])
        for kb in range(17):
            n = min(4, K1H - kb * 4)
            par = kb % 2
            ao = par * 2048
            q.dma("y", arb2[:, ao:ao + n * 512].rearrange("p (k c) -> p k c", k=n),
                  Av_d[0, :, kb * 4:kb * 4 + n, :])
            q.dma("y", aib2[:, ao:ao + n * 512].rearrange("p (k c) -> p k c", k=n),
                  Av_d[1, :, kb * 4:kb * 4 + n, :])
            ho = par * 4096
            q.dma("y", hpsb[:, ho:ho + n * 1024].rearrange("p (k two c) -> k p (two c)", k=n, two=2)
                  if False else hpsb[:, ho:ho + n * 512].rearrange("p (k c) -> p k c", k=n),
                  Hp_d[kb * 4:kb * 4 + n, :, :].rearrange("k p c -> p k c"))
            q.dma("y", hpsb[:, ho + 2048:ho + 2048 + n * 512].rearrange("p (k c) -> p k c", k=n),
                  Hs_d[kb * 4:kb * 4 + n, :, :].rearrange("k p c -> p k c"))
            for kk in range(n):
                k1 = kb * 4 + kk
                xp = x_psa if k1 % 2 == 0 else x_psb
                bp = b_psa if k1 % 2 == 0 else b_psb
                sso = (k1 % 2) * 1024
                q.mm(xp[:, 0:512], le[:, 0:128], arb2[:, ao + kk * 512:ao + (kk + 1) * 512],
                     start=True, stop=False)
                q.mm(xp[:, 0:512], le[:, 128:256], aib2[:, ao + kk * 512:ao + (kk + 1) * 512],
                     start=False, stop=True)
                q.mm_fence()
                q.op("v", "tensor_tensor", ssb[:, sso:sso + 512], xp[:, 0:512],
                     hpsb[:, ho + kk * 512:ho + (kk + 1) * 512], ALU.mult)
                q.op("v", "tensor_tensor", ssb[:, sso + 512:sso + 1024], xp[:, 0:512],
                     hpsb[:, ho + 2048 + kk * 512:ho + 2048 + (kk + 1) * 512], ALU.mult)
                q.mm(bp[0:64, 0:512], la[:, 0:64], ssb[:, sso:sso + 512], start=True, stop=False)
                q.mm(bp[0:64, 0:512], la[:, 64:128], ssb[:, sso + 512:sso + 1024], start=False, stop=True)
                q.mm(bp[64:128, 0:512], la[:, 128:192], ssb[:, sso:sso + 512], start=True, stop=False)
                q.mm(bp[64:128, 0:512], la[:, 192:256], ssb[:, sso + 512:sso + 1024], start=False, stop=True)
                q.mm_fence()
                q.op("s", "copy", bbb[:, ao + kk * 512:ao + (kk + 1) * 512], bp[:, 0:512])
            q.dma("y", B_d[0, kb * 4:kb * 4 + n, :, :].rearrange("k t c -> t k c"),
                  bbb[0:64, ao:ao + n * 512].rearrange("p (k c) -> p k c", k=n))
            q.dma("y", B_d[1, kb * 4:kb * 4 + n, :, :].rearrange("k t c -> t k c"),
                  bbb[64:128, ao:ao + n * 512].rearrange("p (k c) -> p k c", k=n))
        mbr = cs2.enter_context(nc.sbuf_tensor("mbr", [K1H, 64 * 64], BF16))
        mbmi = cs2.enter_context(nc.sbuf_tensor("mbmi", [K1H, 64 * 64], BF16))
        bgrb = cs2.enter_context(nc.sbuf_tensor("bgr", [K1H, 2 * 4 * 1024], BF16))
        ytb = cs2.enter_context(nc.sbuf_tensor("yt", [64, 2 * 4 * 512], BF16))
        y_psa = cs2.enter_context(nc.psum_tensor("y_ps", [64, 512], F32))
        y_psb = cs2.enter_context(nc.psum_tensor("y_psB", [64, 512], F32))
        q.dma("y", mbr[:, :], mbr_d[:, :])
        q.dma("y", mbmi[:, :], mbmi_d[:, :])
        yxv = yx_in.rearrange("(t1 t2) c -> t2 t1 c", t2=64)
        for tb in range(16):
            par = tb % 2
            bo = par * 4096
            q.dma("y", bgrb[:, bo:bo + 2048].rearrange("p (t c) -> p t c", t=4),
                  B_d[0, :, tb * 4:(tb + 1) * 4, :])
            q.dma("y", bgrb[:, bo + 2048:bo + 4096].rearrange("p (t c) -> p t c", t=4),
                  B_d[1, :, tb * 4:(tb + 1) * 4, :])
            for kk in range(4):
                t2 = tb * 4 + kk
                yp = y_psa if t2 % 2 == 0 else y_psb
                q.mm(yp[:, 0:512], mbr[:, t2 * 64:(t2 + 1) * 64],
                     bgrb[:, bo + kk * 512:bo + (kk + 1) * 512],
                     start=True, stop=False)
                q.mm(yp[:, 0:512], mbmi[:, t2 * 64:(t2 + 1) * 64],
                     bgrb[:, bo + 2048 + kk * 512:bo + 2048 + (kk + 1) * 512],
                     start=False, stop=True)
                q.mm_fence()
                q.op("s", "copy", ytb[:, par * 2048 + kk * 512:par * 2048 + (kk + 1) * 512],
                     yp[:, 0:512])
            q.dma("y", yxv[tb * 4:(tb + 1) * 4, :, :].rearrange("t2 t1 c -> t1 t2 c"),
                  ytb[:, par * 2048:par * 2048 + 2048].rearrange("p (t c) -> p t c", t=4))
        cs2.close()
        q.cc("AllGather", ALU.bypass, PAIRS, [yx_in.ap().opt()], [yx_out.ap().opt()])

        q.barrier()
        Yb = sb("Yb", [128, NCH * T], BF16)
        q1a = sb("q1a", [128, NCH * T], BF16)
        rowx = sb("rowx", [1, 2 * T], BF16)
        q.barrier()
        x2_scope = ExitStack()
        X = x2_scope.enter_context(nc.sbuf_tensor("Xs2", [128, NCH * T], F32))
        q.dma("y", X[:, :], Xsp_d[:, :])
        yvfm = q1a
        q.barrier()
        with nc.sbuf_tensor("ut2", [128, C], BF16) as ut2, \
             nc.sbuf_tensor("yct", [128, C], BF16) as yct, \
             nc.sbuf_tensor("ytmp", [128, C], BF16) as ytmp, \
             nc.sbuf_tensor("sel2", [128, 2], F32) as sel2, \
             nc.sbuf_tensor("wbuf5", [128, NCH * 128], BF16) as wbuf5, \
             nc.sbuf_tensor("boc", [128, NCH], F32) as boc, \
             nc.psum_tensor("tp_ps", [128, 128], BF16) as tp_ps, \
             nc.psum_tensor("tp_psB", [128, 128], BF16) as tp_psB, \
             nc.psum_tensor("wo_ps", [128, 512], F32) as wo_ps, \
             nc.psum_tensor("wo_psB", [128, 512], F32) as wo_psB:
            q.dma("y", boc[:, :], bod[:, :])
            q.dma("y", sel2[:, :], seld[:, :])
            for tc in range(16):
                for part in range(2):
                    base = part * 4096
                    q.dma("y", ytmp[:, 0:512], yx_out[base + tc * 128:base + (tc + 1) * 128, :])
                    q.dma("y", ytmp[:, 512:1024], yx_out[base + 2048 + tc * 128:base + 2048 + (tc + 1) * 128, :])
                    q.op("v", "tensor_scalar", ytmp[:, 0:512], ytmp[:, 0:512],
                         sel2[:, 0:1], None, ALU.mult)
                    q.op("v", "scalar_tensor_tensor", yct[:, part * 512:(part + 1) * 512],
                         ytmp[:, 512:1024], sel2[:, 1:2], ytmp[:, 0:512], ALU.mult, ALU.add)
                q.dma("y", ut2[:, :], u_d[tc * 128:(tc + 1) * 128, :])
                q.op("v", "tensor_tensor", yct[:, :], yct[:, :], ut2[:, :], ALU.mult)
                for ch in range(NCH):
                    tp = tp_ps if ch % 2 == 0 else tp_psB
                    q.op("t", "transpose", tp[:, :], yct[:, ch * 128:(ch + 1) * 128],
                         ident[:, :])
                    q.op("s", "copy", yvfm[:, ch * T + tc * 128:ch * T + (tc + 1) * 128],
                         tp[:, :])

            def wo_cb(j, sl, ps):
                q.op("v", "tensor_scalar", Yb[:, j * T + sl * 512:j * T + sl * 512 + 512],
                     ps[:, 0:512], boc[:, j:j + 1], None, ALU.add)

            emit_fm_matmul(q, yvfm, wod, NCH, NCH, [wo_ps, wo_psB], wo_cb, wbuf5)
        emit_postln_residual(q, S, Yb, X)

        q.barrier()
        with nc.sbuf_tensor("wpl", [128, H], F32) as wpl, \
             nc.sbuf_tensor("acc", [128, 40], F32) as acc, \
             nc.sbuf_tensor("wqb", [128, H], BF16) as wqb, \
             nc.psum_tensor("wq_ps", [1, 512], F32) as wq_ps, \
             nc.psum_tensor("wc_ps", [128, 64], F32) as wc_ps:
            for wi, (src, dst, npl, wid) in enumerate(((w1d, w1q_d, NCH, H), (w2d, w2q_d, 32, C))):
                for p in range(npl):
                    q.dma("y", wpl[:, 0:wid], src[p])
                    q.op("s", "activation", wqb[:, 0:wid], wpl[:, 0:wid], AF.Abs,
                         accum_out=acc[:, p:p + 1])
                q.mm(wq_ps[0:1, 0:npl], S.ones_f[:, 0:1], acc[:, 0:npl], start=True, stop=True)
                q.mm_fence()
                q.op("v", "reduce_sum", wsc[0:1, wi * 4:wi * 4 + 1], wq_ps[0:1, 0:npl], AX)
                q.op("v", "tensor_scalar", wsc[0:1, wi * 4 + 1:wi * 4 + 2],
                     wsc[0:1, wi * 4:wi * 4 + 1], 1.0 / (npl * 128 * wid), None, ALU.mult)
                q.op("v", "tensor_scalar", wsc[0:1, wi * 4 + 1:wi * 4 + 2],
                     wsc[0:1, wi * 4 + 1:wi * 4 + 2], 1e-5, None, ALU.max)
                q.op("v", "reciprocal", wsc[0:1, wi * 4 + 2:wi * 4 + 3],
                     wsc[0:1, wi * 4 + 1:wi * 4 + 2])
                q.mm(wc_ps[:, 0:1], S.ones_fr[0:1, :], wsc[0:1, wi * 4 + 2:wi * 4 + 3],
                     start=True, stop=True)
                q.mm(wc_ps[:, 1:2], S.ones_fr[0:1, :], wsc[0:1, wi * 4 + 1:wi * 4 + 2],
                     start=True, stop=True)
                q.mm_fence()
                q.op("v", "tensor_copy", wcol[:, 2 * wi:2 * wi + 1], wc_ps[:, 0:1])
                q.op("v", "tensor_copy", wcol[:, 2 * wi + 1:2 * wi + 2], wc_ps[:, 1:2])
                for p in range(npl):
                    q.dma("y", wpl[:, 0:wid], src[p])
                    q.op("v", "tensor_scalar", wpl[:, 0:wid], wpl[:, 0:wid],
                         wcol[:, 2 * wi:2 * wi + 1], RND, ALU.mult, ALU.add)
                    q.op("v", "tensor_scalar", wpl[:, 0:wid], wpl[:, 0:wid],
                         -RND, -1.0, ALU.add, ALU.max)
                    q.op("v", "tensor_scalar", wqb[:, 0:wid], wpl[:, 0:wid],
                         1.0, None, ALU.min)
                    q.dma("y", dst[p], wqb[:, 0:wid])

        q.barrier()
        with nc.sbuf_tensor("ln1b", [128, NCH * T], BF16) as ln1b, \
             nc.sbuf_tensor("l1g", [128, 2 * NCH], F32) as l1g, \
             nc.sbuf_tensor("mx", [128, 16], BF16) as mx, \
             nc.sbuf_tensor("mrow16", [16, 128], BF16) as mrow16, \
             nc.psum_tensor("trb_ps", [128, 128], BF16) as trb_ps, \
             nc.psum_tensor("bcx_ps", [128, 1024], F32) as bcx_ps:
            q.dma("y", l1g[:, 0:NCH], l1gd[:, :])
            q.dma("y", l1g[:, NCH:2 * NCH], l1bd[:, :])
            mrowf = S.srows[0:1, 0:T]
            emit_ln_stats(q, S, X)
            for sl in range(4):
                q.mm(S.bc_ps[:, 0:512], S.ones_b[0:1, :], S.mb[0:1, sl * 512:(sl + 1) * 512])
                q.mm(S.bc_ps[:, 512:1024], S.ones_b[0:1, :], S.rb[0:1, sl * 512:(sl + 1) * 512])
                q.mm_fence()
                for ch in range(NCH):
                    o = ch * T + sl * 512
                    q.op("v", "tensor_tensor", S.t1[:, 0:512], X[:, o:o + 512],
                         S.bc_ps[:, 0:512], ALU.subtract)
                    q.op("v", "tensor_tensor", S.t1[:, 0:512], S.t1[:, 0:512],
                         S.bc_ps[:, 512:1024], ALU.mult)
                    q.op("v", "tensor_scalar", ln1b[:, o:o + 512], S.t1[:, 0:512],
                         l1g[:, ch:ch + 1], l1g[:, NCH + ch:NCH + ch + 1], ALU.mult, ALU.add)
            q.dma("y", Xsp_d[:, :], X[:, :])
            for ch in range(NCH):
                q.op("s", "activation", X[:, ch * T:(ch + 1) * T], ln1b[:, ch * T:(ch + 1) * T],
                     AF.Abs)
            q.op("v", "tensor_tensor", X[:, 0:T], X[:, 0:T], X[:, T:2 * T], ALU.max)
            q.op("v", "tensor_tensor", X[:, 2 * T:3 * T], X[:, 2 * T:3 * T], X[:, 3 * T:4 * T], ALU.max)
            q.op("v", "tensor_tensor", X[:, 4 * T:5 * T], X[:, 4 * T:5 * T], X[:, 5 * T:6 * T], ALU.max)
            q.op("v", "tensor_tensor", X[:, 6 * T:7 * T], X[:, 6 * T:7 * T], X[:, 7 * T:8 * T], ALU.max)
            q.op("v", "tensor_tensor", X[:, 0:T], X[:, 0:T], X[:, 2 * T:3 * T], ALU.max)
            q.op("v", "tensor_tensor", X[:, 4 * T:5 * T], X[:, 4 * T:5 * T], X[:, 6 * T:7 * T], ALU.max)
            q.op("v", "tensor_tensor", X[:, 0:T], X[:, 0:T], X[:, 4 * T:5 * T], ALU.max)
            q.op("s", "copy", Yb[:, 0:T], X[:, 0:T])
            for tc in range(16):
                q.op("t", "transpose", trb_ps[:, :], Yb[:, tc * 128:(tc + 1) * 128], ident[:, :])
                q.op("v", "reduce_max", mx[:, tc:tc + 1], trb_ps[:, :], AX)
            q.op("t", "transpose", trb_ps[0:16, :], mx[:, 0:16], ident[:, :])
            q.op("s", "copy", mrow16[:, :], trb_ps[0:16, 0:128])
            q.dma("y", sxm_d[:, :], mrow16[:, :])
            q.dma("g", mrowf[0:1, :], sxm_d.rearrange("a b -> () (a b)"))
            q.op("v", "tensor_scalar", mrowf[0:1, :], mrowf[0:1, :], 1e-5, None, ALU.max)
            q.op("v", "tensor_scalar", rowx[0:1, T:2 * T], mrowf[0:1, :],
                 wsc[0:1, 1:2], 1.0 / 127.0, ALU.mult, ALU.mult)
            q.op("s", "activation", mrowf[0:1, :], mrowf[0:1, :], AF.Ln)
            q.op("s", "activation", mrowf[0:1, :], mrowf[0:1, :], AF.Exp, scale=-1.0)
            q.op("v", "tensor_scalar", rowx[0:1, 0:T], mrowf[0:1, :], 127.0, None, ALU.mult)
            for sl in range(4):
                q.mm(bcx_ps[:, 0:512], S.ones_b[0:1, :], rowx[0:1, sl * 512:(sl + 1) * 512])
                q.mm(bcx_ps[:, 512:1024], S.ones_b[0:1, :], rowx[0:1, T + sl * 512:T + (sl + 1) * 512])
                q.mm_fence()
                for ch in range(NCH):
                    o = ch * T + sl * 512
                    q.op("v", "tensor_tensor", S.t1[:, 0:512], ln1b[:, o:o + 512],
                         bcx_ps[:, 0:512], ALU.mult)
                    q.op("v", "tensor_scalar", S.t1[:, 0:512], S.t1[:, 0:512],
                         RND, -RND, ALU.add, ALU.add)
                    q.op("v", "tensor_tensor", q1a[:, o:o + 512], S.t1[:, 0:512],
                         bcx_ps[:, 512:1024], ALU.mult)
        x2_scope.close()

        q.barrier()
        cs3 = ExitStack()
        wbig_a = cs3.enter_context(nc.sbuf_tensor("wbig_a", [128, 8 * 1024], BF16))
        wbig_b = cs3.enter_context(nc.sbuf_tensor("wbig_b", [128, 8 * 1024], BF16))
        gbf2 = cs3.enter_context(nc.sbuf_tensor("gbf", [128, 4 * H], BF16))
        gsl = cs3.enter_context(nc.sbuf_tensor("gsl", [128, 1024], F32))
        acc2 = cs3.enter_context(nc.sbuf_tensor("acc2", [128, 4], F32))
        acc2b = cs3.enter_context(nc.sbuf_tensor("acc2b", [128, 4], F32))
        cols2 = cs3.enter_context(nc.sbuf_tensor("cols", [128, 64], F32))
        rows2 = cs3.enter_context(nc.sbuf_tensor("rows2", [1, 3 * H], BF16))
        b2c = cs3.enter_context(nc.sbuf_tensor("b2sb", [128, NCH], F32))
        b1_ps = cs3.enter_context(nc.psum_tensor("b1_ps", [128, 512], F32))
        b1_psB = cs3.enter_context(nc.psum_tensor("b1_psB", [128, 512], F32))
        tq_psa = cs3.enter_context(nc.psum_tensor("tq_ps", [128, 128], BF16))
        tq_psb = cs3.enter_context(nc.psum_tensor("tq_psB", [128, 128], BF16))
        bl2_ps = cs3.enter_context(nc.psum_tensor("bl2_ps", [128, 1024], F32))
        g2_ps = S.bc_ps
        q.dma("g", rows2[0:1, 0:H], b1rd[0:1, :])
        q.dma("g", rows2[0:1, H:2 * H], l2gd[0:1, :])
        q.dma("g", rows2[0:1, 2 * H:3 * H], l2bd[0:1, :])
        q.dma("y", b2c[:, :], b2cd[:, :])
        for tcq in range(4):
            for sub in range(4):
                q.op("g", "memset", cols2[:, sub * 16:sub * 16 + 2], 0.0)
            for quarter in range(4):
                wb = wbig_a if quarter % 2 == 0 else wbig_b
                q.dma("y", wb[:, :].rearrange("p (k c) -> p k c", k=NCH),
                      w1q_d[:, :, quarter * 1024:(quarter + 1) * 1024].rearrange("k p c -> p k c"))
                for hs in range(2):
                    for sub in range(4):
                        tc = 4 * tcq + sub
                        idx = (quarter * 2 + hs) * 4 + sub
                        bp = b1_ps if idx % 2 == 0 else b1_psB
                        for k in range(NCH):
                            q.mm(bp[:, 0:512],
                                 q1a[:, k * T + tc * 128:k * T + (tc + 1) * 128],
                                 wb[:, k * 1024 + hs * 512:k * 1024 + hs * 512 + 512],
                                 start=(k == 0), stop=False)
                        ho = quarter * 1024 + hs * 512
                        q.mm(bp[:, 0:512], S.ones_b[0:1, 0:128],
                             rows2[0:1, ho:ho + 512], start=False, stop=True)
                        q.mm_fence()
                        par2 = idx % 2
                        co = sub * 16
                        gs = gsl[:, par2 * 512:par2 * 512 + 512]
                        q.op("s", "activation", gs, bp[:, 0:512], AF.Gelu,
                             accum_out=acc2[:, par2 * 2:par2 * 2 + 1])
                        q.op("v", "tensor_tensor", cols2[:, co:co + 1], cols2[:, co:co + 1],
                             acc2[:, par2 * 2:par2 * 2 + 1], ALU.add)
                        q.op("s", "activation", S.t1[:, 0:512], gs, AF.Square,
                             accum_out=acc2[:, par2 * 2 + 1:par2 * 2 + 2])
                        q.op("v", "tensor_tensor", cols2[:, co + 1:co + 2], cols2[:, co + 1:co + 2],
                             acc2[:, par2 * 2 + 1:par2 * 2 + 2], ALU.add)
                        q.op("s", "copy", gbf2[:, sub * H + ho:sub * H + ho + 512], gs)
            for sub in range(4):
                tc = 4 * tcq + sub
                co = sub * 16
                cols = cols2[:, co:co + 16]
                gbf = gbf2[:, sub * H:(sub + 1) * H]
                zt = gbf
                q.op("v", "tensor_scalar", cols[:, 4:5], cols[:, 0:1], 1.0 / H, None, ALU.mult)
                q.op("v", "tensor_tensor", cols[:, 5:6], cols[:, 4:5], cols[:, 4:5], ALU.mult)
                q.op("v", "scalar_tensor_tensor", cols[:, 5:6], cols[:, 1:2], 1.0 / H,
                     cols[:, 5:6], ALU.mult, ALU.subtract)
                q.op("v", "tensor_scalar", cols[:, 5:6], cols[:, 5:6], float(EPS), None, ALU.add)
                q.op("s", "activation", cols[:, 6:7], cols[:, 5:6], AF.Ln)
                q.op("s", "activation", cols[:, 6:7], cols[:, 6:7], AF.Exp, scale=-0.5)
                q.op("v", "tensor_tensor", cols[:, 7:8], cols[:, 4:5], cols[:, 6:7], ALU.mult)
                q.op("v", "tensor_scalar", cols[:, 7:8], cols[:, 7:8], -1.0, None, ALU.mult)
                for hs in range(8):
                    ho = hs * 512
                    q.mm(g2_ps[:, 0:512], S.ones_b[0:1, :], rows2[0:1, H + ho:H + ho + 512])
                    q.mm(g2_ps[:, 512:1024], S.ones_b[0:1, :], rows2[0:1, 2 * H + ho:2 * H + ho + 512])
                    q.mm_fence()
                    q.op("v", "tensor_scalar", zt[:, ho:ho + 512], gbf[:, ho:ho + 512],
                         cols[:, 6:7], cols[:, 7:8], ALU.mult, ALU.add)
                    q.op("v", "tensor_tensor", zt[:, ho:ho + 512], zt[:, ho:ho + 512],
                         g2_ps[:, 0:512], ALU.mult)
                    q.op("v", "tensor_tensor", zt[:, ho:ho + 512], zt[:, ho:ho + 512],
                         g2_ps[:, 512:1024], ALU.add)
                q.op("v", "tensor_reduce", cols[:, 8:9], zt[:, :], AX, ALU.max,
                     apply_absolute_value=True)
                q.op("v", "tensor_scalar", cols[:, 8:9], cols[:, 8:9], 1e-5, None, ALU.max)
                q.op("v", "tensor_scalar", cols[:, 9:10], cols[:, 8:9],
                     wcol[:, 3:4], 1.0 / 127.0, ALU.mult, ALU.mult)
                q.op("s", "activation", cols[:, 10:11], cols[:, 8:9], AF.Ln)
                q.op("s", "activation", cols[:, 10:11], cols[:, 10:11], AF.Exp, scale=-1.0)
                q.op("v", "tensor_scalar", cols[:, 11:12], cols[:, 10:11], 127.0, None, ALU.mult)
                for hs in range(8):
                    ho = hs * 512
                    q.op("v", "tensor_scalar", S.t1[:, 0:512], zt[:, ho:ho + 512],
                         cols[:, 11:12], RND, ALU.mult, ALU.add)
                    q.op("v", "tensor_scalar", S.t1[:, 0:512], S.t1[:, 0:512],
                         -RND, None, ALU.add)
                    q.op("v", "tensor_scalar", zt[:, ho:ho + 512], S.t1[:, 0:512],
                         cols[:, 9:10], None, ALU.mult)
                for j2 in range(32):
                    tq = tq_psa if j2 % 2 == 0 else tq_psb
                    q.op("t", "transpose", tq[:, :], zt[:, j2 * 128:(j2 + 1) * 128], ident[:, :])
                    q.op("s", "copy", zt[:, j2 * 128:(j2 + 1) * 128], tq[:, :])
            for sp in range(2):
                for j2b in range(4):
                    wb = wbig_a if j2b % 2 == 0 else wbig_b
                    q.dma("y", wb[:, :].rearrange("p (j c) -> p j c", j=8),
                          w2q_d[j2b * 8:(j2b + 1) * 8].rearrange("j p c -> p j c"))
                    for ls in range(2):
                        sub = sp * 2 + ls
                        bp2 = bl2_ps if ls == 0 else g2_ps
                        for jj in range(8):
                            j2 = j2b * 8 + jj
                            for half in range(2):
                                q.mm(bp2[:, half * 512:(half + 1) * 512],
                                     gbf2[:, sub * H + j2 * 128:sub * H + (j2 + 1) * 128],
                                     wb[:, jj * C + half * 512:jj * C + half * 512 + 512],
                                     start=(j2 == 0), stop=(j2 == 31))
                q.mm_fence()
                for ls in range(2):
                    sub = sp * 2 + ls
                    tc = 4 * tcq + sub
                    bp2 = bl2_ps if ls == 0 else g2_ps
                    ystage = gbf2[:, sub * H:sub * H + C]
                    q.op("s", "copy", ystage[:, 0:C], bp2[:, 0:1024])
                    for co in range(NCH):
                        tq = tq_psa if co % 2 == 0 else tq_psb
                        q.op("t", "transpose", tq[:, :], ystage[:, co * 128:(co + 1) * 128],
                             ident[:, :])
                        q.op("v", "tensor_scalar", Yb[:, co * T + tc * 128:co * T + (tc + 1) * 128],
                             tq[:, :], b2c[:, co:co + 1], None, ALU.add)
        cs3.close()
        q.barrier()
        with nc.sbuf_tensor("Xs3", [128, NCH * T], F32) as X3s:
            q.dma("y", X3s[:, :], Xsp_d[:, :])
            emit_postln_residual(q, S, Yb, X3s)
            q.dma("y", X4out[:, :], X3s[:, :])
            q.finish()
    return nc


def fused_in_maps(inputs):
    f32 = np.float32
    x = np.asarray(inputs["L0_attn_x"], f32)
    w_qkv = np.asarray(inputs["L0_attn_w_qkv"], f32)
    wc1 = np.asarray(inputs["L0_attn_w_conv1"], f32)
    wg = np.asarray(inputs["L0_ffn_w_gate"], np.int32)
    wu = np.asarray(inputs["L0_ffn_w_up"], np.int32)
    wd = np.asarray(inputs["L0_ffn_w_down"], np.int32)
    P1 = {k[8:]: np.asarray(v, f32) for k, v in inputs.items() if k.startswith("L1_attn_")}
    P2 = {k[7:]: np.asarray(v, f32) for k, v in inputs.items() if k.startswith("L1_ffn_")}
    fc = _fft_consts()
    shared = {
        "Wqkv": chunked_lhsT(np.ascontiguousarray(w_qkv.T)),
        "wc1": _cols(wc1),
        "wg": chunked_lhsT(np.ascontiguousarray(wg.T)).astype(np.int32),
        "wu": chunked_lhsT(np.ascontiguousarray(wu.T)).astype(np.int32),
        "wd": np.ascontiguousarray(wd.T.reshape(24, 128, 1024)).astype(np.int32),
        "Wu": np.ascontiguousarray(P1["Wu"].reshape(8, 128, 1024)),
        "Wv": np.ascontiguousarray(P1["Wv"].reshape(8, 128, 1024)),
        "bu": P1["bu"][None, :], "bv": P1["bv"][None, :],
        "W1": P1["W1"], "b1c": P1["b1"][:, None],
        "W2": P1["W2"], "b2c": P1["b2"][:, None],
        "peT": _pos_emb_T(), "dkc": _decay_cols(), **fc,
        "Wo_l": chunked_lhsT(P1["Wo"]), "bo_c": _cols(P1["bo"]),
        "identin": np.eye(128, dtype=BFD),
        "ln1g_c": _cols(P2["ln1_g"]), "ln1b_c": _cols(P2["ln1_b"]),
        "w1T": np.ascontiguousarray(P2["w1"].T.reshape(8, 128, 4096)),
        "b1r": P2["b1"][None, :],
        "ln2g_r": P2["ln2_g"][None, :], "ln2b_r": P2["ln2_b"][None, :],
        "w2T": np.ascontiguousarray(P2["w2"].T.reshape(32, 128, 1024)),
        "b2_c": _cols(P2["b2"]),
    }
    maps = []
    for core in range(8):
        b, h = core // 2, core % 2
        csl = slice(512 * h, 512 * h + 512)
        sel = np.zeros((128, 2), f32)
        sel[:, h] = 1.0
        m = dict(shared)
        m["X"] = fm_layout(np.ascontiguousarray(x[b, 2048 * h:2048 * h + 2048, :].T))
        m["W3o"] = np.ascontiguousarray(P1["W3"][:, csl])
        m["b3r"] = np.ascontiguousarray(P1["b3"][None, csl])
        m["selc"] = sel
        maps.append(m)
    return maps


def kernel(**inputs):
    nc = build_fused()
    maps = fused_in_maps(inputs)
    res = run_bass_kernel_spmd(nc, maps, core_ids=list(range(8))).results
    out = np.empty((4, 4096, 1024), np.float32)
    for core in range(8):
        b, h = core // 2, core % 2
        out[b, 2048 * h:2048 * h + 2048, :] = un_fm(np.asarray(res[core]["X4"]), NCH, T).T
    return out


# version 17 (final)
# speedup vs baseline: 1.0491x; 1.0491x over previous
import math
import numpy as np
import ml_dtypes
from contextlib import ExitStack

import concourse.bass as bass
import concourse.mybir as mybir
from concourse.bass_utils import run_bass_kernel_spmd

F32 = mybir.dt.float32
BF16 = mybir.dt.bfloat16
I32 = mybir.dt.int32
AF = mybir.ActivationFunctionType
ALU = mybir.AluOpType
AX = mybir.AxisListType.X
BFD = ml_dtypes.bfloat16

T = 2048
C = 1024
NCH = 8
EPS = 1e-5
RND = 12582912.0
N_FFT = 8192
K1H = 65


def _fft_consts():
    s1 = np.arange(64); k1 = np.arange(K1H); s2 = np.arange(64)
    k2 = np.arange(64); t1 = np.arange(64)
    base = np.exp(-2j * np.pi * np.outer(s1, k1) / 128)
    tw = np.exp(-2j * np.pi * np.outer(s2, k1) / N_FFT)
    Cc = base[None, :, :] * tw[:, None, :]
    E = np.exp(-2j * np.pi * np.outer(s2, k2) / 64)
    Finv = np.exp(2j * np.pi * np.outer(np.arange(64), k2) / 64)
    w_k1 = np.where((k1 == 0) | (k1 == 64), 1.0, 2.0) / N_FFT
    Mt2 = np.empty((64, 64, K1H), np.complex128)
    for tt in range(64):
        Mt2[tt] = np.exp(2j * np.pi * (64 * t1[:, None] + tt) * k1[None, :] / N_FFT) * w_k1[None, :]
    c_re = np.ascontiguousarray(Cc.real.transpose(1, 0, 2).reshape(64, 64 * K1H)).astype(BFD)
    c_im = np.ascontiguousarray(Cc.imag.transpose(1, 0, 2).reshape(64, 64 * K1H)).astype(BFD)
    LE1 = np.concatenate([E.real, E.imag], 1).astype(BFD)
    LE2 = np.concatenate([-E.imag, E.real], 1).astype(BFD)
    Fr = Finv.real.T; Fi = Finv.imag.T
    La_rS = np.concatenate([Fr, -Fr], 0).astype(BFD)
    La_rS2 = np.concatenate([-Fi, -Fi], 0).astype(BFD)
    La_iS = np.concatenate([Fi, -Fi], 0).astype(BFD)
    La_iS2 = np.concatenate([Fr, Fr], 0).astype(BFD)
    Mb_r = np.ascontiguousarray(Mt2.real.transpose(2, 0, 1).reshape(K1H, 64 * 64)).astype(BFD)
    Mb_mi = np.ascontiguousarray((-Mt2.imag).transpose(2, 0, 1).reshape(K1H, 64 * 64)).astype(BFD)
    return dict(c_re=c_re, c_im=c_im, LE1=LE1, LE2=LE2,
                La_rS=La_rS, La_rS2=La_rS2, La_iS=La_iS, La_iS2=La_iS2,
                Mb_r=Mb_r, Mb_mi=Mb_mi)


def _pos_emb_T():
    L = 4096
    positions = np.arange(L, dtype=np.float64)[:, None]
    div = np.exp(np.arange(0, 64, 2, dtype=np.float64) * -(np.log(10000.0) / 64))
    pe = np.zeros((L, 64), np.float64)
    pe[:, 0::2] = np.sin(positions * div)
    pe[:, 1::2] = np.cos(positions * div)
    return np.ascontiguousarray(pe.T).astype(BFD)


def _decay_cols():
    d = np.exp(-0.01 * np.arange(4096, dtype=np.float64)).astype(np.float32)
    return np.ascontiguousarray(d.reshape(32, 128).T)


def chunked_lhsT(wT, out_cols=128):
    d_in, d_out = wT.shape
    nk, nj = d_in // 128, d_out // out_cols
    a = wT.reshape(nk, 128, nj, out_cols).transpose(2, 1, 0, 3)
    return np.ascontiguousarray(a.reshape(nj, 128, nk * out_cols))


def fm_layout(x2d):
    R, Co = x2d.shape
    nk = R // 128
    return np.ascontiguousarray(x2d.reshape(nk, 128, Co).transpose(1, 0, 2).reshape(128, nk * Co))


def un_fm(x, nk, Co):
    return np.ascontiguousarray(np.asarray(x).reshape(128, nk, Co).transpose(1, 0, 2).reshape(nk * 128, Co))



ENGS = ("t", "v", "s", "g", "y")


class SQ:

    def __init__(self, nc, es, n_dma_sems=24):
        self.nc = nc
        self.obj = {"t": nc.tensor, "v": nc.vector, "s": nc.scalar,
                    "g": nc.gpsimd, "y": nc.sync}
        self.csem = {e: es.enter_context(nc.semaphore(f"c_{e}")) for e in ENGS}
        self.cval = {e: 0 for e in ENGS}
        self.dsem = {e: [es.enter_context(nc.semaphore(f"d_{e}{i}")) for i in range(n_dma_sems)]
                     for e in ("g", "y", "s")}
        self.dval = {e: [0] * n_dma_sems for e in ("g", "y", "s")}
        self.dcur = {e: 0 for e in ("g", "y", "s")}
        self.dopen = {e: False for e in ("g", "y", "s")}
        self.seen = {c: {} for c in ENGS}
        self.writer = {}
        self.readers = {}
        self._last_mm = None
        self._mm_reads = set()
        self._mm_writes = set()

    def _sem_of(self, key):
        if key[0] == "c":
            return self.csem[key[1]]
        return self.dsem[key[1]][key[2]]

    def _wait(self, cons, key, val):
        if key == ("c", cons):
            return
        if key[0] == "d":
            val = self.dval[key[1]][key[2]]
            if self.dcur[key[1]] == key[2] and self.dopen[key[1]]:
                self.dopen[key[1]] = False
                self.dcur[key[1]] = (key[2] + 1) % len(self.dsem[key[1]])
        if self.seen[cons].get(key, 0) < val:
            self.obj[cons].wait_ge(self._sem_of(key), val)
            self.seen[cons][key] = val

    def _self_key(self, cons):
        return ("c", cons)

    def _wait_same_engine_raw(self, cons, key, val):
        if key[0] == "d":
            val = self.dval[key[1]][key[2]]
            if self.dcur[key[1]] == key[2] and self.dopen[key[1]]:
                self.dopen[key[1]] = False
                self.dcur[key[1]] = (key[2] + 1) % len(self.dsem[key[1]])
        if self.seen[cons].get(key, 0) < val:
            self.obj[cons].wait_ge(self._sem_of(key), val)
            self.seen[cons][key] = val

    def _touch_read(self, cons, names):
        for nm in names:
            w = self.writer.get(nm)
            if w is not None:
                key, val = w
                if key == ("c", cons):
                    self._wait_same_engine_raw(cons, key, val)
                else:
                    self._wait(cons, key, val)

    def _touch_write(self, cons, names):
        for nm in names:
            w = self.writer.get(nm)
            if w is not None:
                key, val = w
                if key == ("c", cons):
                    self._wait_same_engine_raw(cons, key, val)
                else:
                    self._wait(cons, key, val)
            for key, val in self.readers.get(nm, {}).items():
                if key == ("c", cons):
                    self._wait_same_engine_raw(cons, key, val)
                else:
                    self._wait(cons, key, val)

    def _record(self, cons, clock, reads, writes):
        for nm in reads:
            self.readers.setdefault(nm, {})[clock[0]] = clock[1]
        for nm in writes:
            self.writer[nm] = clock
            self.readers[nm] = {}

    @staticmethod
    def _nm(ap):
        t = getattr(ap, "tensor", None)
        return t.name if t is not None else None

    def _split(self, a, kw):
        aps = [x for x in a if hasattr(x, "tensor")]
        out = aps[0]
        reads = {self._nm(x) for x in aps[1:]}
        wr = {self._nm(out)}
        for k, v in kw.items():
            if hasattr(v, "tensor"):
                if k == "accum_out":
                    wr.add(self._nm(v))
                else:
                    reads.add(self._nm(v))
        return reads, wr

    def _flush_if_conflict(self, names):
        if self._last_mm is not None and (names & (self._mm_reads | self._mm_writes)):
            self.mm_fence()

    def op(self, cons, name, *a, **kw):
        reads, writes = self._split(a, kw)
        self._flush_if_conflict(reads | writes)
        if cons == "t" and self._last_mm is not None:
            self.mm_fence()
        self._touch_read(cons, reads)
        self._touch_write(cons, writes)
        ins = getattr(self.obj[cons], name)(*a, **kw)
        self.cval[cons] += 1
        ins.then_inc(self.csem[cons], 1)
        clock = (("c", cons), self.cval[cons])
        self._record(cons, clock, reads, writes)
        return ins

    def mm(self, out, lhsT, rhs, start=True, stop=True):
        reads = {self._nm(lhsT), self._nm(rhs)}
        writes = {self._nm(out)}
        self._touch_read("t", reads)
        self._touch_write("t", writes)
        ins = self.obj["t"].matmul(out, lhsT, rhs, start=start, stop=stop)
        self._last_mm = ins
        self._mm_reads |= reads
        self._mm_writes |= writes
        return ins

    def mm_fence(self):
        if self._last_mm is None:
            return
        self.cval["t"] += 1
        self._last_mm.then_inc(self.csem["t"], 1)
        clock = (("c", "t"), self.cval["t"])
        self._record("t", clock, self._mm_reads, self._mm_writes)
        self._last_mm = None
        self._mm_reads = set()
        self._mm_writes = set()

    def dma(self, issuer, out, in_, **kw):
        reads = {self._nm(in_)}
        writes = {self._nm(out)}
        self._flush_if_conflict(reads | writes)
        self._touch_read(issuer, reads)
        self._touch_write(issuer, writes)
        ins = self.obj[issuer].dma_start(out, in_, **kw)
        i = self.dcur[issuer]
        self.dval[issuer][i] += 16
        ins.then_inc(self.dsem[issuer][i], 16)
        self.dopen[issuer] = True
        clock = (("d", issuer, i), self.dval[issuer][i])
        self._record(issuer, clock, reads, writes)
        return ins

    def cc(self, kind, op, groups, ins_, outs_):
        reads = {self._nm(x) for x in ins_}
        writes = {self._nm(x) for x in outs_}
        self._flush_if_conflict(reads | writes)
        self._touch_read("g", reads)
        self._touch_write("g", writes)
        ins = self.obj["g"].collective_compute(kind, op, replica_groups=groups,
                                               ins=ins_, outs=outs_)
        self.cval["g"] += 1
        ins.then_inc(self.csem["g"], 1)
        clock = (("c", "g"), self.cval["g"])
        self._record("g", clock, reads, writes)
        return ins

    def barrier(self):
        self.mm_fence()
        for e in ("g", "y", "s"):
            self.dopen[e] = False
            self.dcur[e] = (self.dcur[e] + 1) % len(self.dsem[e])
        keys = [(("c", e2), self.cval[e2]) for e2 in ENGS if self.cval[e2]]
        for e in ("g", "y", "s"):
            for i, v in enumerate(self.dval[e]):
                if v:
                    keys.append((("d", e, i), v))
        for cons in ENGS:
            for key, val in keys:
                if key == ("c", cons):
                    continue
                if self.seen[cons].get(key, 0) < val:
                    self.obj[cons].wait_ge(self._sem_of(key), val)
                    self.seen[cons][key] = val

    def finish(self):
        self.mm_fence()
        eng = self.obj["y"]
        for e in ENGS:
            if e != "y" and self.cval[e]:
                if self.seen["y"].get(("c", e), 0) < self.cval[e]:
                    eng.wait_ge(self.csem[e], self.cval[e])
                    self.seen["y"][("c", e)] = self.cval[e]
        for e in ("g", "y", "s"):
            for i, v in enumerate(self.dval[e]):
                if v and self.seen["y"].get(("d", e, i), 0) < v:
                    eng.wait_ge(self.dsem[e][i], v)
                    self.seen["y"][("d", e, i)] = v


class SQ_SERIAL:
    def __init__(self, nc, es, n_dma_sems=24):
        self.nc = nc
        self.obj = {"t": nc.tensor, "v": nc.vector, "s": nc.scalar,
                    "g": nc.gpsimd, "y": nc.sync}
        self.csem = {e: es.enter_context(nc.semaphore(f"c_{e}")) for e in ENGS}
        self.cval = {e: 0 for e in ENGS}
        self.dsem = {e: [es.enter_context(nc.semaphore(f"d_{e}{i}")) for i in range(n_dma_sems)]
                     for e in ("g", "y", "s")}
        self.dval = {e: [0] * n_dma_sems for e in ("g", "y", "s")}
        self.dcur = {e: 0 for e in ("g", "y", "s")}
        self.dopen = {e: False for e in ("g", "y", "s")}
        self.seen = {c: {} for c in ENGS}
        self._last_mm = None

    def _targets(self):
        t = {}
        for e in ENGS:
            t[("c", e)] = self.cval[e]
        for e in ("g", "y", "s"):
            for i, v in enumerate(self.dval[e]):
                if v:
                    t[("d", e, i)] = v
        return t

    def _wait_all(self, cons):
        eng = self.obj[cons]
        for key, tgt in self._targets().items():
            if key == ("c", cons):
                continue
            if self.seen[cons].get(key, 0) < tgt:
                sem = self.csem[key[1]] if key[0] == "c" else self.dsem[key[1]][key[2]]
                eng.wait_ge(sem, tgt)
                self.seen[cons][key] = tgt
        for e in ("g", "y", "s"):
            if self.dopen[e]:
                self.dopen[e] = False
                self.dcur[e] = (self.dcur[e] + 1) % len(self.dsem[e])

    def _self_wait(self, cons):
        if cons == "t":
            return
        tgt = self.cval[cons]
        if tgt and self.seen[cons].get(("self",), 0) < tgt:
            self.obj[cons].wait_ge(self.csem[cons], tgt)
            self.seen[cons][("self",)] = tgt

    def op(self, cons, name, *a, **kw):
        self._wait_all(cons)
        self._self_wait(cons)
        ins = getattr(self.obj[cons], name)(*a, **kw)
        self.cval[cons] += 1
        ins.then_inc(self.csem[cons], 1)
        return ins

    def mm(self, out, lhsT, rhs, start=True, stop=True):
        self._wait_all("t")
        ins = self.obj["t"].matmul(out, lhsT, rhs, start=start, stop=stop)
        self._last_mm = ins
        return ins

    def mm_fence(self):
        assert self._last_mm is not None
        self.cval["t"] += 1
        self._last_mm.then_inc(self.csem["t"], 1)
        self._last_mm = None

    def dma(self, issuer, out, in_, **kw):
        self._wait_all(issuer)
        ins = self.obj[issuer].dma_start(out, in_, **kw)
        i = self.dcur[issuer]
        self.dval[issuer][i] += 16
        ins.then_inc(self.dsem[issuer][i], 16)
        self.dopen[issuer] = True
        return ins

    def cc(self, kind, op, groups, ins_, outs_):
        self._wait_all("g")
        ins = self.obj["g"].collective_compute(kind, op, replica_groups=groups,
                                               ins=ins_, outs=outs_)
        self.cval["g"] += 1
        ins.then_inc(self.csem["g"], 1)
        return ins

    def finish(self):
        self._wait_all("y")



class Scratch:

    def __init__(self, nc, es):
        sb = lambda n, sh, dt: es.enter_context(nc.sbuf_tensor(n, sh, dt))
        psa = lambda n, sh, dt: es.enter_context(nc.psum_tensor(n, sh, dt))
        self.ones_f = sb("ones_f", [128, 1], F32)
        self.ones_fr = sb("ones_fr", [1, 128], F32)
        self.ones_bc = sb("ones_bc", [128, 1], BF16)
        self.ones_b = sb("ones_b", [1, 128], BF16)
        self.x2 = sb("x2", [128, 512], F32)
        self.t1 = sb("t1", [128, 512], F32)
        self.rrow = sb("rrow", [1, T], F32)
        self.mb = sb("mb", [1, T], BF16)
        self.rb = sb("rb", [1, T], BF16)
        self.srows = sb("srows", [1, 2 * T], F32)
        self.bc_ps = psa("bc_ps", [128, 1024], F32)
        self.stat_ps = self.bc_ps

    def init(self, q):
        q.op("g", "memset", self.ones_f[:, :], 1.0)
        q.op("g", "memset", self.ones_fr[:, :], 1.0)
        q.op("g", "memset", self.ones_bc[:, :], 1.0)
        q.op("g", "memset", self.ones_b[:, :], 1.0)


def emit_ln_stats(q, S, X):
    ones = S.ones_f if X.dtype == F32 else S.ones_bc
    for sl in range(4):
        for ch in range(NCH):
            o = ch * T + sl * 512
            q.op("s", "activation", S.x2[:, 0:512], X[:, o:o + 512], AF.Square)
            q.mm(S.stat_ps[0:1, 0:512], ones[:, 0:1], X[:, o:o + 512],
                 start=(ch == 0), stop=(ch == NCH - 1))
            q.mm(S.stat_ps[0:1, 512:1024], S.ones_f[:, 0:1], S.x2[:, 0:512],
                 start=(ch == 0), stop=(ch == NCH - 1))
            q.mm_fence()
        q.op("v", "tensor_copy", S.srows[0:1, sl * 512:(sl + 1) * 512], S.stat_ps[0:1, 0:512])
        q.op("v", "tensor_copy", S.srows[0:1, T + sl * 512:T + (sl + 1) * 512], S.stat_ps[0:1, 512:1024])
    mrow = S.srows[0:1, 0:T]
    vrow = S.srows[0:1, T:2 * T]
    q.op("v", "tensor_scalar", mrow, mrow, 1.0 / C, None, ALU.mult)
    q.op("v", "tensor_tensor", S.rrow[0:1, :], mrow, mrow, ALU.mult)
    q.op("v", "scalar_tensor_tensor", vrow, vrow, 1.0 / C, S.rrow[0:1, :],
         ALU.mult, ALU.subtract)
    q.op("v", "tensor_scalar", vrow, vrow, float(EPS), None, ALU.add)
    q.op("s", "activation", S.rrow[0:1, :], vrow, AF.Ln)
    q.op("s", "activation", S.rrow[0:1, :], S.rrow[0:1, :], AF.Exp, scale=-0.5)
    q.op("s", "copy", S.mb[0:1, :], mrow)
    q.op("s", "copy", S.rb[0:1, :], S.rrow[0:1, :])


def emit_ln_apply(q, S, X, xn):
    for sl in range(4):
        q.mm(S.bc_ps[:, 0:512], S.ones_b[0:1, :], S.mb[0:1, sl * 512:(sl + 1) * 512])
        q.mm(S.bc_ps[:, 512:1024], S.ones_b[0:1, :], S.rb[0:1, sl * 512:(sl + 1) * 512])
        q.mm_fence()
        for ch in range(NCH):
            o = ch * T + sl * 512
            q.op("v", "tensor_tensor", S.t1[:, 0:512], X[:, o:o + 512],
                 S.bc_ps[:, 0:512], ALU.subtract)
            q.op("v", "tensor_tensor", xn[:, o:o + 512], S.t1[:, 0:512],
                 S.bc_ps[:, 512:1024], ALU.mult)


def emit_postln_residual(q, S, Y, X):
    emit_ln_stats(q, S, Y)
    for sl in range(4):
        q.mm(S.bc_ps[:, 0:512], S.ones_b[0:1, :], S.mb[0:1, sl * 512:(sl + 1) * 512])
        q.mm(S.bc_ps[:, 512:1024], S.ones_b[0:1, :], S.rb[0:1, sl * 512:(sl + 1) * 512])
        q.mm_fence()
        for ch in range(NCH):
            o = ch * T + sl * 512
            q.op("v", "tensor_tensor", S.t1[:, 0:512], Y[:, o:o + 512],
                 S.bc_ps[:, 0:512], ALU.subtract)
            q.op("v", "tensor_tensor", S.t1[:, 0:512], S.t1[:, 0:512],
                 S.bc_ps[:, 512:1024], ALU.mult)
            q.op("v", "tensor_tensor", X[:, o:o + 512], X[:, o:o + 512],
                 S.t1[:, 0:512], ALU.add)


def emit_fm_matmul(q, xn, wq_dram, nj, nk, psum, out_cb, wbuf, kind="copy",
                   t_total=T, rhs_off=0):
    nts = t_total // 512
    psl = psum if isinstance(psum, (list, tuple)) else [psum]
    wbl = wbuf if isinstance(wbuf, (list, tuple)) else [wbuf]
    for j in range(nj):
        wb = wbl[j % len(wbl)]
        q.dma("g", wb[:, 0:nk * 128], wq_dram[j])
        for sl in range(nts):
            ps = psl[(j * nts + sl) % len(psl)]
            for k in range(nk):
                q.mm(ps[:, 0:512], wb[:, k * 128:(k + 1) * 128],
                     xn[:, k * t_total + rhs_off + sl * 512: k * t_total + rhs_off + sl * 512 + 512],
                     start=(k == 0), stop=(k == nk - 1))
            q.mm_fence()
            out_cb(j, sl, ps)



_L1_CACHE = {}


def build_l1():
    nc = bass.Bass("TRN2", target_bir_lowering=False)
    Xd = nc.dram_tensor("X", [128, NCH * T], F32, kind="ExternalInput").ap()
    Wd = nc.dram_tensor("W", [24, 128, NCH * 128], F32, kind="ExternalInput").ap()
    wc1d = nc.dram_tensor("wc1", [128, NCH], F32, kind="ExternalInput").ap()
    kout = nc.dram_tensor("kbf", [128, NCH * T], BF16, kind="ExternalOutput").ap()
    vout = nc.dram_tensor("vbf", [128, NCH * T], BF16, kind="ExternalOutput").ap()
    aout = nc.dram_tensor("agent", [1, T], F32, kind="ExternalOutput").ap()
    zout = nc.dram_tensor("zloc", [128, NCH], F32, kind="ExternalOutput").ap()

    with ExitStack() as es:
        sb = lambda n, sh, dt: es.enter_context(nc.sbuf_tensor(n, sh, dt))
        psa = lambda n, sh, dt: es.enter_context(nc.psum_tensor(n, sh, dt))
        S = Scratch(nc, es)
        q = SQ(nc, es)
        X = sb("Xs", [128, NCH * T], F32)
        xn = sb("xn", [128, NCH * T], BF16)
        kb = sb("kb", [128, T], BF16)
        vb = sb("vb", [128, T], BF16)
        qb = sb("qb", [128, T], BF16)
        wbuf = sb("wbuf", [128, NCH * 128], BF16)
        wc1b = sb("wc1b", [128, NCH], BF16)
        arow = sb("arow", [1, T], F32)
        abrow = sb("abrow", [1, T], BF16)
        abc = sb("abc", [128, T], BF16)
        prod = qb
        zcol = sb("zcol", [128, NCH], F32)
        mm_ps = psa("mm_ps", [128, 512], F32)
        ag_ps = psa("ag_ps", [1, 512], F32)
        ab_ps = psa("ab_ps", [128, 512], F32)

        q.dma("y", X[:, :], Xd[:, :])
        q.dma("g", wc1b[:, :], wc1d[:, :])
        S.init(q)
        q.op("g", "memset", arow[0:1, :], 0.0)
        emit_ln_stats(q, S, X)
        emit_ln_apply(q, S, X, xn)

        state = {"abc_done": False}

        def make_abc():
            q.op("s", "activation", abrow[0:1, :], arow[0:1, :], AF.Copy, scale=float(C ** -0.5))
            for sl in range(4):
                q.mm(ab_ps[:, 0:512], S.ones_b[0:1, :], abrow[0:1, sl * 512:(sl + 1) * 512])
                q.mm_fence()
                q.op("s", "copy", abc[:, sl * 512:(sl + 1) * 512], ab_ps[:, 0:512])

        def out_cb(j, sl, ps):
            if j < 8:
                q.op("s", "copy", qb[:, sl * 512:(sl + 1) * 512], ps[:, 0:512])
                q.mm(ag_ps[0:1, 0:512], wc1b[:, j:j + 1],
                     qb[:, sl * 512:(sl + 1) * 512], start=True, stop=True)
                q.mm_fence()
                q.op("v", "tensor_tensor", arow[0:1, sl * 512:(sl + 1) * 512],
                     arow[0:1, sl * 512:(sl + 1) * 512], ag_ps[0:1, 0:512], ALU.add)
            elif j < 16:
                if not state["abc_done"]:
                    state["abc_done"] = True
                    make_abc()
                ch = j - 8
                q.op("s", "copy", kb[:, sl * 512:(sl + 1) * 512], ps[:, 0:512])
                if sl == 3:
                    q.op("v", "tensor_tensor", prod[:, :], kb[:, :], abc[:, :], ALU.mult)
                    q.op("v", "reduce_sum", zcol[:, ch:ch + 1], prod[:, :], AX)
                    q.dma("y", kout[:, ch * T:(ch + 1) * T], kb[:, :])
            else:
                ch = j - 16
                q.op("s", "copy", vb[:, sl * 512:(sl + 1) * 512], ps[:, 0:512])
                if sl == 3:
                    q.dma("y", vout[:, ch * T:(ch + 1) * T], vb[:, :])

        emit_fm_matmul(q, xn, Wd, 24, NCH, mm_ps, out_cb, wbuf)
        q.dma("y", aout[0:1, :], arow[0:1, :])
        q.dma("y", zout[:, :], zcol[:, :])
        q.finish()
    return nc


def build_l2():
    nc = bass.Bass("TRN2", target_bir_lowering=False)
    zb = nc.dram_tensor("z_both", [2, 128, NCH], F32, kind="ExternalInput").ap()
    vd = nc.dram_tensor("vbf", [128, NCH * T], BF16, kind="ExternalInput").ap()
    mzout = nc.dram_tensor("mz", [1, 2], F32, kind="ExternalOutput").ap()
    wvout = nc.dram_tensor("wv", [128, NCH], F32, kind="ExternalOutput").ap()

    with ExitStack() as es:
        sb = lambda n, sh, dt: es.enter_context(nc.sbuf_tensor(n, sh, dt))
        psa = lambda n, sh, dt: es.enter_context(nc.psum_tensor(n, sh, dt))
        S = Scratch(nc, es)
        q = SQ(nc, es)
        vb = sb("vb", [128, NCH * T], BF16)
        z2 = sb("z2", [2, 128 * NCH], F32)
        zc = sb("zc", [128, NCH], F32)
        zrow = sb("zrow", [1, C], F32)
        acr = sb("acr", [1, C], F32)
        acb = sb("acb", [128, NCH], BF16)
        srow = sb("srow", [1, T], F32)
        erow = sb("erow", [1, T], F32)
        ebrow = sb("ebrow", [1, T], BF16)
        ebc = sb("ebc", [128, T], BF16)
        prod = sb("prod", [128, T], BF16)
        wvc = sb("wvc", [128, NCH], F32)
        red = sb("red", [1, 4], F32)
        mzrow = sb("mz2", [1, 2], F32)
        s_ps = psa("s_ps", [1, 512], F32)
        b_ps = psa("b_ps", [128, 512], F32)
        zdram = nc.dram_tensor("zrow_d", [128, NCH], F32)
        acd = nc.dram_tensor("acd", [128, NCH], F32)

        q.dma("y", vb[:, :], vd[:, :])
        q.dma("y", zc[:, :], zb[0])
        z1 = sb("z1", [128, NCH], F32)
        q.dma("y", z1[:, :], zb[1])
        S.init(q)
        q.op("v", "tensor_tensor", zc[:, :], zc[:, :], z1[:, :], ALU.add)
        q.dma("y", zdram[:, :], zc[:, :])
        q.dma("y", zrow[0:1, :], zdram.rearrange("p k -> () (p k)"))
        q.op("v", "reduce_max", red[0:1, 0:1], zrow[0:1, :], AX)
        q.op("v", "tensor_scalar", acr[0:1, :], zrow[0:1, :], red[0:1, 0:1], None, ALU.subtract)
        q.op("s", "activation", acr[0:1, :], acr[0:1, :], AF.Exp)
        q.op("v", "reduce_sum", red[0:1, 1:2], acr[0:1, :], AX)
        q.op("v", "reciprocal", red[0:1, 2:3], red[0:1, 1:2])
        q.op("v", "tensor_scalar", acr[0:1, :], acr[0:1, :], red[0:1, 2:3], None, ALU.mult)
        q.dma("y", acd.rearrange("p k -> () (p k)"), acr[0:1, :])
        q.dma("g", acb[:, :], acd[:, :])
        for sl in range(4):
            for ch in range(NCH):
                q.mm(s_ps[0:1, 0:512], acb[:, ch:ch + 1],
                     vb[:, ch * T + sl * 512:ch * T + sl * 512 + 512],
                     start=(ch == 0), stop=(ch == NCH - 1))
            q.mm_fence()
            q.op("s", "copy", srow[0:1, sl * 512:(sl + 1) * 512], s_ps[0:1, 0:512])
        q.op("v", "reduce_max", red[0:1, 0:1], srow[0:1, :], AX)
        q.op("v", "tensor_scalar", erow[0:1, :], srow[0:1, :], red[0:1, 0:1], None, ALU.subtract)
        q.op("s", "activation", erow[0:1, :], erow[0:1, :], AF.Exp)
        q.op("v", "reduce_sum", red[0:1, 1:2], erow[0:1, :], AX)
        q.op("v", "tensor_copy", mzrow[0:1, 0:1], red[0:1, 0:1])
        q.op("v", "tensor_copy", mzrow[0:1, 1:2], red[0:1, 1:2])
        q.op("s", "copy", ebrow[0:1, :], erow[0:1, :])
        for sl in range(4):
            q.mm(b_ps[:, 0:512], S.ones_b[0:1, :], ebrow[0:1, sl * 512:(sl + 1) * 512])
            q.mm_fence()
            q.op("s", "copy", ebc[:, sl * 512:(sl + 1) * 512], b_ps[:, 0:512])
        for ch in range(NCH):
            q.op("v", "tensor_tensor", prod[:, :], vb[:, ch * T:(ch + 1) * T], ebc[:, :], ALU.mult)
            q.op("v", "reduce_sum", wvc[:, ch:ch + 1], prod[:, :], AX)
        q.dma("y", mzout[0:1, :], mzrow[0:1, :])
        q.dma("y", wvout[:, :], wvc[:, :])
        q.finish()
    return nc






def emit_colbcast(q, S, ps, row_bf, ch, width=512):
    q.mm(ps[:, 0:width], row_bf[0:1, ch * 128:(ch + 1) * 128], S.ones_brow[0:1, 0:width])
    q.mm_fence()


def build_l3():
    nc = bass.Bass("TRN2", target_bir_lowering=False)
    Xd = nc.dram_tensor("X", [128, NCH * T], F32, kind="ExternalInput").ap()
    std = nc.dram_tensor("stats", [1, 4], F32, kind="ExternalInput").ap()
    wvd = nc.dram_tensor("wv2", [2, 1, C], F32, kind="ExternalInput").ap()
    wgd = nc.dram_tensor("wg", [24, 128, NCH * 128], I32, kind="ExternalInput").ap()
    wud = nc.dram_tensor("wu", [24, 128, NCH * 128], I32, kind="ExternalInput").ap()
    wdd = nc.dram_tensor("wd", [24, 128, NCH * 128], I32, kind="ExternalInput").ap()
    Wud = nc.dram_tensor("Wu", [NCH, 128, C], F32, kind="ExternalInput").ap()
    Wvd = nc.dram_tensor("Wv", [NCH, 128, C], F32, kind="ExternalInput").ap()
    bud = nc.dram_tensor("bu", [1, C], F32, kind="ExternalInput").ap()
    bvd = nc.dram_tensor("bv", [1, C], F32, kind="ExternalInput").ap()
    X2out = nc.dram_tensor("X2", [128, NCH * T], F32, kind="ExternalOutput").ap()
    uout = nc.dram_tensor("u_tm", [T, C], BF16, kind="ExternalOutput").ap()
    vout = nc.dram_tensor("v_tm", [T, C], BF16, kind="ExternalOutput").ap()

    with ExitStack() as es:
        sb = lambda n, sh, dt: es.enter_context(nc.sbuf_tensor(n, sh, dt))
        psa = lambda n, sh, dt: es.enter_context(nc.psum_tensor(n, sh, dt))
        S = Scratch(nc, es)
        q = SQ(nc, es)
        S.ones_brow = sb("ones_brow", [1, 512], BF16)
        X = sb("Xs", [128, NCH * T], F32)
        xn = sb("xn", [128, NCH * T], BF16)
        attn_c = ExitStack()
        sb2 = lambda n, sh, dt: attn_c.enter_context(nc.sbuf_tensor(n, sh, dt))
        rows = sb2("rows", [1, 3 * C], F32)
        st4 = sb2("st4", [1, 16], F32)
        lnvb = sb2("lnvb", [1, C], BF16)
        bc_ps2 = psa("bc_ps2", [128, 512], F32)

        q.dma("y", X[:, :], Xd[:, :])
        q.dma("y", rows[0:1, 0:C], wvd[0])
        q.dma("y", rows[0:1, C:2 * C], wvd[1])
        q.dma("y", st4[0:1, 0:4], std[0:1, :])
        S.init(q)
        q.op("g", "memset", S.ones_brow[:, :], 1.0)

        q.op("v", "tensor_tensor", st4[0:1, 4:5], st4[0:1, 0:1], st4[0:1, 2:3], ALU.max)
        q.op("v", "tensor_tensor", st4[0:1, 5:6], st4[0:1, 0:1], st4[0:1, 4:5], ALU.subtract)
        q.op("v", "tensor_tensor", st4[0:1, 6:7], st4[0:1, 2:3], st4[0:1, 4:5], ALU.subtract)
        q.op("s", "activation", st4[0:1, 5:7], st4[0:1, 5:7], AF.Exp)
        q.op("v", "tensor_tensor", st4[0:1, 7:8], st4[0:1, 5:6], st4[0:1, 1:2], ALU.mult)
        q.op("v", "scalar_tensor_tensor", st4[0:1, 7:8], st4[0:1, 3:4], st4[0:1, 6:7],
             st4[0:1, 7:8], ALU.mult, ALU.add)
        q.op("v", "reciprocal", st4[0:1, 8:9], st4[0:1, 7:8])
        q.op("v", "tensor_tensor", st4[0:1, 9:10], st4[0:1, 5:6], st4[0:1, 8:9], ALU.mult)
        q.op("v", "tensor_tensor", st4[0:1, 10:11], st4[0:1, 6:7], st4[0:1, 8:9], ALU.mult)
        q.op("v", "tensor_scalar", rows[0:1, 2 * C:3 * C], rows[0:1, 0:C], st4[0:1, 9:10],
             None, ALU.mult)
        q.op("v", "scalar_tensor_tensor", rows[0:1, 2 * C:3 * C], rows[0:1, C:2 * C],
             st4[0:1, 10:11], rows[0:1, 2 * C:3 * C], ALU.mult, ALU.add)
        av = rows[0:1, 2 * C:3 * C]
        q.op("v", "reduce_sum", st4[0:1, 11:12], av, AX)
        q.op("v", "tensor_scalar", st4[0:1, 11:12], st4[0:1, 11:12], 1.0 / C, None, ALU.mult)
        q.op("v", "tensor_scalar", av, av, st4[0:1, 11:12], None, ALU.subtract)
        q.op("s", "activation", rows[0:1, 0:C], av, AF.Square)
        q.op("v", "reduce_sum", st4[0:1, 12:13], rows[0:1, 0:C], AX)
        q.op("v", "tensor_scalar", st4[0:1, 12:13], st4[0:1, 12:13], 1.0 / C, None, ALU.mult)
        q.op("v", "tensor_scalar", st4[0:1, 12:13], st4[0:1, 12:13], float(EPS), None, ALU.add)
        q.op("s", "activation", st4[0:1, 12:13], st4[0:1, 12:13], AF.Ln)
        q.op("s", "activation", st4[0:1, 12:13], st4[0:1, 12:13], AF.Exp, scale=-0.5)
        q.op("v", "tensor_scalar", av, av, st4[0:1, 12:13], None, ALU.mult)
        q.op("s", "copy", lnvb[0:1, :], av)
        for ch in range(NCH):
            emit_colbcast(q, S, bc_ps2, lnvb, ch)
            for sl in range(4):
                o = ch * T + sl * 512
                q.op("v", "tensor_tensor", X[:, o:o + 512], X[:, o:o + 512],
                     bc_ps2[:, 0:512], ALU.add)

        attn_c.close()
        emit_ln_stats(q, S, X)
        emit_ln_apply(q, S, X, xn)
        with nc.sbuf_tensor("yacc", [128, NCH * T], F32) as yacc, \
             nc.sbuf_tensor("gch", [128, T], BF16) as gch, \
             nc.sbuf_tensor("wbuf3", [128, NCH * 128], BF16) as wbuf3, \
             nc.psum_tensor("ff_ps", [128, 512], F32) as ff_ps, \
             nc.psum_tensor("dn_ps", [128, 512], F32) as dn_ps:
            hj = gch
            q.op("g", "memset", yacc[:, :], 0.0)
            for j in range(24):
                q.dma("g", wbuf3[:, 0:NCH * 128], wgd[j])
                for sl in range(4):
                    for k in range(NCH):
                        q.mm(ff_ps[:, 0:512], wbuf3[:, k * 128:(k + 1) * 128],
                             xn[:, k * T + sl * 512:k * T + sl * 512 + 512],
                             start=(k == 0), stop=(k == NCH - 1))
                    q.mm_fence()
                    q.op("s", "activation", gch[:, sl * 512:(sl + 1) * 512], ff_ps[:, 0:512],
                         AF.Silu, scale=1.0 / 127.0)
                q.dma("g", wbuf3[:, 0:NCH * 128], wud[j])
                for sl in range(4):
                    for k in range(NCH):
                        q.mm(ff_ps[:, 0:512], wbuf3[:, k * 128:(k + 1) * 128],
                             xn[:, k * T + sl * 512:k * T + sl * 512 + 512],
                             start=(k == 0), stop=(k == NCH - 1))
                    q.mm_fence()
                    q.op("v", "tensor_tensor", hj[:, sl * 512:(sl + 1) * 512],
                         gch[:, sl * 512:(sl + 1) * 512], ff_ps[:, 0:512], ALU.mult)
                q.dma("g", wbuf3[:, 0:NCH * 128], wdd[j])
                for co in range(NCH):
                    for sl in range(4):
                        q.mm(dn_ps[:, 0:512], wbuf3[:, co * 128:(co + 1) * 128],
                             hj[:, sl * 512:(sl + 1) * 512], start=True, stop=True)
                        q.mm_fence()
                        o = co * T + sl * 512
                        q.op("v", "tensor_tensor", yacc[:, o:o + 512], yacc[:, o:o + 512],
                             dn_ps[:, 0:512], ALU.add)
            for ch in range(NCH):
                q.op("s", "activation", yacc[:, ch * T:(ch + 1) * T], yacc[:, ch * T:(ch + 1) * T],
                     AF.Copy, scale=1.0 / (127.0 * 127.0))
            emit_ln_stats(q, S, yacc)
            for sl in range(4):
                q.mm(S.bc_ps[:, 0:512], S.ones_b[0:1, :], S.mb[0:1, sl * 512:(sl + 1) * 512])
                q.mm(S.bc_ps[:, 512:1024], S.ones_b[0:1, :], S.rb[0:1, sl * 512:(sl + 1) * 512])
                q.mm_fence()
                for ch in range(NCH):
                    o = ch * T + sl * 512
                    q.op("v", "tensor_tensor", S.t1[:, 0:512], yacc[:, o:o + 512],
                         S.bc_ps[:, 0:512], ALU.subtract)
                    q.op("v", "tensor_tensor", S.t1[:, 0:512], S.t1[:, 0:512],
                         S.bc_ps[:, 512:1024], ALU.mult)
                    q.op("v", "tensor_tensor", X[:, o:o + 512], X[:, o:o + 512],
                         S.t1[:, 0:512], ALU.add)
        q.dma("y", X2out[:, :], X[:, :])

        emit_ln_stats(q, S, X)
        emit_ln_apply(q, S, X, xn)
        with nc.sbuf_tensor("wup", [128, NCH * C], BF16) as wup, \
             nc.sbuf_tensor("wvp", [128, NCH * C], BF16) as wvp, \
             nc.sbuf_tensor("brow", [1, 2 * C], BF16) as brow, \
             nc.sbuf_tensor("uvt", [128, C], BF16) as uvt, \
             nc.psum_tensor("uv_ps", [128, 512], F32) as uv_ps:
            for k in range(NCH):
                q.dma("g", wup[:, k * C:(k + 1) * C], Wud[k])
                q.dma("g", wvp[:, k * C:(k + 1) * C], Wvd[k])
            q.dma("g", brow[0:1, 0:C], bud[0:1, :])
            q.dma("g", brow[0:1, C:2 * C], bvd[0:1, :])
            for which, wpl, bofs, outd in ((0, wup, 0, uout), (1, wvp, C, vout)):
                for tc in range(16):
                    for half in range(2):
                        for k in range(NCH):
                            q.mm(uv_ps[:, 0:512],
                                 xn[:, k * T + tc * 128:k * T + (tc + 1) * 128],
                                 wpl[:, k * C + half * 512:k * C + half * 512 + 512],
                                 start=(k == 0), stop=False)
                        q.mm(uv_ps[:, 0:512], S.ones_b[0:1, 0:128],
                             brow[0:1, bofs + half * 512:bofs + half * 512 + 512],
                             start=False, stop=True)
                        q.mm_fence()
                        q.op("s", "copy", uvt[:, half * 512:half * 512 + 512], uv_ps[:, 0:512])
                    q.dma("y", outd[tc * 128:(tc + 1) * 128, :], uvt[:, :])
        q.finish()
    return nc


def build_l4():
    fc = _fft_consts()
    nc = bass.Bass("TRN2", target_bir_lowering=False)
    vsl = nc.dram_tensor("vsl", [4096, 512], BF16, kind="ExternalInput").ap()
    W1d = nc.dram_tensor("W1", [64, 64], F32, kind="ExternalInput").ap()
    b1d = nc.dram_tensor("b1c", [64, 1], F32, kind="ExternalInput").ap()
    W2d = nc.dram_tensor("W2", [64, 64], F32, kind="ExternalInput").ap()
    b2d = nc.dram_tensor("b2c", [64, 1], F32, kind="ExternalInput").ap()
    W3d = nc.dram_tensor("W3o", [64, 512], F32, kind="ExternalInput").ap()
    b3d = nc.dram_tensor("b3r", [1, 512], F32, kind="ExternalInput").ap()
    peTd = nc.dram_tensor("peT", [64, 4096], BF16, kind="ExternalInput").ap()
    dkd = nc.dram_tensor("dkc", [128, 32], F32, kind="ExternalInput").ap()
    cre_d = nc.dram_tensor("c_re", [64, 64 * K1H], BF16, kind="ExternalInput").ap()
    cim_d = nc.dram_tensor("c_im", [64, 64 * K1H], BF16, kind="ExternalInput").ap()
    le1_d = nc.dram_tensor("LE1", [64, 128], BF16, kind="ExternalInput").ap()
    le2_d = nc.dram_tensor("LE2", [64, 128], BF16, kind="ExternalInput").ap()
    lars_d = nc.dram_tensor("La_rS", [128, 64], BF16, kind="ExternalInput").ap()
    lars2_d = nc.dram_tensor("La_rS2", [128, 64], BF16, kind="ExternalInput").ap()
    lais_d = nc.dram_tensor("La_iS", [128, 64], BF16, kind="ExternalInput").ap()
    lais2_d = nc.dram_tensor("La_iS2", [128, 64], BF16, kind="ExternalInput").ap()
    mbr_d = nc.dram_tensor("Mb_r", [K1H, 64 * 64], BF16, kind="ExternalInput").ap()
    mbmi_d = nc.dram_tensor("Mb_mi", [K1H, 64 * 64], BF16, kind="ExternalInput").ap()
    yout = nc.dram_tensor("y_conv", [4096, 512], BF16, kind="ExternalOutput").ap()

    h_d = nc.dram_tensor("h_d", [4096, 512], BF16)
    Av_d = nc.dram_tensor("Av_d", [2, 64, K1H, 512], BF16)
    Ah_d = nc.dram_tensor("Ah_d", [2, 64, K1H, 512], BF16)
    Hp_d = nc.dram_tensor("Hp_d", [K1H, 128, 512], BF16)
    Hs_d = nc.dram_tensor("Hs_d", [K1H, 128, 512], BF16)
    B_d = nc.dram_tensor("B_d", [2, K1H, 64, 512], BF16)

    with ExitStack() as es:
        sb = lambda n, sh, dt: es.enter_context(nc.sbuf_tensor(n, sh, dt))
        psa = lambda n, sh, dt: es.enter_context(nc.psum_tensor(n, sh, dt))
        q = SQ(nc, es)
        ones_b = sb("ones_b", [1, 512], BF16)
        q.op("g", "memset", ones_b[:, :], 1.0)

        with nc.sbuf_tensor("pe", [64, 4096], BF16) as pe, \
             nc.sbuf_tensor("w12", [64, 128], BF16) as w12, \
             nc.sbuf_tensor("b12", [64, 2], F32) as b12, \
             nc.sbuf_tensor("w3", [64, 512], BF16) as w3, \
             nc.sbuf_tensor("b3rs", [1, 512], BF16) as b3r, \
             nc.sbuf_tensor("cur1", [64, 4096], BF16) as cur1, \
             nc.sbuf_tensor("cur2", [64, 4096], BF16) as cur2, \
             nc.sbuf_tensor("dk", [128, 32], F32) as dk, \
             nc.sbuf_tensor("htile", [128, 512], BF16) as htile, \
             nc.psum_tensor("f_ps", [64, 512], F32) as f_ps, \
             nc.psum_tensor("f2_ps", [128, 512], F32) as f2_ps:
            q.dma("y", pe[:, :], peTd[:, :])
            q.dma("g", w12[:, 0:64], W1d[:, :])
            q.dma("g", w12[:, 64:128], W2d[:, :])
            q.dma("y", b12[:, 0:1], b1d[:, :])
            q.dma("y", b12[:, 1:2], b2d[:, :])
            q.dma("g", w3[:, :], W3d[:, :])
            q.dma("g", b3r[0:1, :], b3d[0:1, :])
            q.dma("y", dk[:, :], dkd[:, :])
            for stage, (src, dst, bcol) in enumerate(((pe, cur1, 0), (cur1, cur2, 1))):
                for sl in range(8):
                    q.mm(f_ps[:, 0:512], w12[:, stage * 64:stage * 64 + 64],
                         src[:, sl * 512:(sl + 1) * 512], start=True, stop=True)
                    q.mm_fence()
                    q.op("s", "activation", dst[:, sl * 512:(sl + 1) * 512], f_ps[:, 0:512],
                         AF.Gelu_apprx_tanh, bias=b12[:, bcol:bcol + 1])
            for g in range(32):
                q.mm(f2_ps[:, 0:512], cur2[:, g * 128:(g + 1) * 128], w3[:, :],
                     start=True, stop=False)
                q.mm(f2_ps[:, 0:512], ones_b[0:1, 0:128], b3r[0:1, :],
                     start=False, stop=True)
                q.mm_fence()
                q.op("s", "activation", htile[:, :], f2_ps[:, 0:512], AF.Copy,
                     scale=dk[:, g:g + 1])
                q.dma("y", h_d[g * 128:(g + 1) * 128, :], htile[:, :])

        with nc.sbuf_tensor("grid", [64, 64 * 512], BF16) as grid, \
             nc.sbuf_tensor("gtmp", [64, 8192], BF16) as gtmp, \
             nc.sbuf_tensor("selsb", [128, 2], F32) as selc, \
             nc.sbuf_tensor("cre", [64, 64 * K1H], BF16) as cre, \
             nc.sbuf_tensor("cim", [64, 64 * K1H], BF16) as cim, \
             nc.sbuf_tensor("abuf", [K1H, 8 * 2 * 512], BF16) as abuf, \
             nc.psum_tensor("s1_ps", [K1H, 1024], F32) as s1_ps:
            q.dma("y", selc[:, :], seld[:, :])
            q.dma("y", cre[:, :], cre_d[:, :])
            q.dma("y", cim[:, :], cim_d[:, :])
            for src, dstA in ((vex_out, Av_d), (h_d, Ah_d)):
                if src is vex_out:
                    for s2b8 in range(8):
                        rows8 = src.rearrange("(s1 s2) c -> s1 s2 c", s2=64)[:, s2b8 * 8:(s2b8 + 1) * 8, :]
                        q.dma("y", gtmp[:, 0:4096].rearrange("p (s c) -> p s c", s=8),
                              rows8[:, :, 0:512])
                        q.dma("y", gtmp[:, 4096:8192].rearrange("p (s c) -> p s c", s=8),
                              rows8[:, :, 512:1024])
                        q.op("v", "tensor_scalar", gtmp[:, 0:4096], gtmp[:, 0:4096],
                             selc[0:64, 0:1], None, ALU.mult)
                        q.op("v", "scalar_tensor_tensor", grid[:, s2b8 * 4096:(s2b8 + 1) * 4096],
                             gtmp[:, 4096:8192], selc[0:64, 1:2], gtmp[:, 0:4096],
                             ALU.mult, ALU.add)
                else:
                    q.dma("y", grid[:, :].rearrange("p (s c) -> p s c", s=64),
                          src.rearrange("(s1 s2) c -> s1 s2 c", s2=64))
                for s2b in range(8):
                    for i in range(8):
                        s2 = s2b * 8 + i
                        q.mm(s1_ps[:, 0:512], cre[:, s2 * K1H:(s2 + 1) * K1H],
                             grid[:, s2 * 512:(s2 + 1) * 512], start=True, stop=True)
                        q.mm_fence()
                        q.op("s", "copy", abuf[:, (i * 2) * 512:(i * 2) * 512 + 512],
                             s1_ps[:, 0:512])
                        q.mm(s1_ps[:, 512:1024], cim[:, s2 * K1H:(s2 + 1) * K1H],
                             grid[:, s2 * 512:(s2 + 1) * 512], start=True, stop=True)
                        q.mm_fence()
                        q.op("s", "copy", abuf[:, (i * 2 + 1) * 512:(i * 2 + 1) * 512 + 512],
                             s1_ps[:, 512:1024])
                    av = abuf[:, :].rearrange("k (s comp c) -> k s comp c", s=8, comp=2)
                    for comp in range(2):
                        q.dma("y", dstA[comp, s2b * 8:(s2b + 1) * 8, :, :].rearrange("s k c -> k s c"),
                              av[:, :, comp, :])

        with nc.sbuf_tensor("ar", [64, 512], BF16) as ar, \
             nc.sbuf_tensor("ai", [64, 512], BF16) as ai, \
             nc.sbuf_tensor("le", [64, 256], BF16) as le, \
             nc.sbuf_tensor("hp", [128, 512], BF16) as hp, \
             nc.psum_tensor("x_ps", [128, 512], F32) as x_ps:
            q.dma("y", le[:, 0:128], le1_d[:, :])
            q.dma("y", le[:, 128:256], le2_d[:, :])
            for k1 in range(K1H):
                q.dma("y", ar[:, :], Ah_d[0, :, k1, :])
                q.dma("y", ai[:, :], Ah_d[1, :, k1, :])
                q.mm(x_ps[:, 0:512], le[:, 0:128], ar[:, :], start=True, stop=False)
                q.mm(x_ps[:, 0:512], le[:, 128:256], ai[:, :], start=False, stop=True)
                q.mm_fence()
                q.op("s", "copy", hp[:, :], x_ps[:, 0:512])
                q.dma("y", Hp_d[k1, :, :], hp[:, :])
                q.dma("y", Hs_d[k1, 64:128, :], hp[0:64, :])
                q.dma("y", Hs_d[k1, 0:64, :], hp[64:128, :])

        with nc.sbuf_tensor("ar2", [64, 512], BF16) as ar, \
             nc.sbuf_tensor("ai2", [64, 512], BF16) as ai, \
             nc.sbuf_tensor("le2", [64, 256], BF16) as le, \
             nc.sbuf_tensor("la", [128, 256], BF16) as la, \
             nc.sbuf_tensor("hps", [128, 1024], BF16) as hps, \
             nc.sbuf_tensor("ss", [128, 1024], BF16) as ss, \
             nc.sbuf_tensor("bb", [128, 512], BF16) as bb, \
             nc.psum_tensor("x2_ps", [128, 512], F32) as x_ps, \
             nc.psum_tensor("b_ps", [128, 512], F32) as b_ps:
            q.dma("y", le[:, 0:128], le1_d[:, :])
            q.dma("y", le[:, 128:256], le2_d[:, :])
            q.dma("y", la[:, 0:64], lars_d[:, :])
            q.dma("y", la[:, 64:128], lars2_d[:, :])
            q.dma("y", la[:, 128:192], lais_d[:, :])
            q.dma("y", la[:, 192:256], lais2_d[:, :])
            for k1 in range(K1H):
                q.dma("y", ar[:, :], Av_d[0, :, k1, :])
                q.dma("y", ai[:, :], Av_d[1, :, k1, :])
                q.dma("y", hps[:, 0:512], Hp_d[k1, :, :])
                q.dma("y", hps[:, 512:1024], Hs_d[k1, :, :])
                q.mm(x_ps[:, 0:512], le[:, 0:128], ar[:, :], start=True, stop=False)
                q.mm(x_ps[:, 0:512], le[:, 128:256], ai[:, :], start=False, stop=True)
                q.mm_fence()
                q.op("v", "tensor_tensor", ss[:, 0:512], x_ps[:, 0:512], hps[:, 0:512], ALU.mult)
                q.op("v", "tensor_tensor", ss[:, 512:1024], x_ps[:, 0:512], hps[:, 512:1024], ALU.mult)
                q.mm(b_ps[0:64, 0:512], la[:, 0:64], ss[:, 0:512], start=True, stop=False)
                q.mm(b_ps[0:64, 0:512], la[:, 64:128], ss[:, 512:1024], start=False, stop=True)
                q.mm(b_ps[64:128, 0:512], la[:, 128:192], ss[:, 0:512], start=True, stop=False)
                q.mm(b_ps[64:128, 0:512], la[:, 192:256], ss[:, 512:1024], start=False, stop=True)
                q.mm_fence()
                q.op("s", "copy", bb[:, :], b_ps[:, 0:512])
                q.dma("y", B_d[0, k1, :, :], bb[0:64, :])
                q.dma("y", B_d[1, k1, :, :], bb[64:128, :])

        with nc.sbuf_tensor("mbr", [K1H, 64 * 64], BF16) as mbr, \
             nc.sbuf_tensor("mbmi", [K1H, 64 * 64], BF16) as mbmi, \
             nc.sbuf_tensor("bgr", [K1H, 1024], BF16) as bgr, \
             nc.sbuf_tensor("yt", [64, 512], BF16) as yt, \
             nc.psum_tensor("y_ps", [64, 512], F32) as y_ps:
            q.dma("y", mbr[:, :], mbr_d[:, :])
            q.dma("y", mbmi[:, :], mbmi_d[:, :])
            for t2 in range(64):
                q.dma("y", bgr[:, 0:512], B_d[0, :, t2, :])
                q.dma("y", bgr[:, 512:1024], B_d[1, :, t2, :])
                q.mm(y_ps[:, 0:512], mbr[:, t2 * 64:(t2 + 1) * 64], bgr[:, 0:512],
                     start=True, stop=False)
                q.mm(y_ps[:, 0:512], mbmi[:, t2 * 64:(t2 + 1) * 64], bgr[:, 512:1024],
                     start=False, stop=True)
                q.mm_fence()
                q.op("s", "copy", yt[:, :], y_ps[:, 0:512])
                q.dma("y", yout.rearrange("(t1 t2) c -> t2 t1 c", t2=64)[t2], yt[:, :])
        q.finish()
    return nc


def build_l5():
    nc = bass.Bass("TRN2", target_bir_lowering=False)
    Xd = nc.dram_tensor("X2", [128, NCH * T], F32, kind="ExternalInput").ap()
    ud = nc.dram_tensor("u_tm", [T, C], BF16, kind="ExternalInput").ap()
    ycd = nc.dram_tensor("yc_tm", [T, C], BF16, kind="ExternalInput").ap()
    wod = nc.dram_tensor("Wo_l", [NCH, 128, NCH * 128], F32, kind="ExternalInput").ap()
    bod = nc.dram_tensor("bo_c", [128, NCH], F32, kind="ExternalInput").ap()
    identd = nc.dram_tensor("identin", [128, 128], BF16, kind="ExternalInput").ap()
    seld = nc.dram_tensor("selc", [128, 2], F32, kind="ExternalInput").ap()
    l1gd = nc.dram_tensor("ln1g_c", [128, NCH], F32, kind="ExternalInput").ap()
    l1bd = nc.dram_tensor("ln1b_c", [128, NCH], F32, kind="ExternalInput").ap()
    w1d = nc.dram_tensor("w1T", [NCH, 128, 4096], F32, kind="ExternalInput").ap()
    b1rd = nc.dram_tensor("b1r", [1, 4096], F32, kind="ExternalInput").ap()
    l2gd = nc.dram_tensor("ln2g_r", [1, 4096], F32, kind="ExternalInput").ap()
    l2bd = nc.dram_tensor("ln2b_r", [1, 4096], F32, kind="ExternalInput").ap()
    w2d = nc.dram_tensor("w2T", [32, 128, C], F32, kind="ExternalInput").ap()
    b2d = nc.dram_tensor("b2_c", [128, NCH], F32, kind="ExternalInput").ap()
    X4out = nc.dram_tensor("X4", [128, NCH * T], F32, kind="ExternalOutput").ap()

    Xsp_d = nc.dram_tensor("Xsp_d", [128, NCH * T], F32)
    w1q_d = nc.dram_tensor("w1q_d", [NCH, 128, 4096], BF16)
    w2q_d = nc.dram_tensor("w2q_d", [32, 128, C], BF16)
    sxm_d = nc.dram_tensor("sxm_d", [16, 128], BF16)

    H = 4096
    with ExitStack() as es:
        sb = lambda n, sh, dt: es.enter_context(nc.sbuf_tensor(n, sh, dt))
        psa = lambda n, sh, dt: es.enter_context(nc.psum_tensor(n, sh, dt))
        S = Scratch(nc, es)
        q = SQ(nc, es)
        Yb = sb("Yb", [128, NCH * T], BF16)
        ident = sb("ident", [128, 128], BF16)
        wsc = sb("wsc", [1, 8], F32)
        wcol = sb("wcol", [128, 4], F32)
        q1a = sb("q1a", [128, NCH * T], BF16)
        rowx = sb("rowx", [1, 2 * T], BF16)
        x_scope = ExitStack()
        X = x_scope.enter_context(nc.sbuf_tensor("Xs", [128, NCH * T], F32))
        q.dma("y", X[:, :], Xd[:, :])
        q.dma("y", ident[:, :], identd[:, :])
        S.init(q)

        yvfm = q1a
        with nc.sbuf_tensor("ut", [128, C], BF16) as ut, \
             nc.sbuf_tensor("yct", [128, C], BF16) as yct, \
             nc.sbuf_tensor("wbuf5", [128, NCH * 128], BF16) as wbuf5, \
             nc.sbuf_tensor("boc", [128, NCH], F32) as boc, \
             nc.psum_tensor("tp_ps", [128, 128], BF16) as tp_ps, \
             nc.psum_tensor("wo_ps", [128, 512], F32) as wo_ps:
            q.dma("y", boc[:, :], bod[:, :])
            for tc in range(16):
                q.dma("y", ut[:, :], ud[tc * 128:(tc + 1) * 128, :])
                q.dma("y", yct[:, :], ycd[tc * 128:(tc + 1) * 128, :])
                q.op("v", "tensor_tensor", ut[:, :], ut[:, :], yct[:, :], ALU.mult)
                for ch in range(NCH):
                    q.op("t", "transpose", tp_ps[:, :], ut[:, ch * 128:(ch + 1) * 128],
                         ident[:, :])
                    q.op("s", "copy", yvfm[:, ch * T + tc * 128:ch * T + (tc + 1) * 128],
                         tp_ps[:, :])

            def wo_cb(j, sl, ps):
                q.op("v", "tensor_scalar", Yb[:, j * T + sl * 512:j * T + sl * 512 + 512],
                     ps[:, 0:512], boc[:, j:j + 1], None, ALU.add)

            emit_fm_matmul(q, yvfm, wod, NCH, NCH, wo_ps, wo_cb, wbuf5)
        emit_postln_residual(q, S, Yb, X)

        with nc.sbuf_tensor("wpl", [128, H], F32) as wpl, \
             nc.sbuf_tensor("acc", [128, 40], F32) as acc, \
             nc.sbuf_tensor("wqb", [128, H], BF16) as wqb, \
             nc.psum_tensor("wq_ps", [1, 512], F32) as wq_ps, \
             nc.psum_tensor("wc_ps", [128, 64], F32) as wc_ps:
            for wi, (src, dst, npl, wid) in enumerate(((w1d, w1q_d, NCH, H), (w2d, w2q_d, 32, C))):
                for p in range(npl):
                    q.dma("y", wpl[:, 0:wid], src[p])
                    q.op("s", "activation", wqb[:, 0:wid], wpl[:, 0:wid], AF.Abs,
                         accum_out=acc[:, p:p + 1])
                q.mm(wq_ps[0:1, 0:npl], S.ones_f[:, 0:1], acc[:, 0:npl], start=True, stop=True)
                q.mm_fence()
                q.op("v", "reduce_sum", wsc[0:1, wi * 4:wi * 4 + 1], wq_ps[0:1, 0:npl], AX)
                q.op("v", "tensor_scalar", wsc[0:1, wi * 4 + 1:wi * 4 + 2],
                     wsc[0:1, wi * 4:wi * 4 + 1], 1.0 / (npl * 128 * wid), None, ALU.mult)
                q.op("v", "tensor_scalar", wsc[0:1, wi * 4 + 1:wi * 4 + 2],
                     wsc[0:1, wi * 4 + 1:wi * 4 + 2], 1e-5, None, ALU.max)
                q.op("v", "reciprocal", wsc[0:1, wi * 4 + 2:wi * 4 + 3],
                     wsc[0:1, wi * 4 + 1:wi * 4 + 2])
                q.mm(wc_ps[:, 0:1], S.ones_fr[0:1, :], wsc[0:1, wi * 4 + 2:wi * 4 + 3],
                     start=True, stop=True)
                q.mm(wc_ps[:, 1:2], S.ones_fr[0:1, :], wsc[0:1, wi * 4 + 1:wi * 4 + 2],
                     start=True, stop=True)
                q.mm_fence()
                q.op("v", "tensor_copy", wcol[:, 2 * wi:2 * wi + 1], wc_ps[:, 0:1])
                q.op("v", "tensor_copy", wcol[:, 2 * wi + 1:2 * wi + 2], wc_ps[:, 1:2])
                for p in range(npl):
                    q.dma("y", wpl[:, 0:wid], src[p])
                    q.op("v", "tensor_scalar", wpl[:, 0:wid], wpl[:, 0:wid],
                         wcol[:, 2 * wi:2 * wi + 1], RND, ALU.mult, ALU.add)
                    q.op("v", "tensor_scalar", wpl[:, 0:wid], wpl[:, 0:wid],
                         -RND, -1.0, ALU.add, ALU.max)
                    q.op("v", "tensor_scalar", wqb[:, 0:wid], wpl[:, 0:wid],
                         1.0, None, ALU.min)
                    q.dma("y", dst[p], wqb[:, 0:wid])

        with nc.sbuf_tensor("ln1b", [128, NCH * T], BF16) as ln1b, \
             nc.sbuf_tensor("l1g", [128, 2 * NCH], F32) as l1g, \
             nc.sbuf_tensor("mx", [128, 16], BF16) as mx, \
             nc.sbuf_tensor("mrow16", [16, 128], BF16) as mrow16, \
             nc.psum_tensor("trb_ps", [128, 128], BF16) as trb_ps, \
             nc.psum_tensor("bcx_ps", [128, 1024], F32) as bcx_ps:
            q.dma("y", l1g[:, 0:NCH], l1gd[:, :])
            q.dma("y", l1g[:, NCH:2 * NCH], l1bd[:, :])
            mrowf = S.srows[0:1, 0:T]
            emit_ln_stats(q, S, X)
            for sl in range(4):
                q.mm(S.bc_ps[:, 0:512], S.ones_b[0:1, :], S.mb[0:1, sl * 512:(sl + 1) * 512])
                q.mm(S.bc_ps[:, 512:1024], S.ones_b[0:1, :], S.rb[0:1, sl * 512:(sl + 1) * 512])
                q.mm_fence()
                for ch in range(NCH):
                    o = ch * T + sl * 512
                    q.op("v", "tensor_tensor", S.t1[:, 0:512], X[:, o:o + 512],
                         S.bc_ps[:, 0:512], ALU.subtract)
                    q.op("v", "tensor_tensor", S.t1[:, 0:512], S.t1[:, 0:512],
                         S.bc_ps[:, 512:1024], ALU.mult)
                    q.op("v", "tensor_scalar", ln1b[:, o:o + 512], S.t1[:, 0:512],
                         l1g[:, ch:ch + 1], l1g[:, NCH + ch:NCH + ch + 1], ALU.mult, ALU.add)
            q.dma("y", Xsp_d[:, :], X[:, :])
            for ch in range(NCH):
                q.op("s", "activation", X[:, ch * T:(ch + 1) * T], ln1b[:, ch * T:(ch + 1) * T],
                     AF.Abs)
            q.op("v", "tensor_tensor", X[:, 0:T], X[:, 0:T], X[:, T:2 * T], ALU.max)
            q.op("v", "tensor_tensor", X[:, 2 * T:3 * T], X[:, 2 * T:3 * T], X[:, 3 * T:4 * T], ALU.max)
            q.op("v", "tensor_tensor", X[:, 4 * T:5 * T], X[:, 4 * T:5 * T], X[:, 5 * T:6 * T], ALU.max)
            q.op("v", "tensor_tensor", X[:, 6 * T:7 * T], X[:, 6 * T:7 * T], X[:, 7 * T:8 * T], ALU.max)
            q.op("v", "tensor_tensor", X[:, 0:T], X[:, 0:T], X[:, 2 * T:3 * T], ALU.max)
            q.op("v", "tensor_tensor", X[:, 4 * T:5 * T], X[:, 4 * T:5 * T], X[:, 6 * T:7 * T], ALU.max)
            q.op("v", "tensor_tensor", X[:, 0:T], X[:, 0:T], X[:, 4 * T:5 * T], ALU.max)
            q.op("s", "copy", Yb[:, 0:T], X[:, 0:T])
            for tc in range(16):
                q.op("t", "transpose", trb_ps[:, :], Yb[:, tc * 128:(tc + 1) * 128], ident[:, :])
                q.op("v", "reduce_max", mx[:, tc:tc + 1], trb_ps[:, :], AX)
            q.op("t", "transpose", trb_ps[0:16, :], mx[:, 0:16], ident[:, :])
            q.op("s", "copy", mrow16[:, :], trb_ps[0:16, 0:128])
            q.dma("y", sxm_d[:, :], mrow16[:, :])
            q.dma("g", mrowf[0:1, :], sxm_d.rearrange("a b -> () (a b)"))
            q.op("v", "tensor_scalar", mrowf[0:1, :], mrowf[0:1, :], 1e-5, None, ALU.max)
            q.op("v", "tensor_scalar", rowx[0:1, T:2 * T], mrowf[0:1, :],
                 wsc[0:1, 1:2], 1.0 / 127.0, ALU.mult, ALU.mult)
            q.op("s", "activation", mrowf[0:1, :], mrowf[0:1, :], AF.Ln)
            q.op("s", "activation", mrowf[0:1, :], mrowf[0:1, :], AF.Exp, scale=-1.0)
            q.op("v", "tensor_scalar", rowx[0:1, 0:T], mrowf[0:1, :], 127.0, None, ALU.mult)
            for sl in range(4):
                q.mm(bcx_ps[:, 0:512], S.ones_b[0:1, :], rowx[0:1, sl * 512:(sl + 1) * 512])
                q.mm(bcx_ps[:, 512:1024], S.ones_b[0:1, :], rowx[0:1, T + sl * 512:T + (sl + 1) * 512])
                q.mm_fence()
                for ch in range(NCH):
                    o = ch * T + sl * 512
                    q.op("v", "tensor_tensor", S.t1[:, 0:512], ln1b[:, o:o + 512],
                         bcx_ps[:, 0:512], ALU.mult)
                    q.op("v", "tensor_scalar", S.t1[:, 0:512], S.t1[:, 0:512],
                         RND, -RND, ALU.add, ALU.add)
                    q.op("v", "tensor_tensor", q1a[:, o:o + 512], S.t1[:, 0:512],
                         bcx_ps[:, 512:1024], ALU.mult)
        x_scope.close()

        with nc.sbuf_tensor("wbig", [128, 8 * 2048], BF16) as wbig, \
             nc.sbuf_tensor("gbf", [128, H], BF16) as gbf, \
             nc.sbuf_tensor("gsl", [128, 512], F32) as gsl, \
             nc.sbuf_tensor("zt", [128, H], BF16) as zt, \
             nc.sbuf_tensor("q2fm", [128, 32 * 128], BF16) as q2fm, \
             nc.sbuf_tensor("cols", [128, 16], F32) as cols, \
             nc.sbuf_tensor("rows2", [1, 3 * H], BF16) as rows2, \
             nc.psum_tensor("b1_ps", [128, 512], F32) as b1_ps, \
             nc.psum_tensor("tq_ps", [128, 128], BF16) as tq_ps, \
             nc.sbuf_tensor("b2c", [128, NCH], F32) as b2c, \
             nc.psum_tensor("bl2_ps", [128, 1024], F32) as bl2_ps:
            g2_ps = S.bc_ps
            q.dma("g", rows2[0:1, 0:H], b1rd[0:1, :])
            q.dma("g", rows2[0:1, H:2 * H], l2gd[0:1, :])
            q.dma("g", rows2[0:1, 2 * H:3 * H], l2bd[0:1, :])
            q.dma("y", b2c[:, :], b2d[:, :])
            for tc in range(16):
                q.op("g", "memset", cols[:, 0:2], 0.0)
                for half in range(2):
                    q.dma("y", wbig[:, :].rearrange("p (k c) -> p k c", k=NCH),
                          w1q_d[:, :, half * 2048:(half + 1) * 2048].rearrange("k p c -> p k c"))
                    for hs in range(4):
                        for k in range(NCH):
                            q.mm(b1_ps[:, 0:512],
                                 q1a[:, k * T + tc * 128:k * T + (tc + 1) * 128],
                                 wbig[:, k * 2048 + hs * 512:k * 2048 + hs * 512 + 512],
                                 start=(k == 0), stop=False)
                        ho = half * 2048 + hs * 512
                        q.mm(b1_ps[:, 0:512], S.ones_b[0:1, 0:128],
                             rows2[0:1, ho:ho + 512], start=False, stop=True)
                        q.mm_fence()
                        q.op("s", "activation", gsl[:, 0:512], b1_ps[:, 0:512], AF.Gelu,
                             accum_out=cols[:, 2:3])
                        q.op("v", "tensor_tensor", cols[:, 0:1], cols[:, 0:1],
                             cols[:, 2:3], ALU.add)
                        q.op("s", "activation", S.t1[:, 0:512], gsl[:, 0:512], AF.Square,
                             accum_out=cols[:, 3:4])
                        q.op("v", "tensor_tensor", cols[:, 1:2], cols[:, 1:2],
                             cols[:, 3:4], ALU.add)
                        q.op("s", "copy", gbf[:, ho:ho + 512], gsl[:, 0:512])
                q.op("v", "tensor_scalar", cols[:, 4:5], cols[:, 0:1], 1.0 / H, None, ALU.mult)
                q.op("v", "tensor_tensor", cols[:, 5:6], cols[:, 4:5], cols[:, 4:5], ALU.mult)
                q.op("v", "scalar_tensor_tensor", cols[:, 5:6], cols[:, 1:2], 1.0 / H,
                     cols[:, 5:6], ALU.mult, ALU.subtract)
                q.op("v", "tensor_scalar", cols[:, 5:6], cols[:, 5:6], float(EPS), None, ALU.add)
                q.op("s", "activation", cols[:, 6:7], cols[:, 5:6], AF.Ln)
                q.op("s", "activation", cols[:, 6:7], cols[:, 6:7], AF.Exp, scale=-0.5)
                q.op("v", "tensor_tensor", cols[:, 7:8], cols[:, 4:5], cols[:, 6:7], ALU.mult)
                q.op("v", "tensor_scalar", cols[:, 7:8], cols[:, 7:8], -1.0, None, ALU.mult)
                for hs in range(8):
                    ho = hs * 512
                    q.mm(g2_ps[:, 0:512], S.ones_b[0:1, :], rows2[0:1, H + ho:H + ho + 512])
                    q.mm(g2_ps[:, 512:1024], S.ones_b[0:1, :], rows2[0:1, 2 * H + ho:2 * H + ho + 512])
                    q.mm_fence()
                    q.op("v", "tensor_scalar", zt[:, ho:ho + 512], gbf[:, ho:ho + 512],
                         cols[:, 6:7], cols[:, 7:8], ALU.mult, ALU.add)
                    q.op("v", "tensor_tensor", zt[:, ho:ho + 512], zt[:, ho:ho + 512],
                         g2_ps[:, 0:512], ALU.mult)
                    q.op("v", "tensor_tensor", zt[:, ho:ho + 512], zt[:, ho:ho + 512],
                         g2_ps[:, 512:1024], ALU.add)
                q.op("v", "tensor_reduce", cols[:, 8:9], zt[:, :], AX, ALU.max,
                     apply_absolute_value=True)
                q.op("v", "tensor_scalar", cols[:, 8:9], cols[:, 8:9], 1e-5, None, ALU.max)
                q.op("v", "tensor_scalar", cols[:, 9:10], cols[:, 8:9],
                     wcol[:, 3:4], 1.0 / 127.0, ALU.mult, ALU.mult)
                q.op("s", "activation", cols[:, 10:11], cols[:, 8:9], AF.Ln)
                q.op("s", "activation", cols[:, 10:11], cols[:, 10:11], AF.Exp, scale=-1.0)
                q.op("v", "tensor_scalar", cols[:, 11:12], cols[:, 10:11], 127.0, None, ALU.mult)
                for hs in range(8):
                    ho = hs * 512
                    q.op("v", "tensor_scalar", S.t1[:, 0:512], zt[:, ho:ho + 512],
                         cols[:, 11:12], RND, ALU.mult, ALU.add)
                    q.op("v", "tensor_scalar", S.t1[:, 0:512], S.t1[:, 0:512],
                         -RND, None, ALU.add)
                    q.op("v", "tensor_scalar", zt[:, ho:ho + 512], S.t1[:, 0:512],
                         cols[:, 9:10], None, ALU.mult)
                for j2 in range(32):
                    q.op("t", "transpose", tq_ps[:, :], zt[:, j2 * 128:(j2 + 1) * 128], ident[:, :])
                    q.op("s", "copy", q2fm[:, j2 * 128:(j2 + 1) * 128], tq_ps[:, :])
                for j2 in range(32):
                    q.dma("y", wbig[:, 0:C], w2q_d[j2])
                    for half in range(2):
                        q.mm(bl2_ps[:, half * 512:(half + 1) * 512],
                             q2fm[:, j2 * 128:(j2 + 1) * 128],
                             wbig[:, half * 512:(half + 1) * 512],
                             start=(j2 == 0), stop=(j2 == 31))
                    q.mm_fence()
                q.op("s", "copy", zt[:, 0:C], bl2_ps[:, 0:1024])
                for co in range(NCH):
                    q.op("t", "transpose", tq_ps[:, :], zt[:, co * 128:(co + 1) * 128],
                         ident[:, :])
                    q.op("v", "tensor_scalar", Yb[:, co * T + tc * 128:co * T + (tc + 1) * 128],
                         tq_ps[:, :], b2c[:, co:co + 1], None, ALU.add)

        with nc.sbuf_tensor("Xs2", [128, NCH * T], F32) as X2s:
            q.dma("y", X2s[:, :], Xsp_d[:, :])
            emit_postln_residual(q, S, Yb, X2s)
            q.dma("y", X4out[:, :], X2s[:, :])
            q.finish()
    return nc


def _cols(vec):
    return np.ascontiguousarray(np.asarray(vec, np.float32).reshape(NCH, 128).T)


def kernel_multilaunch(**inputs):
    f32 = np.float32
    x = np.asarray(inputs["L0_attn_x"], f32)
    w_qkv = np.asarray(inputs["L0_attn_w_qkv"], f32)
    wc1 = np.asarray(inputs["L0_attn_w_conv1"], f32)
    wg = np.asarray(inputs["L0_ffn_w_gate"], np.int32)
    wu = np.asarray(inputs["L0_ffn_w_up"], np.int32)
    wd = np.asarray(inputs["L0_ffn_w_down"], np.int32)
    P1 = {k[8:]: np.asarray(v, f32) for k, v in inputs.items() if k.startswith("L1_attn_")}
    P2 = {k[7:]: np.asarray(v, f32) for k, v in inputs.items() if k.startswith("L1_ffn_")}

    cores = list(range(8))
    Xs = []
    for core in cores:
        b, h = core // 2, core % 2
        Xs.append(fm_layout(np.ascontiguousarray(x[b, 2048 * h:2048 * (h + 1), :].T)))

    nc1 = build_l1()
    wqkv_l = chunked_lhsT(np.ascontiguousarray(w_qkv.T))
    wc1c = _cols(wc1)
    r1 = run_bass_kernel_spmd(nc1, [{"X": Xs[c], "W": wqkv_l, "wc1": wc1c} for c in cores],
                              core_ids=cores).results

    nc2 = build_l2()
    maps2 = []
    for core in cores:
        pair = core ^ 1
        maps2.append({"z_both": np.stack([r1[core]["zloc"], r1[pair]["zloc"]]).astype(f32),
                      "vbf": r1[core]["vbf"]})
    r2 = run_bass_kernel_spmd(nc2, maps2, core_ids=cores).results

    nc3 = build_l3()
    wg_l = chunked_lhsT(np.ascontiguousarray(wg.T)).astype(np.int32)
    wu_l = chunked_lhsT(np.ascontiguousarray(wu.T)).astype(np.int32)
    wd_l = np.ascontiguousarray(wd.T.reshape(24, 128, 1024)).astype(np.int32)
    Wup = np.ascontiguousarray(P1["Wu"].reshape(8, 128, 1024))
    Wvp = np.ascontiguousarray(P1["Wv"].reshape(8, 128, 1024))
    maps3 = []
    for core in cores:
        a, bb_ = (core & ~1), (core | 1)
        stats = np.array([[r2[a]["mz"][0, 0], r2[a]["mz"][0, 1],
                           r2[bb_]["mz"][0, 0], r2[bb_]["mz"][0, 1]]], f32)
        wv2 = np.stack([un_fm(r2[a]["wv"], NCH, 1)[:, 0][None, :],
                        un_fm(r2[bb_]["wv"], NCH, 1)[:, 0][None, :]]).astype(f32)
        maps3.append({"X": Xs[core], "stats": stats, "wv2": wv2,
                      "wg": wg_l, "wu": wu_l, "wd": wd_l,
                      "Wu": Wup, "Wv": Wvp,
                      "bu": P1["bu"][None, :], "bv": P1["bv"][None, :]})
    r3 = run_bass_kernel_spmd(nc3, maps3, core_ids=cores).results

    nc4 = build_l4()
    fc = _fft_consts()
    peT = _pos_emb_T()
    dkc = _decay_cols()
    maps4 = []
    for core in cores:
        b, h = core // 2, core % 2
        vpair = np.concatenate([np.asarray(r3[2 * b]["v_tm"]),
                                np.asarray(r3[2 * b + 1]["v_tm"])], axis=0)
        csl = slice(512 * h, 512 * h + 512)
        maps4.append({"vsl": np.ascontiguousarray(vpair[:, csl]),
                      "W1": P1["W1"], "b1c": P1["b1"][:, None],
                      "W2": P1["W2"], "b2c": P1["b2"][:, None],
                      "W3o": np.ascontiguousarray(P1["W3"][:, csl]),
                      "b3r": np.ascontiguousarray(P1["b3"][None, csl]),
                      "peT": peT, "dkc": dkc, **fc})
    r4 = run_bass_kernel_spmd(nc4, maps4, core_ids=cores).results

    nc5 = build_l5()
    Wo_l = chunked_lhsT(P1["Wo"])
    maps5 = []
    for core in cores:
        b, h = core // 2, core % 2
        rows = slice(2048 * h, 2048 * h + 2048)
        yc = np.concatenate([np.asarray(r4[2 * b]["y_conv"])[rows, :],
                             np.asarray(r4[2 * b + 1]["y_conv"])[rows, :]], axis=1)
        maps5.append({"X2": np.asarray(r3[core]["X2"]),
                      "u_tm": np.asarray(r3[core]["u_tm"]),
                      "yc_tm": np.ascontiguousarray(yc),
                      "Wo_l": Wo_l, "bo_c": _cols(P1["bo"]),
                      "identin": np.eye(128, dtype=BFD),
                      "ln1g_c": _cols(P2["ln1_g"]), "ln1b_c": _cols(P2["ln1_b"]),
                      "w1T": np.ascontiguousarray(P2["w1"].T.reshape(8, 128, 4096)),
                      "b1r": P2["b1"][None, :],
                      "ln2g_r": P2["ln2_g"][None, :], "ln2b_r": P2["ln2_b"][None, :],
                      "w2T": np.ascontiguousarray(P2["w2"].T.reshape(32, 128, 1024)),
                      "b2_c": _cols(P2["b2"])})
    r5 = run_bass_kernel_spmd(nc5, maps5, core_ids=cores).results

    out = np.empty((4, 4096, 1024), f32)
    for core in cores:
        b, h = core // 2, core % 2
        out[b, 2048 * h:2048 * h + 2048, :] = un_fm(np.asarray(r5[core]["X4"]), NCH, T).T
    return out


def build_fused():
    fc = _fft_consts()
    nc = bass.Bass("TRN2", target_bir_lowering=False)
    AG = lambda name, shape, dt: nc.dram_tensor(name, shape, dt)
    Xd = nc.dram_tensor("X", [128, NCH * T], F32, kind="ExternalInput").ap()
    Wqkv = nc.dram_tensor("Wqkv", [24, 128, NCH * 128], F32, kind="ExternalInput").ap()
    wc1d = nc.dram_tensor("wc1", [128, NCH], F32, kind="ExternalInput").ap()
    wgd = nc.dram_tensor("wg", [24, 128, NCH * 128], I32, kind="ExternalInput").ap()
    wud = nc.dram_tensor("wu", [24, 128, NCH * 128], I32, kind="ExternalInput").ap()
    wdd = nc.dram_tensor("wd", [24, 128, NCH * 128], I32, kind="ExternalInput").ap()
    Wud = nc.dram_tensor("Wu", [NCH, 128, C], F32, kind="ExternalInput").ap()
    Wvd = nc.dram_tensor("Wv", [NCH, 128, C], F32, kind="ExternalInput").ap()
    bud = nc.dram_tensor("bu", [1, C], F32, kind="ExternalInput").ap()
    bvd = nc.dram_tensor("bv", [1, C], F32, kind="ExternalInput").ap()
    W1d = nc.dram_tensor("W1", [64, 64], F32, kind="ExternalInput").ap()
    b1d = nc.dram_tensor("b1c", [64, 1], F32, kind="ExternalInput").ap()
    W2d = nc.dram_tensor("W2", [64, 64], F32, kind="ExternalInput").ap()
    b2d = nc.dram_tensor("b2c", [64, 1], F32, kind="ExternalInput").ap()
    W3d = nc.dram_tensor("W3o", [64, 512], F32, kind="ExternalInput").ap()
    b3d = nc.dram_tensor("b3r", [1, 512], F32, kind="ExternalInput").ap()
    peTd = nc.dram_tensor("peT", [64, 4096], BF16, kind="ExternalInput").ap()
    dkd = nc.dram_tensor("dkc", [128, 32], F32, kind="ExternalInput").ap()
    cre_d = nc.dram_tensor("c_re", [64, 64 * K1H], BF16, kind="ExternalInput").ap()
    cim_d = nc.dram_tensor("c_im", [64, 64 * K1H], BF16, kind="ExternalInput").ap()
    le1_d = nc.dram_tensor("LE1", [64, 128], BF16, kind="ExternalInput").ap()
    le2_d = nc.dram_tensor("LE2", [64, 128], BF16, kind="ExternalInput").ap()
    lars_d = nc.dram_tensor("La_rS", [128, 64], BF16, kind="ExternalInput").ap()
    lars2_d = nc.dram_tensor("La_rS2", [128, 64], BF16, kind="ExternalInput").ap()
    lais_d = nc.dram_tensor("La_iS", [128, 64], BF16, kind="ExternalInput").ap()
    lais2_d = nc.dram_tensor("La_iS2", [128, 64], BF16, kind="ExternalInput").ap()
    mbr_d = nc.dram_tensor("Mb_r", [K1H, 64 * 64], BF16, kind="ExternalInput").ap()
    mbmi_d = nc.dram_tensor("Mb_mi", [K1H, 64 * 64], BF16, kind="ExternalInput").ap()
    wod = nc.dram_tensor("Wo_l", [NCH, 128, NCH * 128], F32, kind="ExternalInput").ap()
    bod = nc.dram_tensor("bo_c", [128, NCH], F32, kind="ExternalInput").ap()
    identd = nc.dram_tensor("identin", [128, 128], BF16, kind="ExternalInput").ap()
    seld = nc.dram_tensor("selc", [128, 2], F32, kind="ExternalInput").ap()
    l1gd = nc.dram_tensor("ln1g_c", [128, NCH], F32, kind="ExternalInput").ap()
    l1bd = nc.dram_tensor("ln1b_c", [128, NCH], F32, kind="ExternalInput").ap()
    w1d = nc.dram_tensor("w1T", [NCH, 128, 4096], F32, kind="ExternalInput").ap()
    b1rd = nc.dram_tensor("b1r", [1, 4096], F32, kind="ExternalInput").ap()
    l2gd = nc.dram_tensor("ln2g_r", [1, 4096], F32, kind="ExternalInput").ap()
    l2bd = nc.dram_tensor("ln2b_r", [1, 4096], F32, kind="ExternalInput").ap()
    w2d = nc.dram_tensor("w2T", [32, 128, C], F32, kind="ExternalInput").ap()
    b2cd = nc.dram_tensor("b2_c", [128, NCH], F32, kind="ExternalInput").ap()
    X4out = nc.dram_tensor("X4", [128, NCH * T], F32, kind="ExternalOutput").ap()
    zx_in = AG("zx_in", [128, NCH], F32)
    zx_out = AG("zx_out", [256, NCH], F32)
    mz_in = AG("mz_in", [1, 1026], F32)
    mz_out = AG("mz_out", [2, 1026], F32)
    vex_in = AG("vex_in", [T, C], BF16)
    vex_out = AG("vex_out", [2 * T, C], BF16)
    yx_in = AG("yx_in", [4096, 512], BF16)
    yx_out = AG("yx_out", [8192, 512], BF16)
    acd = AG("acd2", [128, NCH], F32)
    h_d = AG("h_d", [4096, 512], BF16)
    Av_d = AG("Av_d", [2, 64, K1H, 512], BF16)
    Ah_d = AG("Ah_d", [2, 64, K1H, 512], BF16)
    Hp_d = AG("Hp_d", [K1H, 128, 512], BF16)
    Hs_d = AG("Hs_d", [K1H, 128, 512], BF16)
    B_d = AG("B_d", [2, K1H, 64, 512], BF16)
    Xsp_d = AG("Xsp_d", [128, NCH * T], F32)
    u_d = AG("u_d", [T, C], BF16)
    hffn_d = AG("hffn_d", [24, 128, T], BF16)
    w1q_d = AG("w1q_d", [NCH, 128, 4096], BF16)
    w2q_d = AG("w2q_d", [32, 128, C], BF16)
    sxm_d = AG("sxm_d", [16, 128], BF16)
    PAIRS = [[0, 1], [2, 3], [4, 5], [6, 7]]

    H = 4096
    with ExitStack() as es:
        sb = lambda n, sh, dt: es.enter_context(nc.sbuf_tensor(n, sh, dt))
        psa = lambda n, sh, dt: es.enter_context(nc.psum_tensor(n, sh, dt))
        S = Scratch(nc, es)
        q = SQ(nc, es, n_dma_sems=24)
        S.ones_brow = sb("ones_brow", [1, 512], BF16)
        ident = sb("ident", [128, 128], BF16)
        wsc = sb("wsc", [1, 8], F32)
        wcol = sb("wcol", [128, 4], F32)
        q.dma("y", ident[:, :], identd[:, :])
        S.init(q)
        q.op("g", "memset", S.ones_brow[:, :], 1.0)

        q.barrier()
        x_scope = ExitStack()
        X = x_scope.enter_context(nc.sbuf_tensor("Xs", [128, NCH * T], F32))
        xn = x_scope.enter_context(nc.sbuf_tensor("xn", [128, NCH * T], BF16))
        q.dma("y", X[:, :], Xd[:, :])

        q.barrier()
        with nc.sbuf_tensor("vres", [128, NCH * T], BF16) as vres, \
             nc.sbuf_tensor("qb", [128, T], BF16) as qb, \
             nc.sbuf_tensor("wbuf", [128, NCH * 128], BF16) as wbuf, \
             nc.sbuf_tensor("wbufB", [128, NCH * 128], BF16) as wbufB, \
             nc.sbuf_tensor("wc1b", [128, NCH], BF16) as wc1b, \
             nc.sbuf_tensor("arow", [1, T], F32) as arow, \
             nc.sbuf_tensor("abrow", [1, T], BF16) as abrow, \
             nc.sbuf_tensor("abc", [128, T], BF16) as abc, \
             nc.sbuf_tensor("zcol", [128, NCH], F32) as zcol, \
             nc.sbuf_tensor("rows0", [1, 3 * C + 64], F32) as rows0, \
             nc.sbuf_tensor("acb", [128, NCH], BF16) as acb, \
             nc.psum_tensor("mm_ps", [128, 512], F32) as mm_ps, \
             nc.psum_tensor("mm_psB", [128, 512], F32) as mm_psB, \
             nc.psum_tensor("ag_ps", [1, 512], F32) as ag_ps, \
             nc.psum_tensor("ab_ps", [128, 512], F32) as ab_ps:
            lnvb = abrow
            kb = qb
            q.dma("g", wc1b[:, :], wc1d[:, :])
            q.op("g", "memset", arow[0:1, :], 0.0)
            emit_ln_stats(q, S, X)
            emit_ln_apply(q, S, X, xn)
            prod = qb
            state = {"abc_done": False}

            def make_abc():
                q.op("s", "activation", abrow[0:1, :], arow[0:1, :], AF.Copy,
                     scale=float(C ** -0.5))
                for sl in range(4):
                    q.mm(ab_ps[:, 0:512], S.ones_b[0:1, :], abrow[0:1, sl * 512:(sl + 1) * 512])
                    q.mm_fence()
                    q.op("s", "copy", abc[:, sl * 512:(sl + 1) * 512], ab_ps[:, 0:512])

            def out_cb(j, sl, ps):
                if j < 8:
                    q.op("s", "copy", qb[:, sl * 512:(sl + 1) * 512], ps[:, 0:512])
                    q.mm(ag_ps[0:1, 0:512], wc1b[:, j:j + 1],
                         qb[:, sl * 512:(sl + 1) * 512], start=True, stop=True)
                    q.mm_fence()
                    q.op("v", "tensor_tensor", arow[0:1, sl * 512:(sl + 1) * 512],
                         arow[0:1, sl * 512:(sl + 1) * 512], ag_ps[0:1, 0:512], ALU.add)
                elif j < 16:
                    if not state["abc_done"]:
                        state["abc_done"] = True
                        make_abc()
                    ch = j - 8
                    q.op("s", "copy", kb[:, sl * 512:(sl + 1) * 512], ps[:, 0:512])
                    if sl == 3:
                        q.op("v", "tensor_tensor", prod[:, :], kb[:, :], abc[:, :], ALU.mult)
                        q.op("v", "reduce_sum", zcol[:, ch:ch + 1], prod[:, :], AX)
                else:
                    ch = j - 16
                    q.op("s", "copy", vres[:, ch * T + sl * 512:ch * T + sl * 512 + 512],
                         ps[:, 0:512])

            emit_fm_matmul(q, xn, Wqkv, 24, NCH, [mm_ps, mm_psB], out_cb, [wbuf, wbufB])
            q.dma("y", zx_in[:, :], zcol[:, :])
            q.cc("AllGather", ALU.bypass, PAIRS, [zx_in.ap().opt()], [zx_out.ap().opt()])
            q.dma("y", zcol[:, :], zx_out[0:128, :])
            q.dma("y", S.t1[:, 0:NCH], zx_out[128:256, :])
            q.op("v", "tensor_tensor", zcol[:, :], zcol[:, :], S.t1[:, 0:NCH], ALU.add)
            q.dma("y", zx_in[:, :], zcol[:, :])
            zrow = rows0[0:1, 0:C]
            acr = rows0[0:1, C:2 * C]
            red = rows0[0:1, 3 * C:3 * C + 16]
            q.dma("y", zrow, zx_in.rearrange("p k -> () (p k)"))
            q.op("v", "reduce_max", red[0:1, 0:1], zrow, AX)
            q.op("v", "tensor_scalar", acr, zrow, red[0:1, 0:1], None, ALU.subtract)
            q.op("s", "activation", acr, acr, AF.Exp)
            q.op("v", "reduce_sum", red[0:1, 1:2], acr, AX)
            q.op("v", "reciprocal", red[0:1, 2:3], red[0:1, 1:2])
            q.op("v", "tensor_scalar", acr, acr, red[0:1, 2:3], None, ALU.mult)
            q.dma("y", acd.rearrange("p k -> () (p k)"), acr)
            q.dma("g", acb[:, :], acd[:, :])
            srow = rows0[0:1, 2 * C:3 * C + 0]
            srow = arow[0:1, :]
            for sl in range(4):
                for ch in range(NCH):
                    q.mm(ag_ps[0:1, 0:512], acb[:, ch:ch + 1],
                         vres[:, ch * T + sl * 512:ch * T + sl * 512 + 512],
                         start=(ch == 0), stop=(ch == NCH - 1))
                q.mm_fence()
                q.op("s", "copy", srow[0:1, sl * 512:(sl + 1) * 512], ag_ps[0:1, 0:512])
            erow = abrow
            q.op("v", "reduce_max", red[0:1, 3:4], srow, AX)
            q.op("v", "tensor_scalar", srow, srow, red[0:1, 3:4], None, ALU.subtract)
            q.op("s", "activation", srow, srow, AF.Exp)
            q.op("v", "reduce_sum", red[0:1, 4:5], srow, AX)
            q.op("s", "copy", erow[0:1, :], srow)
            for sl in range(4):
                q.mm(ab_ps[:, 0:512], S.ones_b[0:1, :], erow[0:1, sl * 512:(sl + 1) * 512])
                q.mm_fence()
                q.op("s", "copy", abc[:, sl * 512:(sl + 1) * 512], ab_ps[:, 0:512])
            for ch in range(NCH):
                q.op("v", "tensor_tensor", prod[:, :], vres[:, ch * T:(ch + 1) * T],
                     abc[:, :], ALU.mult)
                q.op("v", "reduce_sum", zcol[:, ch:ch + 1], prod[:, :], AX)
            q.dma("y", zx_in[:, :], zcol[:, :])
            q.dma("y", mz_in[0:1, 0:C].rearrange("o c -> o c"), zx_in.rearrange("p k -> () (p k)"))
            q.dma("y", mz_in[0:1, 1024:1026], red[0:1, 3:5])
            q.cc("AllGather", ALU.bypass, PAIRS, [mz_in.ap().opt()], [mz_out.ap().opt()])
            wv0 = rows0[0:1, 0:C]
            wv1 = rows0[0:1, C:2 * C]
            st4 = rows0[0:1, 3 * C:3 * C + 32]
            q.dma("y", wv0, mz_out[0:1, 0:C])
            q.dma("y", wv1, mz_out[1:2, 0:C])
            q.dma("y", st4[0:1, 0:2], mz_out[0:1, 1024:1026])
            q.dma("y", st4[0:1, 2:4], mz_out[1:2, 1024:1026])
            q.op("v", "tensor_tensor", st4[0:1, 4:5], st4[0:1, 0:1], st4[0:1, 2:3], ALU.max)
            q.op("v", "tensor_tensor", st4[0:1, 5:6], st4[0:1, 0:1], st4[0:1, 4:5], ALU.subtract)
            q.op("v", "tensor_tensor", st4[0:1, 6:7], st4[0:1, 2:3], st4[0:1, 4:5], ALU.subtract)
            q.op("s", "activation", st4[0:1, 5:7], st4[0:1, 5:7], AF.Exp)
            q.op("v", "tensor_tensor", st4[0:1, 7:8], st4[0:1, 5:6], st4[0:1, 1:2], ALU.mult)
            q.op("v", "scalar_tensor_tensor", st4[0:1, 7:8], st4[0:1, 3:4], st4[0:1, 6:7],
                 st4[0:1, 7:8], ALU.mult, ALU.add)
            q.op("v", "reciprocal", st4[0:1, 8:9], st4[0:1, 7:8])
            q.op("v", "tensor_tensor", st4[0:1, 9:10], st4[0:1, 5:6], st4[0:1, 8:9], ALU.mult)
            q.op("v", "tensor_tensor", st4[0:1, 10:11], st4[0:1, 6:7], st4[0:1, 8:9], ALU.mult)
            av = rows0[0:1, 2 * C:3 * C]
            q.op("v", "tensor_scalar", av, wv0, st4[0:1, 9:10], None, ALU.mult)
            q.op("v", "scalar_tensor_tensor", av, wv1, st4[0:1, 10:11], av, ALU.mult, ALU.add)
            q.op("v", "reduce_sum", st4[0:1, 11:12], av, AX)
            q.op("v", "tensor_scalar", st4[0:1, 11:12], st4[0:1, 11:12], 1.0 / C, None, ALU.mult)
            q.op("v", "tensor_scalar", av, av, st4[0:1, 11:12], None, ALU.subtract)
            q.op("s", "activation", wv0, av, AF.Square)
            q.op("v", "reduce_sum", st4[0:1, 12:13], wv0, AX)
            q.op("v", "tensor_scalar", st4[0:1, 12:13], st4[0:1, 12:13], 1.0 / C, None, ALU.mult)
            q.op("v", "tensor_scalar", st4[0:1, 12:13], st4[0:1, 12:13], float(EPS), None, ALU.add)
            q.op("s", "activation", st4[0:1, 12:13], st4[0:1, 12:13], AF.Ln)
            q.op("s", "activation", st4[0:1, 12:13], st4[0:1, 12:13], AF.Exp, scale=-0.5)
            q.op("v", "tensor_scalar", av, av, st4[0:1, 12:13], None, ALU.mult)
            q.op("s", "copy", lnvb[0:1, 0:C], av)
            for ch in range(NCH):
                q.mm(ab_ps[:, 0:512], lnvb[0:1, ch:C:NCH], S.ones_brow[0:1, 0:512])
                q.mm_fence()
                for sl in range(4):
                    o = ch * T + sl * 512
                    q.op("v", "tensor_tensor", X[:, o:o + 512], X[:, o:o + 512],
                         ab_ps[:, 0:512], ALU.add)

        emit_ln_stats(q, S, X)
        emit_ln_apply(q, S, X, xn)
        with nc.sbuf_tensor("Ybf0", [128, NCH * T], BF16) as Ybf, \
             nc.psum_tensor("ff_ps", [128, 512], F32) as ff_psa, \
             nc.psum_tensor("ff_psB", [128, 512], F32) as ff_psb, \
             nc.psum_tensor("dn_ps", [128, 512], F32) as dn_psa, \
             nc.psum_tensor("dn_psB", [128, 512], F32) as dn_psb:
          with nc.sbuf_tensor("gch", [128, T], BF16) as gch, \
               nc.sbuf_tensor("wbuf3", [128, NCH * 128], BF16) as wbuf3a, \
               nc.sbuf_tensor("wbuf3B", [128, NCH * 128], BF16) as wbuf3b:
            hj = gch
            for j in range(24):
                wA = wbuf3a if j % 2 == 0 else wbuf3b
                q.dma("g", wA[:, 0:NCH * 128], wgd[j])
                for sl in range(4):
                    fp = ff_psa if sl % 2 == 0 else ff_psb
                    for k in range(NCH):
                        q.mm(fp[:, 0:512], wA[:, k * 128:(k + 1) * 128],
                             xn[:, k * T + sl * 512:k * T + sl * 512 + 512],
                             start=(k == 0), stop=(k == NCH - 1))
                    q.mm_fence()
                    q.op("s", "activation", gch[:, sl * 512:(sl + 1) * 512], fp[:, 0:512],
                         AF.Silu, scale=1.0 / 127.0)
                wB = wbuf3b if j % 2 == 0 else wbuf3a
                q.dma("g", wB[:, 0:NCH * 128], wud[j])
                for sl in range(4):
                    fp = ff_psa if sl % 2 == 0 else ff_psb
                    for k in range(NCH):
                        q.mm(fp[:, 0:512], wB[:, k * 128:(k + 1) * 128],
                             xn[:, k * T + sl * 512:k * T + sl * 512 + 512],
                             start=(k == 0), stop=(k == NCH - 1))
                    q.mm_fence()
                    q.op("v", "tensor_tensor", hj[:, sl * 512:(sl + 1) * 512],
                         gch[:, sl * 512:(sl + 1) * 512], fp[:, 0:512], ALU.mult)
                q.dma("y", hffn_d[j], hj[:, :])
          q.barrier()
          with nc.sbuf_tensor("hslab", [128, 24 * 512], BF16) as hslab, \
               nc.sbuf_tensor("wdc", [128, 24 * 128], BF16) as wdca, \
               nc.sbuf_tensor("wdcB", [128, 24 * 128], BF16) as wdcb:
            for sl in range(4):
                q.dma("y", hslab[:, :].rearrange("p (j c) -> p j c", j=24),
                      hffn_d[:, :, sl * 512:(sl + 1) * 512].rearrange("j p c -> p j c"))
                for co in range(NCH):
                    wd_b = wdca if co % 2 == 0 else wdcb
                    if sl == 0 or True:
                        q.dma("g", wd_b[:, :].rearrange("p (j c) -> p j c", j=24),
                              wdd[:, :, co * 128:(co + 1) * 128].rearrange("j p c -> p j c"))
                    dp = dn_psa if co % 2 == 0 else dn_psb
                    for j in range(24):
                        q.mm(dp[:, 0:512], wd_b[:, j * 128:(j + 1) * 128],
                             hslab[:, j * 512:(j + 1) * 512],
                             start=(j == 0), stop=(j == 23))
                    q.mm_fence()
                    q.op("s", "activation", Ybf[:, co * T + sl * 512:co * T + sl * 512 + 512],
                         dp[:, 0:512], AF.Copy, scale=1.0 / (127.0 * 127.0))
          q.barrier()
        emit_postln_residual(q, S, Ybf, X)

        emit_ln_stats(q, S, X)
        emit_ln_apply(q, S, X, xn)
        q.barrier()
        with nc.sbuf_tensor("wup", [128, NCH * C], BF16) as wup, \
             nc.sbuf_tensor("wvp", [128, NCH * C], BF16) as wvp, \
             nc.sbuf_tensor("brow", [1, 2 * C], BF16) as brow, \
             nc.sbuf_tensor("uvt", [128, C], BF16) as uvta, \
             nc.sbuf_tensor("uvtB", [128, C], BF16) as uvtb, \
             nc.psum_tensor("uv_ps", [128, 512], F32) as uv_psa, \
             nc.psum_tensor("uv_psB", [128, 512], F32) as uv_psb:
            for k in range(NCH):
                q.dma("g", wup[:, k * C:(k + 1) * C], Wud[k])
                q.dma("g", wvp[:, k * C:(k + 1) * C], Wvd[k])
            q.dma("g", brow[0:1, 0:C], bud[0:1, :])
            q.dma("g", brow[0:1, C:2 * C], bvd[0:1, :])
            for which, wpl, bofs in ((0, wup, 0), (1, wvp, C)):
                for tc in range(16):
                    uvt = uvta if tc % 2 == 0 else uvtb
                    for half in range(2):
                        up = uv_psa if half == 0 else uv_psb
                        for k in range(NCH):
                            q.mm(up[:, 0:512],
                                 xn[:, k * T + tc * 128:k * T + (tc + 1) * 128],
                                 wpl[:, k * C + half * 512:k * C + half * 512 + 512],
                                 start=(k == 0), stop=False)
                        q.mm(up[:, 0:512], S.ones_b[0:1, 0:128],
                             brow[0:1, bofs + half * 512:bofs + half * 512 + 512],
                             start=False, stop=True)
                        q.mm_fence()
                        q.op("s", "copy", uvt[:, half * 512:half * 512 + 512], up[:, 0:512])
                    if which == 0:
                        q.dma("y", u_d[tc * 128:(tc + 1) * 128, :], uvt[:, :])
                    else:
                        q.dma("y", vex_in[tc * 128:(tc + 1) * 128, :], uvt[:, :])
        q.cc("AllGather", ALU.bypass, PAIRS, [vex_in.ap().opt()], [vex_out.ap().opt()])
        q.dma("y", Xsp_d[:, :], X[:, :])
        x_scope.close()

        q.barrier()
        cs1 = ExitStack()
        pe = cs1.enter_context(nc.sbuf_tensor("pe", [64, 4096], BF16))
        w12 = cs1.enter_context(nc.sbuf_tensor("w12", [64, 128], BF16))
        b12 = cs1.enter_context(nc.sbuf_tensor("b12", [64, 2], F32))
        w3 = cs1.enter_context(nc.sbuf_tensor("w3", [64, 512], BF16))
        b3r = cs1.enter_context(nc.sbuf_tensor("b3rs", [1, 512], BF16))
        cur1 = cs1.enter_context(nc.sbuf_tensor("cur1", [64, 4096], BF16))
        cur2 = cs1.enter_context(nc.sbuf_tensor("cur2", [64, 4096], BF16))
        dk = cs1.enter_context(nc.sbuf_tensor("dk", [128, 32], F32))
        htile = cs1.enter_context(nc.sbuf_tensor("htile", [128, 512], BF16))
        f_ps = cs1.enter_context(nc.psum_tensor("f_ps", [64, 512], F32))
        f2_ps = cs1.enter_context(nc.psum_tensor("f2_ps", [128, 512], F32))
        q.dma("y", pe[:, :], peTd[:, :])
        q.dma("g", w12[:, 0:64], W1d[:, :])
        q.dma("g", w12[:, 64:128], W2d[:, :])
        q.dma("y", b12[:, 0:1], b1d[:, :])
        q.dma("y", b12[:, 1:2], b2d[:, :])
        q.dma("g", w3[:, :], W3d[:, :])
        q.dma("g", b3r[0:1, :], b3d[0:1, :])
        q.dma("y", dk[:, :], dkd[:, :])
        for stage, (src, dst, bcol) in enumerate(((pe, cur1, 0), (cur1, cur2, 1))):
            for sl in range(8):
                q.mm(f_ps[:, 0:512], w12[:, stage * 64:stage * 64 + 64],
                     src[:, sl * 512:(sl + 1) * 512], start=True, stop=True)
                q.mm_fence()
                q.op("s", "activation", dst[:, sl * 512:(sl + 1) * 512], f_ps[:, 0:512],
                     AF.Gelu_apprx_tanh, bias=b12[:, bcol:bcol + 1])
        for g in range(32):
            q.mm(f2_ps[:, 0:512], cur2[:, g * 128:(g + 1) * 128], w3[:, :],
                 start=True, stop=False)
            q.mm(f2_ps[:, 0:512], S.ones_b[0:1, 0:128], b3r[0:1, :], start=False, stop=True)
            q.mm_fence()
            q.op("s", "activation", htile[:, :], f2_ps[:, 0:512], AF.Copy,
                 scale=dk[:, g:g + 1])
            q.dma("y", h_d[g * 128:(g + 1) * 128, :], htile[:, :])
        grid = cs1.enter_context(nc.sbuf_tensor("grid", [64, 64 * 512], BF16))
        gtmp = cs1.enter_context(nc.sbuf_tensor("gtmp", [64, 8192], BF16))
        selc = cs1.enter_context(nc.sbuf_tensor("selsb", [128, 2], F32))
        cre = cs1.enter_context(nc.sbuf_tensor("cre", [64, 64 * K1H], BF16))
        cim = cs1.enter_context(nc.sbuf_tensor("cim", [64, 64 * K1H], BF16))
        abuf = cs1.enter_context(nc.sbuf_tensor("abuf", [K1H, 8 * 2 * 512], BF16))
        s1_ps = cs1.enter_context(nc.psum_tensor("s1_ps", [K1H, 1024], F32))
        q.dma("y", selc[:, :], seld[:, :])
        q.dma("y", cre[:, :], cre_d[:, :])
        q.dma("y", cim[:, :], cim_d[:, :])
        for src, dstA in ((vex_out, Av_d), (h_d, Ah_d)):
            if src is vex_out:
                for s2b8 in range(8):
                    rows8 = src.rearrange("(s1 s2) c -> s1 s2 c", s2=64)[:, s2b8 * 8:(s2b8 + 1) * 8, :]
                    q.dma("y", gtmp[:, 0:4096].rearrange("p (s c) -> p s c", s=8),
                          rows8[:, :, 0:512])
                    q.dma("y", gtmp[:, 4096:8192].rearrange("p (s c) -> p s c", s=8),
                          rows8[:, :, 512:1024])
                    q.op("v", "tensor_scalar", gtmp[:, 0:4096], gtmp[:, 0:4096],
                         selc[0:64, 0:1], None, ALU.mult)
                    q.op("v", "scalar_tensor_tensor", grid[:, s2b8 * 4096:(s2b8 + 1) * 4096],
                         gtmp[:, 4096:8192], selc[0:64, 1:2], gtmp[:, 0:4096],
                         ALU.mult, ALU.add)
            else:
                q.dma("y", grid[:, :].rearrange("p (s c) -> p s c", s=64),
                      src.rearrange("(s1 s2) c -> s1 s2 c", s2=64))
            for s2b in range(8):
                for i in range(8):
                    s2 = s2b * 8 + i
                    q.mm(s1_ps[:, 0:512], cre[:, s2 * K1H:(s2 + 1) * K1H],
                         grid[:, s2 * 512:(s2 + 1) * 512], start=True, stop=True)
                    q.mm_fence()
                    q.op("s", "copy", abuf[:, (i * 2) * 512:(i * 2) * 512 + 512],
                         s1_ps[:, 0:512])
                    q.mm(s1_ps[:, 512:1024], cim[:, s2 * K1H:(s2 + 1) * K1H],
                         grid[:, s2 * 512:(s2 + 1) * 512], start=True, stop=True)
                    q.mm_fence()
                    q.op("s", "copy", abuf[:, (i * 2 + 1) * 512:(i * 2 + 1) * 512 + 512],
                         s1_ps[:, 512:1024])
                av2 = abuf[:, :].rearrange("k (s comp c) -> k s comp c", s=8, comp=2)
                for comp in range(2):
                    q.dma("y", dstA[comp, s2b * 8:(s2b + 1) * 8, :, :].rearrange("s k c -> k s c"),
                          av2[:, :, comp, :])
        q.barrier()
        cs1.close()
        cs2 = ExitStack()
        arb = cs2.enter_context(nc.sbuf_tensor("arb", [64, 2 * 4 * 512], BF16))
        aib = cs2.enter_context(nc.sbuf_tensor("aib", [64, 2 * 4 * 512], BF16))
        le = cs2.enter_context(nc.sbuf_tensor("le", [64, 256], BF16))
        hpb = cs2.enter_context(nc.sbuf_tensor("hpb", [128, 2 * 4 * 512], BF16))
        x_psa = cs2.enter_context(nc.psum_tensor("x_ps", [128, 512], F32))
        x_psb = cs2.enter_context(nc.psum_tensor("x_psB", [128, 512], F32))
        q.dma("y", le[:, 0:128], le1_d[:, :])
        q.dma("y", le[:, 128:256], le2_d[:, :])
        for kb in range(17):
            n = min(4, K1H - kb * 4)
            par = kb % 2
            q.dma("y", arb[:, par * 2048:par * 2048 + n * 512].rearrange("p (k c) -> p k c", k=n),
                  Ah_d[0, :, kb * 4:kb * 4 + n, :].rearrange("s k c -> p_is_s k c".replace("p_is_s", "s")))
            q.dma("y", aib[:, par * 2048:par * 2048 + n * 512].rearrange("p (k c) -> p k c", k=n),
                  Ah_d[1, :, kb * 4:kb * 4 + n, :])
            for kk in range(n):
                k1 = kb * 4 + kk
                xp = x_psa if k1 % 2 == 0 else x_psb
                q.mm(xp[:, 0:512], le[:, 0:128],
                     arb[:, par * 2048 + kk * 512:par * 2048 + (kk + 1) * 512],
                     start=True, stop=False)
                q.mm(xp[:, 0:512], le[:, 128:256],
                     aib[:, par * 2048 + kk * 512:par * 2048 + (kk + 1) * 512],
                     start=False, stop=True)
                q.mm_fence()
                q.op("s", "copy", hpb[:, par * 2048 + kk * 512:par * 2048 + (kk + 1) * 512],
                     xp[:, 0:512])
            hps_sl = hpb[:, par * 2048:par * 2048 + n * 512]
            q.dma("y", Hp_d[kb * 4:kb * 4 + n, :, :].rearrange("k p c -> p k c"),
                  hps_sl.rearrange("p (k c) -> p k c", k=n))
            q.dma("y", Hs_d[kb * 4:kb * 4 + n, 64:128, :].rearrange("k p c -> p k c"),
                  hpb[0:64, par * 2048:par * 2048 + n * 512].rearrange("p (k c) -> p k c", k=n))
            q.dma("y", Hs_d[kb * 4:kb * 4 + n, 0:64, :].rearrange("k p c -> p k c"),
                  hpb[64:128, par * 2048:par * 2048 + n * 512].rearrange("p (k c) -> p k c", k=n))
        arb2 = cs2.enter_context(nc.sbuf_tensor("ar2", [64, 2 * 4 * 512], BF16))
        aib2 = cs2.enter_context(nc.sbuf_tensor("ai2", [64, 2 * 4 * 512], BF16))
        la = cs2.enter_context(nc.sbuf_tensor("la", [128, 256], BF16))
        hpsb = cs2.enter_context(nc.sbuf_tensor("hps", [128, 2 * 4 * 1024], BF16))
        ssb = cs2.enter_context(nc.sbuf_tensor("ss", [128, 2 * 1024], BF16))
        bbb = cs2.enter_context(nc.sbuf_tensor("bb", [128, 2 * 4 * 512], BF16))
        b_psa = cs2.enter_context(nc.psum_tensor("b_ps", [128, 512], F32))
        b_psb = cs2.enter_context(nc.psum_tensor("b_psB", [128, 512], F32))
        q.dma("y", la[:, 0:64], lars_d[:, :])
        q.dma("y", la[:, 64:128], lars2_d[:, :])
        q.dma("y", la[:, 128:192], lais_d[:, :])
        q.dma("y", la[:, 192:256], lais2_d[:, :])
        for kb in range(17):
            n = min(4, K1H - kb * 4)
            par = kb % 2
            ao = par * 2048
            q.dma("y", arb2[:, ao:ao + n * 512].rearrange("p (k c) -> p k c", k=n),
                  Av_d[0, :, kb * 4:kb * 4 + n, :])
            q.dma("y", aib2[:, ao:ao + n * 512].rearrange("p (k c) -> p k c", k=n),
                  Av_d[1, :, kb * 4:kb * 4 + n, :])
            ho = par * 4096
            q.dma("y", hpsb[:, ho:ho + n * 1024].rearrange("p (k two c) -> k p (two c)", k=n, two=2)
                  if False else hpsb[:, ho:ho + n * 512].rearrange("p (k c) -> p k c", k=n),
                  Hp_d[kb * 4:kb * 4 + n, :, :].rearrange("k p c -> p k c"))
            q.dma("y", hpsb[:, ho + 2048:ho + 2048 + n * 512].rearrange("p (k c) -> p k c", k=n),
                  Hs_d[kb * 4:kb * 4 + n, :, :].rearrange("k p c -> p k c"))
            for kk in range(n):
                k1 = kb * 4 + kk
                xp = x_psa if k1 % 2 == 0 else x_psb
                bp = b_psa if k1 % 2 == 0 else b_psb
                sso = (k1 % 2) * 1024
                q.mm(xp[:, 0:512], le[:, 0:128], arb2[:, ao + kk * 512:ao + (kk + 1) * 512],
                     start=True, stop=False)
                q.mm(xp[:, 0:512], le[:, 128:256], aib2[:, ao + kk * 512:ao + (kk + 1) * 512],
                     start=False, stop=True)
                q.mm_fence()
                q.op("v", "tensor_tensor", ssb[:, sso:sso + 512], xp[:, 0:512],
                     hpsb[:, ho + kk * 512:ho + (kk + 1) * 512], ALU.mult)
                q.op("v", "tensor_tensor", ssb[:, sso + 512:sso + 1024], xp[:, 0:512],
                     hpsb[:, ho + 2048 + kk * 512:ho + 2048 + (kk + 1) * 512], ALU.mult)
                q.mm(bp[0:64, 0:512], la[:, 0:64], ssb[:, sso:sso + 512], start=True, stop=False)
                q.mm(bp[0:64, 0:512], la[:, 64:128], ssb[:, sso + 512:sso + 1024], start=False, stop=True)
                q.mm(bp[64:128, 0:512], la[:, 128:192], ssb[:, sso:sso + 512], start=True, stop=False)
                q.mm(bp[64:128, 0:512], la[:, 192:256], ssb[:, sso + 512:sso + 1024], start=False, stop=True)
                q.mm_fence()
                q.op("s", "copy", bbb[:, ao + kk * 512:ao + (kk + 1) * 512], bp[:, 0:512])
            q.dma("y", B_d[0, kb * 4:kb * 4 + n, :, :].rearrange("k t c -> t k c"),
                  bbb[0:64, ao:ao + n * 512].rearrange("p (k c) -> p k c", k=n))
            q.dma("y", B_d[1, kb * 4:kb * 4 + n, :, :].rearrange("k t c -> t k c"),
                  bbb[64:128, ao:ao + n * 512].rearrange("p (k c) -> p k c", k=n))
        mbr = cs2.enter_context(nc.sbuf_tensor("mbr", [K1H, 64 * 64], BF16))
        mbmi = cs2.enter_context(nc.sbuf_tensor("mbmi", [K1H, 64 * 64], BF16))
        bgrb = cs2.enter_context(nc.sbuf_tensor("bgr", [K1H, 2 * 4 * 1024], BF16))
        ytb = cs2.enter_context(nc.sbuf_tensor("yt", [64, 2 * 4 * 512], BF16))
        y_psa = cs2.enter_context(nc.psum_tensor("y_ps", [64, 512], F32))
        y_psb = cs2.enter_context(nc.psum_tensor("y_psB", [64, 512], F32))
        q.dma("y", mbr[:, :], mbr_d[:, :])
        q.dma("y", mbmi[:, :], mbmi_d[:, :])
        yxv = yx_in.rearrange("(t1 t2) c -> t2 t1 c", t2=64)
        for tb in range(16):
            par = tb % 2
            bo = par * 4096
            q.dma("y", bgrb[:, bo:bo + 2048].rearrange("p (t c) -> p t c", t=4),
                  B_d[0, :, tb * 4:(tb + 1) * 4, :])
            q.dma("y", bgrb[:, bo + 2048:bo + 4096].rearrange("p (t c) -> p t c", t=4),
                  B_d[1, :, tb * 4:(tb + 1) * 4, :])
            for kk in range(4):
                t2 = tb * 4 + kk
                yp = y_psa if t2 % 2 == 0 else y_psb
                q.mm(yp[:, 0:512], mbr[:, t2 * 64:(t2 + 1) * 64],
                     bgrb[:, bo + kk * 512:bo + (kk + 1) * 512],
                     start=True, stop=False)
                q.mm(yp[:, 0:512], mbmi[:, t2 * 64:(t2 + 1) * 64],
                     bgrb[:, bo + 2048 + kk * 512:bo + 2048 + (kk + 1) * 512],
                     start=False, stop=True)
                q.mm_fence()
                q.op("s", "copy", ytb[:, par * 2048 + kk * 512:par * 2048 + (kk + 1) * 512],
                     yp[:, 0:512])
            q.dma("y", yxv[tb * 4:(tb + 1) * 4, :, :].rearrange("t2 t1 c -> t1 t2 c"),
                  ytb[:, par * 2048:par * 2048 + 2048].rearrange("p (t c) -> p t c", t=4))
        cs2.close()
        q.cc("AllGather", ALU.bypass, PAIRS, [yx_in.ap().opt()], [yx_out.ap().opt()])

        q.barrier()
        Yb = sb("Yb", [128, NCH * T], BF16)
        q1a = sb("q1a", [128, NCH * T], BF16)
        rowx = sb("rowx", [1, 2 * T], BF16)
        q.barrier()
        x2_scope = ExitStack()
        X = x2_scope.enter_context(nc.sbuf_tensor("Xs2", [128, NCH * T], F32))
        q.dma("y", X[:, :], Xsp_d[:, :])
        yvfm = q1a
        q.barrier()
        with nc.sbuf_tensor("ut2", [128, C], BF16) as ut2, \
             nc.sbuf_tensor("yct", [128, C], BF16) as yct, \
             nc.sbuf_tensor("ytmp", [128, C], BF16) as ytmp, \
             nc.sbuf_tensor("sel2", [128, 2], F32) as sel2, \
             nc.sbuf_tensor("wbuf5", [128, NCH * 128], BF16) as wbuf5, \
             nc.sbuf_tensor("boc", [128, NCH], F32) as boc, \
             nc.psum_tensor("tp_ps", [128, 128], BF16) as tp_ps, \
             nc.psum_tensor("tp_psB", [128, 128], BF16) as tp_psB, \
             nc.psum_tensor("wo_ps", [128, 512], F32) as wo_ps, \
             nc.psum_tensor("wo_psB", [128, 512], F32) as wo_psB, \
             nc.sbuf_tensor("wpl", [128, H], F32) as wpl, \
             nc.sbuf_tensor("acc", [128, 40], F32) as acc, \
             nc.sbuf_tensor("wqb", [128, H], BF16) as wqb, \
             nc.psum_tensor("wq_ps", [1, 512], F32) as wq_ps, \
             nc.psum_tensor("wc_ps", [128, 64], F32) as wc_ps:
            q.dma("y", boc[:, :], bod[:, :])
            q.dma("y", sel2[:, :], seld[:, :])
            for tc in range(16):
                for part in range(2):
                    base = part * 4096
                    q.dma("y", ytmp[:, 0:512], yx_out[base + tc * 128:base + (tc + 1) * 128, :])
                    q.dma("y", ytmp[:, 512:1024], yx_out[base + 2048 + tc * 128:base + 2048 + (tc + 1) * 128, :])
                    q.op("v", "tensor_scalar", ytmp[:, 0:512], ytmp[:, 0:512],
                         sel2[:, 0:1], None, ALU.mult)
                    q.op("v", "scalar_tensor_tensor", yct[:, part * 512:(part + 1) * 512],
                         ytmp[:, 512:1024], sel2[:, 1:2], ytmp[:, 0:512], ALU.mult, ALU.add)
                q.dma("y", ut2[:, :], u_d[tc * 128:(tc + 1) * 128, :])
                q.op("v", "tensor_tensor", yct[:, :], yct[:, :], ut2[:, :], ALU.mult)
                for ch in range(NCH):
                    tp = tp_ps if ch % 2 == 0 else tp_psB
                    q.op("t", "transpose", tp[:, :], yct[:, ch * 128:(ch + 1) * 128],
                         ident[:, :])
                    q.op("s", "copy", yvfm[:, ch * T + tc * 128:ch * T + (tc + 1) * 128],
                         tp[:, :])

            def wo_cb(j, sl, ps):
                q.op("v", "tensor_scalar", Yb[:, j * T + sl * 512:j * T + sl * 512 + 512],
                     ps[:, 0:512], boc[:, j:j + 1], None, ALU.add)

            emit_fm_matmul(q, yvfm, wod, NCH, NCH, [wo_ps, wo_psB], wo_cb, wbuf5)
            emit_postln_residual(q, S, Yb, X)
            for wi, (src, dst, npl, wid) in enumerate(((w1d, w1q_d, NCH, H), (w2d, w2q_d, 32, C))):
                for p in range(npl):
                    q.dma("s", wpl[:, 0:wid], src[p])
                    q.op("s", "activation", wqb[:, 0:wid], wpl[:, 0:wid], AF.Abs,
                         accum_out=acc[:, p:p + 1])
                q.mm(wq_ps[0:1, 0:npl], S.ones_f[:, 0:1], acc[:, 0:npl], start=True, stop=True)
                q.mm_fence()
                q.op("v", "reduce_sum", wsc[0:1, wi * 4:wi * 4 + 1], wq_ps[0:1, 0:npl], AX)
                q.op("v", "tensor_scalar", wsc[0:1, wi * 4 + 1:wi * 4 + 2],
                     wsc[0:1, wi * 4:wi * 4 + 1], 1.0 / (npl * 128 * wid), None, ALU.mult)
                q.op("v", "tensor_scalar", wsc[0:1, wi * 4 + 1:wi * 4 + 2],
                     wsc[0:1, wi * 4 + 1:wi * 4 + 2], 1e-5, None, ALU.max)
                q.op("v", "reciprocal", wsc[0:1, wi * 4 + 2:wi * 4 + 3],
                     wsc[0:1, wi * 4 + 1:wi * 4 + 2])
                q.mm(wc_ps[:, 0:1], S.ones_fr[0:1, :], wsc[0:1, wi * 4 + 2:wi * 4 + 3],
                     start=True, stop=True)
                q.mm(wc_ps[:, 1:2], S.ones_fr[0:1, :], wsc[0:1, wi * 4 + 1:wi * 4 + 2],
                     start=True, stop=True)
                q.mm_fence()
                q.op("v", "tensor_copy", wcol[:, 2 * wi:2 * wi + 1], wc_ps[:, 0:1])
                q.op("v", "tensor_copy", wcol[:, 2 * wi + 1:2 * wi + 2], wc_ps[:, 1:2])
                for p in range(npl):
                    q.dma("s", wpl[:, 0:wid], src[p])
                    q.op("v", "tensor_scalar", wpl[:, 0:wid], wpl[:, 0:wid],
                         wcol[:, 2 * wi:2 * wi + 1], RND, ALU.mult, ALU.add)
                    q.op("v", "tensor_scalar", wpl[:, 0:wid], wpl[:, 0:wid],
                         -RND, -1.0, ALU.add, ALU.max)
                    q.op("v", "tensor_scalar", wqb[:, 0:wid], wpl[:, 0:wid],
                         1.0, None, ALU.min)
                    q.dma("s", dst[p], wqb[:, 0:wid])

        q.barrier()
        with nc.sbuf_tensor("ln1b", [128, NCH * T], BF16) as ln1b, \
             nc.sbuf_tensor("l1g", [128, 2 * NCH], F32) as l1g, \
             nc.sbuf_tensor("mx", [128, 16], BF16) as mx, \
             nc.sbuf_tensor("mrow16", [16, 128], BF16) as mrow16, \
             nc.psum_tensor("trb_ps", [128, 128], BF16) as trb_ps, \
             nc.psum_tensor("bcx_ps", [128, 1024], F32) as bcx_ps:
            q.dma("y", l1g[:, 0:NCH], l1gd[:, :])
            q.dma("y", l1g[:, NCH:2 * NCH], l1bd[:, :])
            mrowf = S.srows[0:1, 0:T]
            emit_ln_stats(q, S, X)
            for sl in range(4):
                q.mm(S.bc_ps[:, 0:512], S.ones_b[0:1, :], S.mb[0:1, sl * 512:(sl + 1) * 512])
                q.mm(S.bc_ps[:, 512:1024], S.ones_b[0:1, :], S.rb[0:1, sl * 512:(sl + 1) * 512])
                q.mm_fence()
                for ch in range(NCH):
                    o = ch * T + sl * 512
                    q.op("v", "tensor_tensor", S.t1[:, 0:512], X[:, o:o + 512],
                         S.bc_ps[:, 0:512], ALU.subtract)
                    q.op("v", "tensor_tensor", S.t1[:, 0:512], S.t1[:, 0:512],
                         S.bc_ps[:, 512:1024], ALU.mult)
                    q.op("v", "tensor_scalar", ln1b[:, o:o + 512], S.t1[:, 0:512],
                         l1g[:, ch:ch + 1], l1g[:, NCH + ch:NCH + ch + 1], ALU.mult, ALU.add)
            q.dma("y", Xsp_d[:, :], X[:, :])
            for ch in range(NCH):
                q.op("s", "activation", X[:, ch * T:(ch + 1) * T], ln1b[:, ch * T:(ch + 1) * T],
                     AF.Abs)
            q.op("v", "tensor_tensor", X[:, 0:T], X[:, 0:T], X[:, T:2 * T], ALU.max)
            q.op("v", "tensor_tensor", X[:, 2 * T:3 * T], X[:, 2 * T:3 * T], X[:, 3 * T:4 * T], ALU.max)
            q.op("v", "tensor_tensor", X[:, 4 * T:5 * T], X[:, 4 * T:5 * T], X[:, 5 * T:6 * T], ALU.max)
            q.op("v", "tensor_tensor", X[:, 6 * T:7 * T], X[:, 6 * T:7 * T], X[:, 7 * T:8 * T], ALU.max)
            q.op("v", "tensor_tensor", X[:, 0:T], X[:, 0:T], X[:, 2 * T:3 * T], ALU.max)
            q.op("v", "tensor_tensor", X[:, 4 * T:5 * T], X[:, 4 * T:5 * T], X[:, 6 * T:7 * T], ALU.max)
            q.op("v", "tensor_tensor", X[:, 0:T], X[:, 0:T], X[:, 4 * T:5 * T], ALU.max)
            q.op("s", "copy", Yb[:, 0:T], X[:, 0:T])
            for tc in range(16):
                q.op("t", "transpose", trb_ps[:, :], Yb[:, tc * 128:(tc + 1) * 128], ident[:, :])
                q.op("v", "reduce_max", mx[:, tc:tc + 1], trb_ps[:, :], AX)
            q.op("t", "transpose", trb_ps[0:16, :], mx[:, 0:16], ident[:, :])
            q.op("s", "copy", mrow16[:, :], trb_ps[0:16, 0:128])
            q.dma("y", sxm_d[:, :], mrow16[:, :])
            q.dma("g", mrowf[0:1, :], sxm_d.rearrange("a b -> () (a b)"))
            q.op("v", "tensor_scalar", mrowf[0:1, :], mrowf[0:1, :], 1e-5, None, ALU.max)
            q.op("v", "tensor_scalar", rowx[0:1, T:2 * T], mrowf[0:1, :],
                 wsc[0:1, 1:2], 1.0 / 127.0, ALU.mult, ALU.mult)
            q.op("s", "activation", mrowf[0:1, :], mrowf[0:1, :], AF.Ln)
            q.op("s", "activation", mrowf[0:1, :], mrowf[0:1, :], AF.Exp, scale=-1.0)
            q.op("v", "tensor_scalar", rowx[0:1, 0:T], mrowf[0:1, :], 127.0, None, ALU.mult)
            for sl in range(4):
                q.mm(bcx_ps[:, 0:512], S.ones_b[0:1, :], rowx[0:1, sl * 512:(sl + 1) * 512])
                q.mm(bcx_ps[:, 512:1024], S.ones_b[0:1, :], rowx[0:1, T + sl * 512:T + (sl + 1) * 512])
                q.mm_fence()
                for ch in range(NCH):
                    o = ch * T + sl * 512
                    q.op("v", "tensor_tensor", S.t1[:, 0:512], ln1b[:, o:o + 512],
                         bcx_ps[:, 0:512], ALU.mult)
                    q.op("v", "tensor_scalar", S.t1[:, 0:512], S.t1[:, 0:512],
                         RND, -RND, ALU.add, ALU.add)
                    q.op("v", "tensor_tensor", q1a[:, o:o + 512], S.t1[:, 0:512],
                         bcx_ps[:, 512:1024], ALU.mult)
        x2_scope.close()

        q.barrier()
        cs3 = ExitStack()
        wbig_a = cs3.enter_context(nc.sbuf_tensor("wbig_a", [128, 8 * 1024], BF16))
        wbig_b = cs3.enter_context(nc.sbuf_tensor("wbig_b", [128, 8 * 1024], BF16))
        gbf2 = cs3.enter_context(nc.sbuf_tensor("gbf", [128, 2 * H], BF16))
        gsl = cs3.enter_context(nc.sbuf_tensor("gsl", [128, 1024], F32))
        acc2 = cs3.enter_context(nc.sbuf_tensor("acc2", [128, 4], F32))
        acc2b = cs3.enter_context(nc.sbuf_tensor("acc2b", [128, 4], F32))
        q2fm2 = cs3.enter_context(nc.sbuf_tensor("q2fm", [128, 2 * 32 * 128], BF16))
        cols2 = cs3.enter_context(nc.sbuf_tensor("cols", [128, 32], F32))
        rows2 = cs3.enter_context(nc.sbuf_tensor("rows2", [1, 3 * H], BF16))
        b2c = cs3.enter_context(nc.sbuf_tensor("b2sb", [128, NCH], F32))
        b1_ps = cs3.enter_context(nc.psum_tensor("b1_ps", [128, 512], F32))
        b1_psB = cs3.enter_context(nc.psum_tensor("b1_psB", [128, 512], F32))
        tq_psa = cs3.enter_context(nc.psum_tensor("tq_ps", [128, 128], BF16))
        tq_psb = cs3.enter_context(nc.psum_tensor("tq_psB", [128, 128], BF16))
        bl2_ps = cs3.enter_context(nc.psum_tensor("bl2_ps", [128, 1024], F32))
        g2_ps = S.bc_ps
        q.dma("g", rows2[0:1, 0:H], b1rd[0:1, :])
        q.dma("g", rows2[0:1, H:2 * H], l2gd[0:1, :])
        q.dma("g", rows2[0:1, 2 * H:3 * H], l2bd[0:1, :])
        q.dma("y", b2c[:, :], b2cd[:, :])
        for tcp in range(8):
            q.op("g", "memset", cols2[:, 0:2], 0.0)
            q.op("g", "memset", cols2[:, 16:18], 0.0)
            for quarter in range(4):
                wb = wbig_a if quarter % 2 == 0 else wbig_b
                q.dma("y", wb[:, :].rearrange("p (k c) -> p k c", k=NCH),
                      w1q_d[:, :, quarter * 1024:(quarter + 1) * 1024].rearrange("k p c -> p k c"))
                for hs in range(2):
                    for sub in range(2):
                        tc = 2 * tcp + sub
                        idx = (quarter * 2 + hs) * 2 + sub
                        bp = b1_ps if idx % 2 == 0 else b1_psB
                        for k in range(NCH):
                            q.mm(bp[:, 0:512],
                                 q1a[:, k * T + tc * 128:k * T + (tc + 1) * 128],
                                 wb[:, k * 1024 + hs * 512:k * 1024 + hs * 512 + 512],
                                 start=(k == 0), stop=False)
                        ho = quarter * 1024 + hs * 512
                        q.mm(bp[:, 0:512], S.ones_b[0:1, 0:128],
                             rows2[0:1, ho:ho + 512], start=False, stop=True)
                        q.mm_fence()
                        par2 = idx % 2
                        co = sub * 16
                        gs = gsl[:, par2 * 512:par2 * 512 + 512]
                        q.op("s", "activation", gs, bp[:, 0:512], AF.Gelu,
                             accum_out=acc2[:, par2 * 2:par2 * 2 + 1])
                        q.op("v", "tensor_tensor", cols2[:, co:co + 1], cols2[:, co:co + 1],
                             acc2[:, par2 * 2:par2 * 2 + 1], ALU.add)
                        q.op("s", "activation", S.t1[:, 0:512], gs, AF.Square,
                             accum_out=acc2[:, par2 * 2 + 1:par2 * 2 + 2])
                        q.op("v", "tensor_tensor", cols2[:, co + 1:co + 2], cols2[:, co + 1:co + 2],
                             acc2[:, par2 * 2 + 1:par2 * 2 + 2], ALU.add)
                        q.op("s", "copy", gbf2[:, sub * H + ho:sub * H + ho + 512], gs)
            for sub in range(2):
                tc = 2 * tcp + sub
                co = sub * 16
                cols = cols2[:, co:co + 16]
                gbf = gbf2[:, sub * H:(sub + 1) * H]
                zt = gbf
                q.op("v", "tensor_scalar", cols[:, 4:5], cols[:, 0:1], 1.0 / H, None, ALU.mult)
                q.op("v", "tensor_tensor", cols[:, 5:6], cols[:, 4:5], cols[:, 4:5], ALU.mult)
                q.op("v", "scalar_tensor_tensor", cols[:, 5:6], cols[:, 1:2], 1.0 / H,
                     cols[:, 5:6], ALU.mult, ALU.subtract)
                q.op("v", "tensor_scalar", cols[:, 5:6], cols[:, 5:6], float(EPS), None, ALU.add)
                q.op("s", "activation", cols[:, 6:7], cols[:, 5:6], AF.Ln)
                q.op("s", "activation", cols[:, 6:7], cols[:, 6:7], AF.Exp, scale=-0.5)
                q.op("v", "tensor_tensor", cols[:, 7:8], cols[:, 4:5], cols[:, 6:7], ALU.mult)
                q.op("v", "tensor_scalar", cols[:, 7:8], cols[:, 7:8], -1.0, None, ALU.mult)
                for hs in range(8):
                    ho = hs * 512
                    q.mm(g2_ps[:, 0:512], S.ones_b[0:1, :], rows2[0:1, H + ho:H + ho + 512])
                    q.mm(g2_ps[:, 512:1024], S.ones_b[0:1, :], rows2[0:1, 2 * H + ho:2 * H + ho + 512])
                    q.mm_fence()
                    q.op("v", "tensor_scalar", zt[:, ho:ho + 512], gbf[:, ho:ho + 512],
                         cols[:, 6:7], cols[:, 7:8], ALU.mult, ALU.add)
                    q.op("v", "tensor_tensor", zt[:, ho:ho + 512], zt[:, ho:ho + 512],
                         g2_ps[:, 0:512], ALU.mult)
                    q.op("v", "tensor_tensor", zt[:, ho:ho + 512], zt[:, ho:ho + 512],
                         g2_ps[:, 512:1024], ALU.add)
                q.op("v", "tensor_reduce", cols[:, 8:9], zt[:, :], AX, ALU.max,
                     apply_absolute_value=True)
                q.op("v", "tensor_scalar", cols[:, 8:9], cols[:, 8:9], 1e-5, None, ALU.max)
                q.op("v", "tensor_scalar", cols[:, 9:10], cols[:, 8:9],
                     wcol[:, 3:4], 1.0 / 127.0, ALU.mult, ALU.mult)
                q.op("s", "activation", cols[:, 10:11], cols[:, 8:9], AF.Ln)
                q.op("s", "activation", cols[:, 10:11], cols[:, 10:11], AF.Exp, scale=-1.0)
                q.op("v", "tensor_scalar", cols[:, 11:12], cols[:, 10:11], 127.0, None, ALU.mult)
                for hs in range(8):
                    ho = hs * 512
                    q.op("v", "tensor_scalar", S.t1[:, 0:512], zt[:, ho:ho + 512],
                         cols[:, 11:12], RND, ALU.mult, ALU.add)
                    q.op("v", "tensor_scalar", S.t1[:, 0:512], S.t1[:, 0:512],
                         -RND, None, ALU.add)
                    q.op("v", "tensor_scalar", zt[:, ho:ho + 512], S.t1[:, 0:512],
                         cols[:, 9:10], None, ALU.mult)
                for j2 in range(32):
                    tq = tq_psa if j2 % 2 == 0 else tq_psb
                    q.op("t", "transpose", tq[:, :], zt[:, j2 * 128:(j2 + 1) * 128], ident[:, :])
                    q.op("s", "copy", q2fm2[:, sub * 4096 + j2 * 128:sub * 4096 + (j2 + 1) * 128],
                         tq[:, :])
            for j2b in range(4):
                wb = wbig_a if j2b % 2 == 0 else wbig_b
                q.dma("y", wb[:, :].rearrange("p (j c) -> p j c", j=8),
                      w2q_d[j2b * 8:(j2b + 1) * 8].rearrange("j p c -> p j c"))
                for sub in range(2):
                    bp2 = bl2_ps if sub == 0 else g2_ps
                    for jj in range(8):
                        j2 = j2b * 8 + jj
                        for half in range(2):
                            q.mm(bp2[:, half * 512:(half + 1) * 512],
                                 q2fm2[:, sub * 4096 + j2 * 128:sub * 4096 + (j2 + 1) * 128],
                                 wb[:, jj * C + half * 512:jj * C + half * 512 + 512],
                                 start=(j2 == 0), stop=(j2 == 31))
            q.mm_fence()
            for sub in range(2):
                tc = 2 * tcp + sub
                bp2 = bl2_ps if sub == 0 else g2_ps
                zt = gbf2[:, sub * H:(sub + 1) * H]
                q.op("s", "copy", zt[:, 0:C], bp2[:, 0:1024])
                for co in range(NCH):
                    tq = tq_psa if co % 2 == 0 else tq_psb
                    q.op("t", "transpose", tq[:, :], zt[:, co * 128:(co + 1) * 128],
                         ident[:, :])
                    q.op("v", "tensor_scalar", Yb[:, co * T + tc * 128:co * T + (tc + 1) * 128],
                         tq[:, :], b2c[:, co:co + 1], None, ALU.add)

        cs3.close()
        q.barrier()
        with nc.sbuf_tensor("Xs3", [128, NCH * T], F32) as X3s:
            q.dma("y", X3s[:, :], Xsp_d[:, :])
            emit_postln_residual(q, S, Yb, X3s)
            q.dma("y", X4out[:, :], X3s[:, :])
            q.finish()
    return nc


def fused_in_maps(inputs):
    f32 = np.float32
    x = np.asarray(inputs["L0_attn_x"], f32)
    w_qkv = np.asarray(inputs["L0_attn_w_qkv"], f32)
    wc1 = np.asarray(inputs["L0_attn_w_conv1"], f32)
    wg = np.asarray(inputs["L0_ffn_w_gate"], np.int32)
    wu = np.asarray(inputs["L0_ffn_w_up"], np.int32)
    wd = np.asarray(inputs["L0_ffn_w_down"], np.int32)
    P1 = {k[8:]: np.asarray(v, f32) for k, v in inputs.items() if k.startswith("L1_attn_")}
    P2 = {k[7:]: np.asarray(v, f32) for k, v in inputs.items() if k.startswith("L1_ffn_")}
    fc = _fft_consts()
    shared = {
        "Wqkv": chunked_lhsT(np.ascontiguousarray(w_qkv.T)),
        "wc1": _cols(wc1),
        "wg": chunked_lhsT(np.ascontiguousarray(wg.T)).astype(np.int32),
        "wu": chunked_lhsT(np.ascontiguousarray(wu.T)).astype(np.int32),
        "wd": np.ascontiguousarray(wd.T.reshape(24, 128, 1024)).astype(np.int32),
        "Wu": np.ascontiguousarray(P1["Wu"].reshape(8, 128, 1024)),
        "Wv": np.ascontiguousarray(P1["Wv"].reshape(8, 128, 1024)),
        "bu": P1["bu"][None, :], "bv": P1["bv"][None, :],
        "W1": P1["W1"], "b1c": P1["b1"][:, None],
        "W2": P1["W2"], "b2c": P1["b2"][:, None],
        "peT": _pos_emb_T(), "dkc": _decay_cols(), **fc,
        "Wo_l": chunked_lhsT(P1["Wo"]), "bo_c": _cols(P1["bo"]),
        "identin": np.eye(128, dtype=BFD),
        "ln1g_c": _cols(P2["ln1_g"]), "ln1b_c": _cols(P2["ln1_b"]),
        "w1T": np.ascontiguousarray(P2["w1"].T.reshape(8, 128, 4096)),
        "b1r": P2["b1"][None, :],
        "ln2g_r": P2["ln2_g"][None, :], "ln2b_r": P2["ln2_b"][None, :],
        "w2T": np.ascontiguousarray(P2["w2"].T.reshape(32, 128, 1024)),
        "b2_c": _cols(P2["b2"]),
    }
    maps = []
    for core in range(8):
        b, h = core // 2, core % 2
        csl = slice(512 * h, 512 * h + 512)
        sel = np.zeros((128, 2), f32)
        sel[:, h] = 1.0
        m = dict(shared)
        m["X"] = fm_layout(np.ascontiguousarray(x[b, 2048 * h:2048 * h + 2048, :].T))
        m["W3o"] = np.ascontiguousarray(P1["W3"][:, csl])
        m["b3r"] = np.ascontiguousarray(P1["b3"][None, csl])
        m["selc"] = sel
        maps.append(m)
    return maps


def kernel(**inputs):
    nc = build_fused()
    maps = fused_in_maps(inputs)
    res = run_bass_kernel_spmd(nc, maps, core_ids=list(range(8))).results
    out = np.empty((4, 4096, 1024), np.float32)
    for core in range(8):
        b, h = core // 2, core % 2
        out[b, 2048 * h:2048 * h + 2048, :] = un_fm(np.asarray(res[core]["X4"]), NCH, T).T
    return out
